# version 01
# baseline (speedup 1.0000x reference)
import numpy as np
import ml_dtypes
import concourse.bass as bass
import concourse.mybir as mybir
from concourse.bass_utils import run_bass_kernel_spmd
from contextlib import ExitStack

F32 = mybir.dt.float32
BF16 = mybir.dt.bfloat16
AF = mybir.ActivationFunctionType
OP = mybir.AluOpType
AX = mybir.AxisListType
bfdt = ml_dtypes.bfloat16

NCORES = 8
T = 1024
NCH = 8
CH = 128
H = 32
KVH = 8
D = 128
DM = 4096
KC = 32
NG = 8
NHF = 16

ENGS = ("pe", "dve", "act", "pool", "sp")

_DEBUG_TAPS = []


class Sched:
    def __init__(self):
        self.q = {e: [] for e in ENGS}
        self.cnt = {}

    def emit(self, eng, fn, waits=(), inc=None, inc_by=1):
        waits = tuple(w for w in waits if w is not None)
        self.q[eng].append((fn, waits, inc, inc_by))
        if inc is not None:
            self.cnt[inc] = self.cnt.get(inc, 0) + inc_by
            return (inc, self.cnt[inc])
        return None

    def tok(self, sem):
        return (sem, self.cnt.get(sem, 0))

    def check(self):
        ptr = {e: 0 for e in ENGS}
        val = {}
        progress = True
        while progress:
            progress = False
            for e in ENGS:
                while ptr[e] < len(self.q[e]):
                    fn, waits, inc, inc_by = self.q[e][ptr[e]]
                    if all(val.get(s, 0) >= v for s, v in waits):
                        if inc is not None:
                            val[inc] = val.get(inc, 0) + inc_by
                        ptr[e] += 1
                        progress = True
                    else:
                        break
        stuck = {e: ptr[e] for e in ENGS if ptr[e] < len(self.q[e])}
        if stuck:
            det = {}
            for e, p in stuck.items():
                fn, waits, inc, _ = self.q[e][p]
                det[e] = (p, len(self.q[e]),
                          [(s, v, val.get(s, 0)) for s, v in waits if val.get(s, 0) < v])
            raise RuntimeError(f"schedule deadlock: {det}")

    def materialize(self, nc, sems):
        with nc.Block() as block:
            decos = {"pe": block.tensor, "dve": block.vector, "act": block.scalar,
                     "pool": block.gpsimd, "sp": block.sync}
            for eng in ENGS:
                items = self.q[eng]

                def body_fn(e, items=items, eng=eng):
                    last = {}
                    for fn, waits, inc, inc_by in items:
                        for s, v in waits:
                            if last.get(s, 0) < v:
                                e.wait_ge(sems[s], v)
                                last[s] = v
                        inst = fn(e)
                        if inc is not None:
                            assert inst is not None, f"inc op returned None on {eng}"
                            inst.then_inc(sems[inc], inc_by)
                decos[eng](body_fn)


class Bank:
    def __init__(self, tile):
        self.t = tile
        self.wr = None
        self.readers = []

    def pre_write(self):
        w = list(self.readers)
        self.readers = []
        if self.wr is not None:
            w.append(self.wr)
        return w

    def pre_read(self):
        return [self.wr] if self.wr is not None else []


def build_program(debug_taps=()):
    nc = bass.Bass("TRN2", target_bir_lowering=False)
    tap_phases = set(debug_taps)

    def din(name, shape, dt=F32):
        return nc.dram_tensor(name, list(shape), dt, kind="ExternalInput").ap()

    x_in = din("x", [NCH, 128, DM])
    wq_in = din("wq", [H, 128, KC, 128])
    wg_in = din("wg", [NG, 128, KC, 512])
    wk_in = din("wk", [KVH, 128, KC, 128])
    wv_in = din("wv", [KVH, 128, KC, 128])
    wo_in = din("wo", [NHF, 128, KC, 256])
    wb_in = din("wb", [NHF, 128, KC, 256])
    smallw_in = din("smallw", [128, KC, 169])
    bhat_in = din("bhat", [128, DM])
    bbase_in = din("bbase", [1, DM])
    prow_in = din("prow", [1, 193])
    cosfm_in = din("cosfm", [128, T])
    sinfm_in = din("sinfm", [128, T])
    cum_in = din("cumlt", [128, 128])
    mask4_in = din("mask4", [128, 512])
    idb_in = din("idb", [128, 128], BF16)
    idf_in = din("idf", [128, 128])
    ptb_in = din("ptb", [128, 128], BF16)
    onesb_in = din("onesb", [1, 128], BF16)
    onesf_in = din("onesf", [1, 128])
    crow_in = din("crow", [128, 8])
    urow_in = din("urow", [1, 256])
    unot_in = din("unot", [1, 256])
    mrow_in = din("mrow", [1, 256])

    o_out = nc.dram_tensor("o_out", [NCH, 128, DM], F32, kind="ExternalOutput").ap()

    cc_in = nc.dram_tensor("cc_in", [128, 4128], F32, kind="Internal")
    cc_out = nc.dram_tensor("cc_out", [NCORES * 128, 4128], F32, kind="Internal")
    yscr = nc.dram_tensor("yscr", [NG, 128, NCH, 512], BF16, kind="Internal")
    bscr = nc.dram_tensor("bscr", [NHF, 128, NCH, 256], F32, kind="Internal")

    taps = {}

    def tapdecl(name, shape, dt=F32):
        taps[name] = nc.dram_tensor("tap_" + name, list(shape), dt,
                                    kind="ExternalOutput").ap()
        return taps[name]

    st = ExitStack()

    def sbuf(name, shape, dt):
        return st.enter_context(nc.sbuf_tensor(name, list(shape), dt))

    cosfm = sbuf("sb_cosfm", [128, T], F32)
    sinfm = sbuf("sb_sinfm", [128, T], F32)
    cum = sbuf("sb_cum", [128, 128], F32)
    mask4 = sbuf("sb_mask4", [128, 512], F32)
    idb = sbuf("sb_idb", [128, 128], BF16)
    idf = sbuf("sb_idf", [128, 128], F32)
    ptb = sbuf("sb_ptb", [128, 128], BF16)
    onesb = sbuf("sb_onesb", [1, 128], BF16)
    onesf = sbuf("sb_onesf", [1, 128], F32)
    crow = sbuf("sb_crow", [128, 8], F32)
    urow = sbuf("sb_urow", [1, 256], F32)
    unot = sbuf("sb_unot", [1, 256], F32)
    mrow = sbuf("sb_mrow", [1, 256], F32)
    prow = sbuf("sb_prow", [1, 193], F32)
    preps = sbuf("sb_preps", [128, 193], F32)
    arep = sbuf("sb_arep", [128, 32], F32)
    stats = sbuf("sb_stats", [128, 16], F32)
    tlcur = sbuf("sb_tlcur", [2, 512], F32)
    rhcur = sbuf("sb_rhcur", [2, 512], F32)
    grow_t = sbuf("sb_grow_t", [1, 32], F32)
    dallrow = sbuf("sb_dallrow", [1, 256], F32)
    dprow = sbuf("sb_dprow", [1, 256], F32)
    sfx = sbuf("sb_sfx", [1, 256], F32)
    prow2 = sbuf("sb_prow2", [1, 256], F32)
    prep_sc = sbuf("sb_prep_sc", [128, 256], F32)

    u1 = sbuf("sb_u1", [128, 32768], BF16)
    ovb = sbuf("sb_ovb", [128, 32768], BF16)
    wslot_t = sbuf("sb_wslot", [128, 8192], BF16)
    arena = sbuf("sb_arena", [128, 18432], BF16)
    arena_f = arena.bitcast(F32)
    ovb_f = ovb.bitcast(F32)

    k_tm = ovb[:, 0:8192]
    v_tm = ovb[:, 8192:16384]
    state = ovb_f[:, 8192:12288]
    dt_all = ovb_f[:, 12288:12544]
    g_all = ovb_f[:, 12544:12800]
    expg_all = ovb_f[:, 12800:13056]
    kdsc_all = ovb_f[:, 13056:13312]
    drep_all = ovb_f[:, 13312:13568]
    hw_fm = ovb[:, 27136:28160]
    w_all = ovb_f[:, 14080:14144]
    glsum = ovb_f[:, 14144:14176]
    dall = ovb_f[:, 14176:14208]
    gT = ovb_f[0:32, 14208:15232]
    ngT = ovb_f[0:32, 15232:16256]
    gatesil = ovb[:, 0:32768]

    k_tm3 = k_tm.rearrange("p (n f) -> p n f", f=1024)
    v_tm3 = v_tm.rearrange("p (n f) -> p n f", f=1024)
    xn_fm = u1[:, 0:32768].rearrange("p (k t) -> p k t", t=1024)
    yg_fm = u1[:, 0:32768].rearrange("p (ec r) -> p ec r", r=1024)
    wslot = [wslot_t[:, 0:4096].rearrange("p (k d) -> p k d", d=128),
             wslot_t[:, 4096:8192].rearrange("p (k d) -> p k d", d=128)]

    pbk = [st.enter_context(nc.psum_tensor(f"pb{i}", [128, 512], F32)) for i in range(6)]
    tbk = [st.enter_context(nc.psum_tensor(f"tb{i}", [128, 1024], BF16)) for i in range(2)]
    B = [Bank(t) for t in pbk]
    TB = [Bank(t) for t in tbk]

    sems = {n: st.enter_context(nc.semaphore(n))
            for n in ["pe", "dve", "act", "pool", "sp", "dsp", "dpool", "cc"]}

    S = Sched()

    def dma(queue, out_ap, in_ap, waits=()):
        semn = {"sp": "dsp", "pool": "dpool"}[queue]
        return S.emit(queue, lambda e, o=out_ap, i=in_ap: e.dma_start(out=o, in_=i),
                      waits=waits, inc=semn, inc_by=16)

    def fence(queue, via="sp"):
        semn = {"sp": "dsp", "pool": "dpool"}[queue]
        full = S.cnt.get(semn, 0)
        return S.emit(via, lambda e: e.nop(), waits=[(semn, full)], inc=via)

    ctoks = []
    for dst, src in [(cosfm, cosfm_in), (sinfm, sinfm_in), (cum, cum_in),
                     (mask4, mask4_in), (idb, idb_in), (idf, idf_in), (ptb, ptb_in),
                     (onesb, onesb_in), (onesf, onesf_in), (crow, crow_in),
                     (urow, urow_in), (unot, unot_in), (mrow, mrow_in),
                     (prow, prow_in)]:
        ctoks.append(dma("sp", dst[:, :], src))
    tok_consts = fence("sp")

    S.emit("dve", lambda e: e.memset(tlcur[0:2, :], 1.0), inc="dve")
    S.emit("dve", lambda e: e.memset(rhcur[0:2, :], 1.0), inc="dve")
    tok_gl = S.emit("dve", lambda e: e.memset(glsum[:, :], 0.0), inc="dve")
    tok_st0 = S.emit("dve", lambda e: e.memset(state[:, :], 0.0), inc="dve")

    wts = B[2].pre_write() + [tok_consts]
    tok = S.emit("pe", lambda e: e.matmul(pbk[2][:, 0:193], onesf[0:1, :],
                                          prow[0:1, :], start=True, stop=True),
                 waits=wts, inc="pe")
    B[2].wr = tok
    tok_preps = S.emit("dve", lambda e: e.tensor_copy(preps[:, :], pbk[2][:, 0:193]),
                       waits=B[2].pre_read(), inc="dve")
    B[2].readers.append(tok_preps)
    tk = S.emit("act", lambda e: e.activation(arep[:, :], preps[:, 128:160], AF.Exp),
                waits=[tok_preps], inc="act")
    tok_arep = S.emit("dve", lambda e: e.tensor_scalar_mul(arep[:, :], arep[:, :], -1.0),
                      waits=[tk], inc="dve")
    tok_nbthr = S.emit("dve", lambda e: e.tensor_scalar_mul(stats[:, 15:16],
                                                            preps[:, 192:193], -1.0),
                       waits=[tok_preps], inc="dve")

    x_st = arena_f[:, 0:4096]
    xn_tm = arena[:, 8192:12288]
    prev_x_readers = []
    prev_xn_tok = None
    for n in range(NCH):
        tok_x = dma("sp", x_st[:, :], x_in[n], waits=prev_x_readers)
        tok_x = fence("sp")
        prev_x_readers = []
        t_sum = S.emit("dve", lambda e: e.reduce_sum(stats[:, 0:1], x_st[:, :], axis=AX.X),
                       waits=[tok_x], inc="dve")
        t_ssq = S.emit("act", lambda e: e.activation(xn_tm[:, :], x_st[:, :], AF.Square,
                                                     accum_out=stats[:, 1:2]),
                       waits=[tok_x, prev_xn_tok], inc="act")
        t_m = S.emit("dve", lambda e: e.tensor_scalar_mul(stats[:, 2:3], stats[:, 0:1],
                                                          1.0 / DM),
                     waits=[t_sum], inc="dve")
        t_msq = S.emit("dve", lambda e: e.tensor_scalar(stats[:, 3:4], stats[:, 2:3],
                                                        stats[:, 2:3], None, OP.mult),
                       waits=[t_m], inc="dve")
        t_va = S.emit("dve", lambda e: e.tensor_scalar(stats[:, 4:5], stats[:, 1:2],
                                                       1.0 / DM, stats[:, 3:4],
                                                       OP.mult, OP.subtract),
                      waits=[t_ssq, t_msq], inc="dve")
        t_ln = S.emit("act", lambda e: e.activation(stats[:, 5:6], stats[:, 4:5], AF.Ln,
                                                    bias=crow[:, 0:1]),
                      waits=[t_va, tok_consts], inc="act")
        t_rstd = S.emit("act", lambda e: e.activation(stats[:, 6:7], stats[:, 5:6], AF.Exp,
                                                      scale=-0.5),
                        waits=[t_ln], inc="act")
        t_xn = S.emit("dve", lambda e: e.tensor_scalar(xn_tm[:, :], x_st[:, :],
                                                       stats[:, 2:3], stats[:, 6:7],
                                                       OP.subtract, OP.mult),
                      waits=[t_rstd, prev_xn_tok], inc="dve")
        prev_x_readers = [t_xn]
        ev_tok = None
        for grp in range(4):
            bank = TB[grp % 2]
            wts = bank.pre_write() + [t_xn, tok_consts]
            last = None
            for j8 in range(8):
                j = grp * 8 + j8

                def mk(j=j, j8=j8, bank=bank):
                    def f(e):
                        return e.transpose(bank.t[:, j8 * 128:(j8 + 1) * 128],
                                           xn_tm[:, j * 128:(j + 1) * 128], idb[:, :])
                    return f
                last = S.emit("pe", mk(), waits=(wts if j8 == 0 else []),
                              inc="pe" if j8 == 7 else None)
            bank.wr = last

            def mkev(grp=grp, n=n, bank=bank):
                def f(e):
                    dst = xn_fm[:, grp * 8:(grp + 1) * 8, n * 128:(n + 1) * 128]
                    src = bank.t[:, 0:1024].rearrange("p (j t) -> p j t", t=128)
                    return e.tensor_copy(dst, src)
                return f
            ev_tok = S.emit("dve", mkev(), waits=bank.pre_read(), inc="dve")
            bank.readers.append(ev_tok)
        prev_xn_tok = ev_tok
    tok_xn_done = S.tok("dve")

    if "P0" in tap_phases:
        tp = tapdecl("xn_fm", [128, 32768], BF16)
        dma("sp", tp, u1[:, 0:32768], waits=[S.tok("dve")])

    smallw = arena[:, 0:5408].rearrange("p (k c) -> p k c", c=169)
    zt = arena_f[:, 2752:2784]
    zt2 = arena_f[:, 2784:2816]
    zt3 = arena_f[:, 2816:2848]
    hw_tmp = arena[:, 5696:5824]
    gex = arena_f[:, 2912:2920]
    kbfA = arena[:, 6144:6656]
    t1f = arena_f[:, 3328:3840]
    t2f = arena_f[:, 3840:4352]
    kfm_t = arena[:, 8704:9216]
    xc_t = arena[:, 9216:9728]
    kd_t = arena[:, 9728:10240]

    tok_smallw = dma("pool", smallw, smallw_in, waits=[tok_xn_done])
    tok_smallw = fence("pool")

    prev_t_wi = None
    prev_t_tr = None
    prev_t_glr = None
    for n in range(NCH):
        bank = B[n % 2]
        wts = bank.pre_write() + [tok_smallw, tok_xn_done]
        last = None
        for k in range(KC):
            def mk(k=k, n=n, bank=bank):
                def f(e):
                    return e.matmul(bank.t[:, 0:169],
                                    xn_fm[:, k, n * 128:(n + 1) * 128],
                                    smallw[:, k, :],
                                    start=(k == 0), stop=(k == KC - 1))
                return f
            last = S.emit("pe", mk(), waits=(wts if k == 0 else []),
                          inc="pe" if k == KC - 1 else None)
        bank.wr = last
        rd = bank.pre_read()
        t_z = S.emit("dve", lambda e, b=bank: e.tensor_add(zt[:, :], b.t[:, 137:169],
                                                           preps[:, 160:192]),
                     waits=rd + [tok_preps, prev_t_wi], inc="dve")
        bank.readers.append(t_z)
        t_n = S.emit("dve", lambda e: e.tensor_scalar_mul(zt2[:, :], zt[:, :], -1.0),
                     waits=[t_z], inc="dve")
        t_a = S.emit("dve", lambda e: e.tensor_max(zt2[:, :], zt2[:, :], zt[:, :]),
                     waits=[t_n], inc="dve")
        t_e = S.emit("act", lambda e: e.activation(zt2[:, :], zt2[:, :], AF.Exp, scale=-1.0),
                     waits=[t_a], inc="act")
        t_l = S.emit("act", lambda e: e.activation(zt2[:, :], zt2[:, :], AF.Ln,
                                                   bias=crow[:, 1:2]),
                     waits=[t_e], inc="act")
        t_r = S.emit("dve", lambda e: e.tensor_scalar_max(zt3[:, :], zt[:, :], 0.0),
                     waits=[t_l], inc="dve")
        t_dt = S.emit("dve", lambda e, n=n: e.tensor_add(dt_all[:, n * 32:(n + 1) * 32],
                                                         zt3[:, :], zt2[:, :]),
                      waits=[t_r], inc="dve")
        t_g = S.emit("dve", lambda e, n=n: e.tensor_mul(g_all[:, n * 32:(n + 1) * 32],
                                                        dt_all[:, n * 32:(n + 1) * 32],
                                                        arep[:, :]),
                     waits=[t_dt, tok_arep], inc="dve")
        wts2 = B[2].pre_write() + [t_g, tok_consts]
        t_cum = S.emit("pe", lambda e, n=n: e.matmul(pbk[2][:, 0:32], cum[:, :],
                                                     g_all[:, n * 32:(n + 1) * 32],
                                                     start=True, stop=True),
                       waits=wts2, inc="pe")
        B[2].wr = t_cum
        t_G = S.emit("dve", lambda e, n=n: e.tensor_copy(g_all[:, n * 32:(n + 1) * 32],
                                                         pbk[2][:, 0:32]),
                     waits=B[2].pre_read(), inc="dve")
        B[2].readers.append(t_G)
        t_eg = S.emit("act", lambda e, n=n: e.activation(expg_all[:, n * 32:(n + 1) * 32],
                                                         g_all[:, n * 32:(n + 1) * 32],
                                                         AF.Exp),
                      waits=[t_G], inc="act")
        t_gr = dma("pool", grow_t[0:1, :], g_all[127:128, n * 32:(n + 1) * 32],
                   waits=[t_G, prev_t_glr])
        t_gr = fence("pool")
        wts3 = B[2].pre_write() + [t_gr]
        t_glr = S.emit("pe", lambda e: e.matmul(pbk[2][:, 32:64], onesf[0:1, :],
                                                grow_t[0:1, :], start=True, stop=True),
                       waits=wts3, inc="pe")
        B[2].wr = t_glr
        prev_t_glr = t_glr
        rd2 = B[2].pre_read()
        t_sub = S.emit("dve", lambda e, n=n: e.tensor_sub(zt[:, :], pbk[2][:, 32:64],
                                                          g_all[:, n * 32:(n + 1) * 32]),
                       waits=rd2 + [t_eg], inc="dve")
        B[2].readers.append(t_sub)
        S.emit("act", lambda e, n=n: e.activation(kdsc_all[:, n * 32:(n + 1) * 32],
                                                  zt[:, :], AF.Exp),
               waits=[t_sub], inc="act")
        t_dr = S.emit("act", lambda e, n=n: e.activation(drep_all[:, n * 32:(n + 1) * 32],
                                                         pbk[2][:, 32:64], AF.Exp),
                      waits=rd2, inc="act")
        B[2].readers.append(t_dr)
        t_gl = S.emit("dve", lambda e: e.tensor_add(glsum[:, :], glsum[:, :],
                                                    pbk[2][:, 32:64]),
                      waits=rd2 + [tok_gl], inc="dve")
        B[2].readers.append(t_gl)
        tok_gl = t_gl
        wts4 = B[2].pre_write() + [t_G, tok_consts]
        t_gt = S.emit("pe", lambda e, n=n: e.transpose(pbk[2][0:32, 0:128],
                                                       g_all[:, n * 32:(n + 1) * 32],
                                                       idf[:, :]),
                      waits=wts4, inc="pe")
        B[2].wr = t_gt
        rd3 = B[2].pre_read()
        t_gtc = S.emit("dve", lambda e, n=n: e.tensor_copy(gT[:, n * 128:(n + 1) * 128],
                                                           pbk[2][0:32, 0:128]),
                       waits=rd3, inc="dve")
        B[2].readers.append(t_gtc)
        S.emit("dve", lambda e, n=n: e.tensor_scalar_mul(ngT[:, n * 128:(n + 1) * 128],
                                                         gT[:, n * 128:(n + 1) * 128],
                                                         -1.0),
               waits=[t_gtc], inc="dve")
        t_nm = S.emit("dve", lambda e, b=bank: e.reduce_max(stats[:, 8:9],
                                                            b.t[:, 128:136],
                                                            axis=AX.X, negate=True),
                      waits=rd + [prev_t_wi], inc="dve")
        bank.readers.append(t_nm)
        t_ge = S.emit("act", lambda e, b=bank: e.activation(gex[:, :], b.t[:, 128:136],
                                                            AF.Exp, bias=stats[:, 8:9],
                                                            accum_out=stats[:, 9:10]),
                      waits=[t_nm], inc="act")
        bank.readers.append(t_ge)
        t_s1 = S.emit("act", lambda e, b=bank: e.activation(stats[:, 10:11],
                                                            b.t[:, 136:137], AF.Exp,
                                                            scale=-1.0,
                                                            bias=stats[:, 15:16]),
                      waits=rd + [tok_nbthr], inc="act")
        bank.readers.append(t_s1)
        t_s2 = S.emit("act", lambda e: e.activation(stats[:, 10:11], stats[:, 10:11],
                                                    AF.Ln, bias=crow[:, 1:2]),
                      waits=[t_s1], inc="act")
        t_s3 = S.emit("act", lambda e: e.activation(stats[:, 10:11], stats[:, 10:11],
                                                    AF.Exp, scale=-1.0),
                      waits=[t_s2], inc="act")
        t_nt = S.emit("dve", lambda e: e.tensor_scalar_mul(stats[:, 11:12],
                                                           stats[:, 10:11], -0.125),
                      waits=[t_s3], inc="dve")
        t_gi = S.emit("act", lambda e: e.activation(stats[:, 12:13], stats[:, 9:10], AF.Ln),
                      waits=[t_ge], inc="act")
        t_gi = S.emit("act", lambda e: e.activation(stats[:, 12:13], stats[:, 12:13],
                                                    AF.Exp, scale=-1.0),
                      waits=[t_gi], inc="act")
        t_ad = S.emit("dve", lambda e: e.tensor_scalar(gex[:, :], gex[:, :],
                                                       stats[:, 12:13], stats[:, 11:12],
                                                       OP.mult, OP.add),
                      waits=[t_gi, t_nt], inc="dve")
        t_wr = S.emit("dve", lambda e: e.tensor_scalar_max(gex[:, :], gex[:, :], 0.0),
                      waits=[t_ad], inc="dve")
        t_ws = S.emit("dve", lambda e: e.reduce_sum(stats[:, 13:14], gex[:, :], axis=AX.X),
                      waits=[t_wr], inc="dve")
        t_iz = S.emit("dve", lambda e: e.tensor_scalar(stats[:, 14:15], stats[:, 13:14],
                                                       0.0, None, OP.is_equal),
                      waits=[t_ws], inc="dve")
        t_w2 = S.emit("dve", lambda e: e.tensor_add(stats[:, 13:14], stats[:, 13:14],
                                                    stats[:, 14:15]),
                      waits=[t_iz], inc="dve")
        t_wl = S.emit("act", lambda e: e.activation(stats[:, 13:14], stats[:, 13:14], AF.Ln),
                      waits=[t_w2], inc="act")
        t_wi = S.emit("act", lambda e: e.activation(stats[:, 13:14], stats[:, 13:14],
                                                    AF.Exp, scale=-1.0, bias=crow[:, 2:3]),
                      waits=[t_wl], inc="act")
        prev_t_wi = t_wi
        t_wf = S.emit("dve", lambda e, n=n: e.tensor_scalar(w_all[:, n * 8:(n + 1) * 8],
                                                            gex[:, :], stats[:, 13:14],
                                                            None, OP.mult),
                      waits=[t_wi], inc="dve")
        last_hw = None
        for ei in range(8):
            def mkhw(ei=ei, n=n, bank=bank):
                def f(e):
                    return e.tensor_scalar(hw_tmp[:, ei * 16:(ei + 1) * 16],
                                           bank.t[:, ei * 16:(ei + 1) * 16],
                                           w_all[:, n * 8 + ei:n * 8 + ei + 1],
                                           None, OP.mult)
                return f
            last_hw = S.emit("dve", mkhw(), waits=([t_wf, prev_t_tr] if ei == 0 else []),
                             inc="dve" if ei == 7 else None)
        bank.readers.append(last_hw)
        tbank = TB[n % 2]
        wts5 = tbank.pre_write() + [last_hw, tok_consts]
        t_tr = S.emit("pe", lambda e, tb=tbank: e.transpose(tb.t[:, 0:128], hw_tmp[:, :],
                                                            idb[:, :]),
                      waits=wts5, inc="pe")
        tbank.wr = t_tr
        prev_t_tr = t_tr
        t_he = S.emit("dve", lambda e, n=n, tb=tbank: e.tensor_copy(
            hw_fm[:, n * 128:(n + 1) * 128], tb.t[:, 0:128]),
            waits=tbank.pre_read(), inc="dve")
        tbank.readers.append(t_he)
    tok_small_done = S.tok("dve")

    slot_readers = [[], []]
    sl_i = 0
    pj_i = 0
    for which, w_in_, dst3 in [("k", wk_in, k_tm3), ("v", wv_in, v_tm3)]:
        for kv in range(KVH):
            sl = sl_i % 2
            sl_i += 1
            tok_w = dma("pool", wslot[sl], w_in_[kv], waits=slot_readers[sl])
            tok_w = fence("pool")
            slot_readers[sl] = []
            for half in range(2):
                bank = B[pj_i % 2]
                pj_i += 1
                wts = bank.pre_write() + [tok_w, tok_xn_done]
                last = None
                for k in range(KC):
                    def mk(k=k, half=half, sl=sl, bank=bank):
                        def f(e):
                            return e.matmul(bank.t[:, 0:512], wslot[sl][:, k, :],
                                            xn_fm[:, k, half * 512:(half + 1) * 512],
                                            start=(k == 0), stop=(k == KC - 1))
                        return f
                    last = S.emit("pe", mk(), waits=(wts if k == 0 else []),
                                  inc="pe" if k == KC - 1 else None)
                bank.wr = last
                slot_readers[sl].append(last)
                rd = bank.pre_read()
                if which == "k":
                    t_c = S.emit("dve", lambda e, b=bank: e.tensor_copy(kbfA[:, :],
                                                                        b.t[:, 0:512]),
                                 waits=rd, inc="dve")
                    bank.readers.append(t_c)
                    wts2 = B[2].pre_write() + [t_c]
                    t_rot = S.emit("pe", lambda e: e.matmul(pbk[2][:, 0:512], ptb[:, :],
                                                            kbfA[:, :], start=True,
                                                            stop=True),
                                   waits=wts2, inc="pe")
                    B[2].wr = t_rot
                    t_t1 = S.emit("dve", lambda e, b=bank, half=half: e.tensor_mul(
                        t1f[:, :], b.t[:, 0:512],
                        cosfm[:, half * 512:(half + 1) * 512]),
                        waits=rd, inc="dve")
                    bank.readers.append(t_t1)
                    rd2 = B[2].pre_read()
                    t_t2 = S.emit("dve", lambda e, half=half: e.tensor_mul(
                        t2f[:, :], pbk[2][:, 0:512],
                        sinfm[:, half * 512:(half + 1) * 512]),
                        waits=rd2, inc="dve")
                    B[2].readers.append(t_t2)
                    t_src = S.emit("dve", lambda e: e.tensor_add(kfm_t[:, :], t1f[:, :],
                                                                 t2f[:, :]),
                                   waits=[t_t1, t_t2], inc="dve")
                else:
                    t_src = S.emit("dve", lambda e, b=bank: e.tensor_copy(kfm_t[:, :],
                                                                          b.t[:, 0:512]),
                                   waits=rd, inc="dve")
                    bank.readers.append(t_src)
                tb2 = TB[pj_i % 2]
                wts3 = tb2.pre_write() + [t_src, tok_consts]
                last_t = None
                for j in range(4):
                    def mktr(j=j, tb2=tb2):
                        def f(e):
                            return e.transpose(tb2.t[:, j * 128:(j + 1) * 128],
                                               kfm_t[:, j * 128:(j + 1) * 128],
                                               idb[:, :])
                        return f
                    last_t = S.emit("pe", mktr(), waits=(wts3 if j == 0 else []),
                                    inc="pe" if j == 3 else None)
                tb2.wr = last_t

                def mkev(half=half, kv=kv, dst3=dst3, tb2=tb2):
                    def f(e):
                        dst = dst3[:, 4 * half:4 * half + 4, kv * 128:(kv + 1) * 128]
                        src = tb2.t[:, 0:512].rearrange("p (j t) -> p j t", t=128)
                        return e.tensor_copy(dst, src)
                    return f
                t_ev = S.emit("dve", mkev(), waits=tb2.pre_read(), inc="dve")
                tb2.readers.append(t_ev)
    tok_kv_done = S.tok("dve")

    prev_cs = None
    for n in range(NCH):
        for g in range(NG):
            last_xc = None
            for hp in range(4):
                h = 4 * g + hp

                def mkxc(hp=hp, h=h, n=n, g=g):
                    def f(e):
                        return e.tensor_scalar(xc_t[:, hp * 128:(hp + 1) * 128],
                                               v_tm3[:, n, g * 128:(g + 1) * 128],
                                               dt_all[:, n * 32 + h:n * 32 + h + 1],
                                               None, OP.mult)
                    return f
                last_xc = S.emit("dve", mkxc(),
                                 waits=([tok_kv_done, prev_cs] if hp == 0 else []),
                                 inc="dve" if hp == 3 else None)
            last_kd = None
            for hp in range(4):
                h = 4 * g + hp

                def mkkd(hp=hp, h=h, n=n, g=g):
                    def f(e):
                        return e.tensor_scalar(kd_t[:, hp * 128:(hp + 1) * 128],
                                               k_tm3[:, n, g * 128:(g + 1) * 128],
                                               kdsc_all[:, n * 32 + h:n * 32 + h + 1],
                                               None, OP.mult)
                    return f
                last_kd = S.emit("dve", mkkd(), inc="dve" if hp == 3 else None)
            bank = B[5]
            wts = bank.pre_write() + [last_xc, last_kd]
            last_cs = None
            for hp in range(4):
                def mkcs(hp=hp, bank=bank):
                    def f(e):
                        return e.matmul(bank.t[:, hp * 128:(hp + 1) * 128],
                                        kd_t[:, hp * 128:(hp + 1) * 128],
                                        xc_t[:, hp * 128:(hp + 1) * 128],
                                        start=True, stop=True)
                    return f
                last_cs = S.emit("pe", mkcs(), waits=(wts if hp == 0 else []),
                                 inc="pe" if hp == 3 else None)
            bank.wr = last_cs
            prev_cs = last_cs
            rd = bank.pre_read()
            last_up = None
            for hp in range(4):
                h = 4 * g + hp

                def mkup(hp=hp, h=h, n=n, bank=bank):
                    def f(e):
                        blk = state[:, h * 128:(h + 1) * 128]
                        return e.scalar_tensor_tensor(
                            blk, blk, drep_all[:, n * 32 + h:n * 32 + h + 1],
                            bank.t[:, hp * 128:(hp + 1) * 128], OP.mult, OP.add)
                    return f
                last_up = S.emit("dve", mkup(),
                                 waits=(rd + [tok_st0] if hp == 0 else []),
                                 inc="dve" if hp == 3 else None)
            bank.readers.append(last_up)
    tok_state_done = S.tok("dve")
    tok_dall = S.emit("act", lambda e: e.activation(dall[:, :], glsum[:, :], AF.Exp),
                      waits=[tok_gl], inc="act")

    if "P1" in tap_phases:
        for nm, buf, wd, dt_ in [("k_tm", k_tm, 8192, BF16), ("v_tm", v_tm, 8192, BF16),
                                 ("dt_all", dt_all, 256, F32), ("g_all", g_all, 256, F32),
                                 ("kdsc_all", kdsc_all, 256, F32),
                                 ("drep_all", drep_all, 256, F32),
                                 ("s_c", state, 4096, F32), ("dall", dall, 32, F32),
                                 ("hw_fm", hw_fm, 1024, BF16),
                                 ("w_all", w_all, 64, F32),
                                 ("expg_all", expg_all, 256, F32)]:
            tp = tapdecl(nm, [128, wd], dt_)
            dma("sp", tp, buf[:, 0:wd],
                waits=[tok_state_done, tok_dall, S.tok("act"), S.tok("dve")])

    t_cc1 = dma("pool", cc_in.ap()[:, 0:4096], state[:, :], waits=[tok_state_done])
    t_cc2 = dma("pool", cc_in.ap()[:, 4096:4128], dall[:, :], waits=[tok_dall])
    tok_ccin = fence("pool")
    tok_cc = S.emit("pool", lambda e: e.collective_compute(
        "AllGather", OP.bypass, replica_groups=[list(range(NCORES))],
        ins=[cc_in.ap().opt()], outs=[cc_out.ap().opt()]),
        waits=[tok_ccin], inc="cc")

    wbslot = arena[:, 0:8192].rearrange("p (k c) -> p k c", c=256)
    xres = arena_f[:, 4096:6144].rearrange("p (n c) -> p n c", c=256)
    bout = arena_f[:, 6144:6656]
    bhat_sb = wslot_t[:, 0:4096]
    bbase_sb = wslot_t[0:1, 4096:8192]
    tok_bhat = dma("pool", bhat_sb[:, :], bhat_in,
                   waits=slot_readers[0] + slot_readers[1])
    tok_bbase = dma("pool", bbase_sb[0:1, :], bbase_in)
    tok_bbase = fence("pool")
    tok_bhat = tok_bbase
    wb_readers = [S.tok("pe"), S.tok("dve")]
    xres_readers = [S.tok("dve")]
    bout_tok = [None, None]
    for hf in range(NHF):
        tok_wb = dma("pool", wbslot, wb_in[hf], waits=wb_readers)
        tok_wb = fence("pool")
        wb_readers = []
        tok_xr = dma("sp", xres,
                     x_in.rearrange("n p f -> p n f")[:, :, hf * 256:(hf + 1) * 256],
                     waits=xres_readers)
        xres_readers = []
        for n in range(NCH):
            bank = B[n % 2]
            wts = bank.pre_write() + [tok_wb, tok_bhat, tok_bbase, tok_small_done]
            last = None
            for k in range(KC):
                def mk(k=k, n=n, bank=bank):
                    def f(e):
                        return e.matmul(bank.t[:, 0:256],
                                        xn_fm[:, k, n * 128:(n + 1) * 128],
                                        wbslot[:, k, :], start=(k == 0), stop=False)
                    return f
                last = S.emit("pe", mk(), waits=(wts if k == 0 else []))

            def mkmoe(n=n, hf=hf, bank=bank):
                def f(e):
                    return e.matmul(bank.t[:, 0:256], hw_fm[:, n * 128:(n + 1) * 128],
                                    bhat_sb[:, hf * 256:(hf + 1) * 256],
                                    start=False, stop=False)
                return f
            S.emit("pe", mkmoe())

            def mkbias(n=n, hf=hf, bank=bank):
                def f(e):
                    return e.matmul(bank.t[:, 0:256], onesb[0:1, :],
                                    bbase_sb[0:1, hf * 256:(hf + 1) * 256],
                                    start=False, stop=True)
                return f
            last = S.emit("pe", mkbias(), inc="pe")
            bank.wr = last
            wb_readers.append(last)
            rd = bank.pre_read()
            sl = n % 2

            def mkev(n=n, sl=sl, bank=bank):
                def f(e):
                    return e.tensor_add(bout[:, sl * 256:(sl + 1) * 256],
                                        bank.t[:, 0:256], xres[:, n, :])
                return f
            t_ev = S.emit("dve", mkev(), waits=rd + [tok_xr, bout_tok[sl]], inc="dve")
            bank.readers.append(t_ev)
            xres_readers.append(t_ev)
            dma("sp", bscr.ap()[hf, :, n, :],
                bout[:, sl * 256:(sl + 1) * 256], waits=[t_ev])
            bout_tok[sl] = fence("sp")

    if "B1" in tap_phases:
        tp = tapdecl("bscr", [NHF, 128, NCH, 256], F32)
        dma("sp", tp, bscr.ap(), waits=[S.tok("dsp")])

    sslot = arena_f[:, 6656:8704]
    tok_dr = dma("pool", dallrow[0:1, :].rearrange("o (r c) -> o r c", c=32),
                 cc_out.ap().rearrange("(r p) c -> r p c", p=128)[:, 0:1, 4096:4128],
                 waits=[tok_cc])
    tok_dr = fence("pool")
    t_d1 = S.emit("dve", lambda e: e.tensor_mul(dprow[0:1, :], dallrow[0:1, :],
                                                urow[0:1, :]),
                  waits=[tok_dr, tok_consts], inc="dve")
    t_d2 = S.emit("dve", lambda e: e.tensor_add(dprow[0:1, :], dprow[0:1, :],
                                                unot[0:1, :]),
                  waits=[t_d1], inc="dve")
    t_sf = S.emit("dve", lambda e: e.memset(sfx[0:1, 224:256], 1.0), waits=[], inc="dve")
    for j in range(6, -1, -1):
        def mksf(j=j):
            def f(e):
                return e.tensor_mul(sfx[0:1, j * 32:(j + 1) * 32],
                                    sfx[0:1, (j + 1) * 32:(j + 2) * 32],
                                    dprow[0:1, (j + 1) * 32:(j + 2) * 32])
            return f
        t_sf = S.emit("dve", mksf(), waits=[t_d2, t_sf], inc="dve")
    t_pr = S.emit("dve", lambda e: e.tensor_mul(prow2[0:1, :], sfx[0:1, :], mrow[0:1, :]),
                  waits=[t_sf], inc="dve")
    wts = B[2].pre_write() + [t_pr]
    t_mm = S.emit("pe", lambda e: e.matmul(pbk[2][:, 0:256], onesf[0:1, :],
                                           prow2[0:1, :], start=True, stop=True),
                  waits=wts, inc="pe")
    B[2].wr = t_mm
    t_prep = S.emit("dve", lambda e: e.tensor_copy(prep_sc[:, :], pbk[2][:, 0:256]),
                    waits=B[2].pre_read(), inc="dve")
    B[2].readers.append(t_prep)
    ss_readers = []
    first = True
    for j in range(NCORES):
        for hh in range(2):
            tok_s = dma("sp", sslot[:, :],
                        cc_out.ap()[j * 128:(j + 1) * 128, hh * 2048:(hh + 1) * 2048],
                        waits=ss_readers + [tok_cc, (tok_ccin if first else None)])
            tok_s = fence("sp")
            ss_readers = []
            last_in = None
            for hq in range(16):
                h = hh * 16 + hq

                def mkin(j=j, h=h, hq=hq, first=first):
                    def f(e):
                        blk = state[:, h * 128:(h + 1) * 128]
                        src = sslot[:, hq * 128:(hq + 1) * 128]
                        sc = prep_sc[:, j * 32 + h:j * 32 + h + 1]
                        if j == 0:
                            return e.tensor_scalar(blk, src, sc, None, OP.mult)
                        return e.scalar_tensor_tensor(blk, src, sc, blk, OP.mult, OP.add)
                    return f
                last_in = S.emit("dve", mkin(),
                                 waits=([tok_s, t_prep] if hq == 0 else []),
                                 inc="dve" if hq == 15 else None)
            ss_readers.append(last_in)
        first = False
    tok_incoming = S.tok("dve")

    if "CC" in tap_phases:
        tp = tapdecl("incoming", [128, 4096], F32)
        dma("sp", tp, state[:, :], waits=[tok_incoming])

    qt_bf = arena[:, 0:512]
    t1q = arena_f[:, 256:768]
    t2q = arena_f[:, 768:1280]
    kfm_a = arena[:, 2560:2688]
    xc_a = arena[:, 2688:3200]
    kd_a = arena[:, 3200:3712]
    sbf_a = arena[:, 3712:4224]
    dsb_a = arena_f[:, 2112:2624]
    tsb_a = arena_f[:, 2624:3136]
    ysb_a = arena_f[:, 3136:3648]
    yint_a = arena_f[:, 3648:4160]
    sq_a = arena_f[:, 4160:4672]
    stbf_a = arena[:, 9344:9856]
    qfm_g = arena[:, 9856:13952]
    qfm_g3 = qfm_g.rearrange("p (n c) -> p n c", c=512)
    ynst = arena[:, 13952:18048]
    ynst_tok = [None, None]

    a2_start = [tok_incoming, S.tok("dsp"), S.tok("act"), S.tok("pe")]
    prev_t_dx = None
    prev_t_y = None
    prev_yb = None
    prev_cs_a2 = None
    prev_t_ln2 = None
    for g in range(NG):
        qfm_toks = []
        first_qw = True
        for hp in range(4):
            h = 4 * g + hp
            sl = sl_i % 2
            sl_i += 1
            tok_w = dma("pool", wslot[sl], wq_in[h],
                        waits=slot_readers[sl] + ([S.tok("pe")] if g == 0 else []))
            tok_w = fence("pool")
            slot_readers[sl] = []
            for half in range(2):
                bank = B[pj_i % 2]
                pj_i += 1
                wts = bank.pre_write() + [tok_w] + (a2_start if g == 0 else [])
                last = None
                for k in range(KC):
                    def mk(k=k, half=half, sl=sl, bank=bank):
                        def f(e):
                            return e.matmul(bank.t[:, 0:512], wslot[sl][:, k, :],
                                            xn_fm[:, k, half * 512:(half + 1) * 512],
                                            start=(k == 0), stop=(k == KC - 1))
                        return f
                    last = S.emit("pe", mk(), waits=(wts if k == 0 else []),
                                  inc="pe" if k == KC - 1 else None)
                bank.wr = last
                slot_readers[sl].append(last)
                rd = bank.pre_read()
                t_c = S.emit("dve", lambda e, b=bank: e.tensor_copy(qt_bf[:, :],
                                                                    b.t[:, 0:512]),
                             waits=rd, inc="dve")
                bank.readers.append(t_c)
                wts2 = B[2].pre_write() + [t_c]
                t_rot = S.emit("pe", lambda e: e.matmul(pbk[2][:, 0:512], ptb[:, :],
                                                        qt_bf[:, :], start=True,
                                                        stop=True),
                               waits=wts2, inc="pe")
                B[2].wr = t_rot
                t_t1 = S.emit("dve", lambda e, b=bank, half=half: e.tensor_mul(
                    t1q[:, :], b.t[:, 0:512],
                    cosfm[:, half * 512:(half + 1) * 512]),
                    waits=rd, inc="dve")
                bank.readers.append(t_t1)
                rd2 = B[2].pre_read()
                t_t2 = S.emit("dve", lambda e, half=half: e.tensor_mul(
                    t2q[:, :], pbk[2][:, 0:512],
                    sinfm[:, half * 512:(half + 1) * 512]),
                    waits=rd2, inc="dve")
                B[2].readers.append(t_t2)

                def mkqw(half=half, hp=hp):
                    def f(e):
                        dst = qfm_g.rearrange("p (n c t) -> p n c t", c=4, t=128)[
                            :, 4 * half:4 * half + 4, hp, :]
                        return e.tensor_add(dst, t1q[:, :].rearrange(
                            "p (n t) -> p n t", t=128), t2q[:, :].rearrange(
                            "p (n t) -> p n t", t=128))
                    return f
                t_qw = S.emit("dve", mkqw(),
                              waits=[t_t1, t_t2] + ([prev_yb] if first_qw else []),
                              inc="dve")
                first_qw = False
                qfm_toks.append(t_qw)
        if "A2DBG" in tap_phases and g == 0:
            tp = tapdecl("qfm_g0", [128, 4096], BF16)
            dma("sp", tp, qfm_g[:, :], waits=qfm_toks)
        for n in range(NCH):
            tb2 = TB[n % 2]
            wts = tb2.pre_write() + [tok_kv_done]
            t_tr = S.emit("pe", lambda e, n=n, g=g, tb2=tb2: e.transpose(
                tb2.t[:, 0:128], k_tm3[:, n, g * 128:(g + 1) * 128], idb[:, :]),
                waits=wts, inc="pe")
            tb2.wr = t_tr
            t_kf = S.emit("dve", lambda e, tb2=tb2: e.tensor_copy(kfm_a[:, :],
                                                                  tb2.t[:, 0:128]),
                          waits=tb2.pre_read(), inc="dve")
            tb2.readers.append(t_kf)
            bank3 = B[3]
            wts = bank3.pre_write() + [t_kf] + qfm_toks
            t_sc = S.emit("pe", lambda e, n=n: e.matmul(pbk[3][:, 0:512], kfm_a[:, :],
                                                        qfm_g3[:, n, :], start=True,
                                                        stop=True),
                          waits=wts, inc="pe")
            bank3.wr = t_sc
            t_tl = dma("pool", tlcur[0:1, :].rearrange("o (c t) -> o c t", t=128),
                       ngT[4 * g:4 * g + 4, n * 128:(n + 1) * 128], waits=[S.tok("pe")])
            t_rh = dma("pool", rhcur[1:2, :].rearrange("o (c t) -> o c t", t=128),
                       gT[4 * g:4 * g + 4, n * 128:(n + 1) * 128], waits=[])
            t_rh = fence("pool")
            bank2 = B[2]
            wts = bank2.pre_write() + [t_rh]
            last_T = None
            for hp in range(4):
                def mkT(hp=hp):
                    def f(e):
                        return e.matmul(pbk[2][:, hp * 128:(hp + 1) * 128],
                                        tlcur[0:2, hp * 128:(hp + 1) * 128],
                                        rhcur[0:2, hp * 128:(hp + 1) * 128],
                                        start=True, stop=True)
                    return f
                last_T = S.emit("pe", mkT(), waits=(wts if hp == 0 else []),
                                inc="pe" if hp == 3 else None)
            bank2.wr = last_T
            rd2 = bank2.pre_read()
            t_tm = S.emit("dve", lambda e: e.tensor_add(tsb_a[:, :], pbk[2][:, 0:512],
                                                        mask4[:, :]),
                          waits=rd2 + [prev_t_dx], inc="dve")
            bank2.readers.append(t_tm)
            t_dx = S.emit("act", lambda e: e.activation(dsb_a[:, :], tsb_a[:, :], AF.Exp),
                          waits=[t_tm], inc="act")
            prev_t_dx = t_dx
            rd3 = bank3.pre_read()
            t_sb = S.emit("dve", lambda e: e.tensor_mul(sbf_a[:, :], pbk[3][:, 0:512],
                                                        dsb_a[:, :]),
                          waits=rd3 + [t_dx], inc="dve")
            bank3.readers.append(t_sb)
            last_xc = None
            for hp in range(4):
                h = 4 * g + hp

                def mkxc(hp=hp, h=h, n=n, g=g):
                    def f(e):
                        return e.tensor_scalar(xc_a[:, hp * 128:(hp + 1) * 128],
                                               v_tm3[:, n, g * 128:(g + 1) * 128],
                                               dt_all[:, n * 32 + h:n * 32 + h + 1],
                                               None, OP.mult)
                    return f
                last_xc = S.emit("dve", mkxc(),
                                 waits=([prev_cs_a2] if hp == 0 else []),
                                 inc="dve" if hp == 3 else None)
            last_kd = None
            for hp in range(4):
                h = 4 * g + hp

                def mkkd(hp=hp, h=h, n=n, g=g):
                    def f(e):
                        return e.tensor_scalar(kd_a[:, hp * 128:(hp + 1) * 128],
                                               k_tm3[:, n, g * 128:(g + 1) * 128],
                                               kdsc_all[:, n * 32 + h:n * 32 + h + 1],
                                               None, OP.mult)
                    return f
                last_kd = S.emit("dve", mkkd(), inc="dve" if hp == 3 else None)
            t_st = S.emit("dve", lambda e, g=g: e.tensor_copy(
                stbf_a[:, :], state[:, g * 512:(g + 1) * 512]),
                waits=[tok_incoming, prev_yb], inc="dve")
            bank4 = B[4]
            wts4 = bank4.pre_write() + [t_sb, last_xc]
            last_ya = None
            for hp in range(4):
                def mkya(hp=hp):
                    def f(e):
                        return e.matmul(pbk[4][:, hp * 128:(hp + 1) * 128],
                                        sbf_a[:, hp * 128:(hp + 1) * 128],
                                        xc_a[:, hp * 128:(hp + 1) * 128],
                                        start=True, stop=True)
                    return f
                last_ya = S.emit("pe", mkya(), waits=(wts4 if hp == 0 else []),
                                 inc="pe" if hp == 3 else None)
            bank4.wr = last_ya
            bank5 = B[5]
            wts5 = bank5.pre_write() + [t_st] + qfm_toks
            last_yb = None
            for hp in range(4):
                def mkyb(hp=hp, n=n):
                    def f(e):
                        return e.matmul(pbk[5][:, hp * 128:(hp + 1) * 128],
                                        qfm_g3[:, n, hp * 128:(hp + 1) * 128],
                                        stbf_a[:, hp * 128:(hp + 1) * 128],
                                        start=True, stop=True)
                    return f
                last_yb = S.emit("pe", mkyb(), waits=(wts5 if hp == 0 else []),
                                 inc="pe" if hp == 3 else None)
            bank5.wr = last_yb
            prev_yb = last_yb
            rd5 = bank5.pre_read()
            last_yi = None
            for hp in range(4):
                h = 4 * g + hp

                def mkyi(hp=hp, h=h, n=n):
                    def f(e):
                        return e.activation(yint_a[:, hp * 128:(hp + 1) * 128],
                                            pbk[5][:, hp * 128:(hp + 1) * 128],
                                            AF.Identity,
                                            scale=expg_all[:, n * 32 + h:n * 32 + h + 1])
                    return f
                last_yi = S.emit("act", mkyi(),
                                 waits=(rd5 + [prev_t_y] if hp == 0 else []),
                                 inc="act" if hp == 3 else None)
            bank5.readers.append(last_yi)
            rd4 = bank4.pre_read()
            t_y = S.emit("dve", lambda e: e.tensor_add(ysb_a[:, :], pbk[4][:, 0:512],
                                                       yint_a[:, :]),
                         waits=rd4 + [last_yi], inc="dve")
            bank4.readers.append(t_y)
            prev_t_y = t_y
            wts5b = bank5.pre_write() + [last_kd, last_xc]
            last_cs = None
            for hp in range(4):
                def mkcs(hp=hp):
                    def f(e):
                        return e.matmul(pbk[5][:, hp * 128:(hp + 1) * 128],
                                        kd_a[:, hp * 128:(hp + 1) * 128],
                                        xc_a[:, hp * 128:(hp + 1) * 128],
                                        start=True, stop=True)
                    return f
                last_cs = S.emit("pe", mkcs(), waits=(wts5b if hp == 0 else []),
                                 inc="pe" if hp == 3 else None)
            bank5.wr = last_cs
            prev_cs_a2 = last_cs
            rd5b = bank5.pre_read()
            last_up = None
            for hp in range(4):
                h = 4 * g + hp

                def mkup(hp=hp, h=h, n=n):
                    def f(e):
                        blk = state[:, h * 128:(h + 1) * 128]
                        return e.scalar_tensor_tensor(
                            blk, blk, drep_all[:, n * 32 + h:n * 32 + h + 1],
                            pbk[5][:, hp * 128:(hp + 1) * 128], OP.mult, OP.add)
                    return f
                last_up = S.emit("dve", mkup(), waits=(rd5b + [t_st] if hp == 0 else []),
                                 inc="dve" if hp == 3 else None)
            bank5.readers.append(last_up)
            t_sq2 = S.emit("dve", lambda e: e.tensor_mul(sq_a[:, :], ysb_a[:, :],
                                                         ysb_a[:, :]),
                           waits=[t_y], inc="dve")
            t_ssq2 = S.emit("dve", lambda e: e.tensor_reduce(
                stats[:, 0:4], sq_a[:, :].rearrange("p (c t) -> p c t", t=128),
                AX.X, OP.add),
                waits=[t_sq2, prev_t_ln2], inc="dve")
            t_ln2 = S.emit("act", lambda e: e.activation(stats[:, 4:8], stats[:, 0:4],
                                                         AF.Ln, scale=1.0 / 128,
                                                         bias=crow[:, 0:1]),
                           waits=[t_ssq2], inc="act")
            prev_t_ln2 = t_ln2
            t_rs2 = S.emit("act", lambda e: e.activation(stats[:, 4:8], stats[:, 4:8],
                                                         AF.Exp, scale=-0.5),
                           waits=[t_ln2], inc="act")
            if "A2DBG" in tap_phases and g == 0 and n == 0:
                for nm_, buf_, wd_, dt2 in [("ysb00", ysb_a, 512, F32),
                                            ("yint00", yint_a, 512, F32),
                                            ("dsb00", dsb_a, 512, F32),
                                            ("tsb00", tsb_a, 512, F32),
                                            ("sbf00", sbf_a, 512, BF16),
                                            ("stbf00", stbf_a, 512, BF16),
                                            ("xc00", xc_a, 512, BF16),
                                            ("kd00", kd_a, 512, BF16),
                                            ("kfm00", kfm_a, 128, BF16)]:
                    tp_ = tapdecl(nm_, [128, wd_], dt2)
                    dma("sp", tp_, buf_[:, 0:wd_],
                        waits=[t_y, last_yi, S.tok("dve"), S.tok("act")])
            sl = (n // 4) % 2
            last_yn = None
            for hp in range(4):
                h = 4 * g + hp

                def mkyn(hp=hp, h=h, n=n, sl=sl):
                    def f(e):
                        dst = ynst[:, sl * 2048 + (n % 4) * 512 + hp * 128:
                                   sl * 2048 + (n % 4) * 512 + (hp + 1) * 128]
                        return e.scalar_tensor_tensor(dst,
                                                      ysb_a[:, hp * 128:(hp + 1) * 128],
                                                      stats[:, 4 + hp:5 + hp],
                                                      preps[:, 0:128],
                                                      OP.mult, OP.mult)
                    return f
                last_yn = S.emit("dve", mkyn(),
                                 waits=([t_rs2, ynst_tok[sl]] if hp == 0 else []),
                                 inc="dve" if hp == 3 else None)
            if n % 4 == 3:
                half = n // 4
                dma("sp", yscr.ap()[g, :, half * 4:half * 4 + 4, :],
                    ynst[:, sl * 2048:(sl + 1) * 2048].rearrange(
                        "p (c t) -> p c t", t=512),
                    waits=[last_yn])
                ynst_tok[sl] = fence("sp")
    tok_a2_done = [S.tok("dve"), S.tok("act"), S.tok("pe"), S.tok("dsp")]

    if "A2" in tap_phases:
        tp = tapdecl("yscr", [NG, 128, NCH, 512], BF16)
        dma("sp", tp, yscr.ap(), waits=tok_a2_done)
        tp2 = tapdecl("state_a2", [128, 4096], F32)
        dma("sp", tp2, state[:, :], waits=tok_a2_done)

    wgslot = arena[:, 0:16384].rearrange("p (k c) -> p k c", c=512)
    wg_readers = [S.tok("dve"), S.tok("pe"), S.tok("dsp")]
    gs3 = gatesil.rearrange("p (gn c) -> p gn c", c=512)
    b2a_first = list(tok_a2_done)
    for g in range(NG):
        tok_wg = dma("pool", wgslot, wg_in[g], waits=wg_readers)
        tok_wg = fence("pool")
        wg_readers = []
        for n in range(NCH):
            bank = B[(g * NCH + n) % 2]
            wts = bank.pre_write() + [tok_wg] + (b2a_first if g == 0 and n == 0 else [])
            last = None
            for k in range(KC):
                def mk(k=k, n=n, bank=bank):
                    def f(e):
                        return e.matmul(bank.t[:, 0:512],
                                        xn_fm[:, k, n * 128:(n + 1) * 128],
                                        wgslot[:, k, :], start=(k == 0),
                                        stop=(k == KC - 1))
                    return f
                last = S.emit("pe", mk(), waits=(wts if k == 0 else []),
                              inc="pe" if k == KC - 1 else None)
            bank.wr = last
            wg_readers.append(last)
            rd = bank.pre_read()

            def mksil(g=g, n=n, bank=bank):
                def f(e):
                    return e.activation(gs3[:, g * NCH + n, :], bank.t[:, 0:512], AF.Silu)
                return f
            t_sil = S.emit("act", mksil(), waits=rd, inc="act")
            bank.readers.append(t_sil)
    tok_b2a = [S.tok("act"), S.tok("pe")]

    ygtmp = arena[:, 0:512]
    ynst2 = arena[:, 512:4608]
    yn2_tok = [None, None]
    yg_first = list(tok_b2a) + [S.tok("dve"), S.tok("dsp")]
    prev_b2b_tr = None
    for g in range(NG):
        for half in range(2):
            sl = (g * 2 + half) % 2
            tok_yn = dma("sp", ynst2[:, sl * 2048:(sl + 1) * 2048].rearrange(
                "p (c t) -> p c t", t=512),
                yscr.ap()[g, :, half * 4:half * 4 + 4, :],
                waits=[yn2_tok[sl]] + (yg_first if g == 0 and half == 0 else []))
            tok_yn = fence("sp")
            for nn in range(4):
                n = half * 4 + nn

                def mkyg(g=g, n=n, nn=nn, sl=sl):
                    def f(e):
                        return e.tensor_mul(ygtmp[:, :],
                                            gs3[:, g * NCH + n, :],
                                            ynst2[:, sl * 2048 + nn * 512:
                                                  sl * 2048 + (nn + 1) * 512])
                    return f
                t_yg = S.emit("dve", mkyg(), waits=[tok_yn, prev_b2b_tr] + (
                    yg_first if g == 0 and half == 0 and nn == 0 else []), inc="dve")
                yn2_tok[sl] = t_yg
                tb2 = TB[n % 2]
                wts = tb2.pre_write() + [t_yg]
                last_t = None
                for hp in range(4):
                    def mktr(hp=hp, tb2=tb2):
                        def f(e):
                            return e.transpose(tb2.t[:, hp * 128:(hp + 1) * 128],
                                               ygtmp[:, hp * 128:(hp + 1) * 128],
                                               idb[:, :])
                        return f
                    last_t = S.emit("pe", mktr(), waits=(wts if hp == 0 else []),
                                    inc="pe" if hp == 3 else None)
                tb2.wr = last_t
                prev_b2b_tr = last_t

                def mkev(g=g, n=n, tb2=tb2):
                    def f(e):
                        dst = yg_fm[:, 4 * g:4 * g + 4, n * 128:(n + 1) * 128]
                        src = tb2.t[:, 0:512].rearrange("p (c t) -> p c t", t=128)
                        return e.tensor_copy(dst, src)
                    return f
                t_ev = S.emit("dve", mkev(), waits=tb2.pre_read(), inc="dve")
                tb2.readers.append(t_ev)
    tok_b2b = [S.tok("dve"), S.tok("pe")]

    if "B2" in tap_phases:
        tp = tapdecl("yg_fm", [128, 32768], BF16)
        dma("sp", tp, u1[:, 0:32768], waits=tok_b2b)
        tp2 = tapdecl("gatesil", [128, 32768], BF16)
        dma("sp", tp2, gatesil[:, :], waits=tok_b2b)

    woslot = arena[:, 0:8192].rearrange("p (k c) -> p k c", c=256)
    bpst = arena_f[:, 4096:6144].rearrange("p (n c) -> p n c", c=256)
    outst = arena_f[:, 6144:6656]
    wo_readers = list(tok_b2b)
    bp_readers = list(tok_b2b)
    out_tok = [None, None]
    for hf in range(NHF):
        tok_wo = dma("pool", woslot, wo_in[hf], waits=wo_readers)
        tok_wo = fence("pool")
        wo_readers = []
        tok_bp = dma("sp", bpst, bscr.ap()[hf], waits=bp_readers + [bout_tok[0], bout_tok[1]])
        tok_bp = fence("sp")
        bp_readers = []
        for n in range(NCH):
            bank = B[n % 2]
            wts = bank.pre_write() + [tok_wo] + (tok_b2b if hf == 0 and n == 0 else [])
            last = None
            for ec in range(KC):
                def mk(ec=ec, n=n, bank=bank):
                    def f(e):
                        return e.matmul(bank.t[:, 0:256],
                                        yg_fm[:, ec, n * 128:(n + 1) * 128],
                                        woslot[:, ec, :], start=(ec == 0),
                                        stop=(ec == KC - 1))
                    return f
                last = S.emit("pe", mk(), waits=(wts if ec == 0 else []),
                              inc="pe" if ec == KC - 1 else None)
            bank.wr = last
            wo_readers.append(last)
            rd = bank.pre_read()
            sl = n % 2

            def mkev(n=n, sl=sl, bank=bank):
                def f(e):
                    return e.tensor_add(outst[:, sl * 256:(sl + 1) * 256],
                                        bank.t[:, 0:256], bpst[:, n, :])
                return f
            t_ev = S.emit("dve", mkev(), waits=rd + [tok_bp, out_tok[sl]], inc="dve")
            bank.readers.append(t_ev)
            bp_readers.append(t_ev)
            dma("sp", o_out[n][:, hf * 256:(hf + 1) * 256],
                outst[:, sl * 256:(sl + 1) * 256], waits=[t_ev])
            out_tok[sl] = fence("sp")

    S.emit("sp", lambda e: e.nop(), waits=[S.tok("dsp"), S.tok("dpool")])

    S.check()
    S.materialize(nc, sems)
    st.close()
    return nc, taps



def _tile_oi(w, nblk, bs):
    O, I = w.shape
    assert O == nblk * bs and I == KC * 128
    return np.ascontiguousarray(
        np.asarray(w).reshape(nblk, bs, KC, 128).transpose(0, 3, 2, 1))


def _consts(core):
    c0 = core * T
    inv = 1.0 / (1e4 ** (np.arange(0, D, 2, dtype=np.float64) / D))
    pos = np.arange(c0, c0 + T, dtype=np.float64)
    ang = pos[None, :] * np.concatenate([inv, inv])[:, None]
    cosfm = np.cos(ang).astype(np.float32)
    sinfm = np.sin(ang).astype(np.float32)
    jj, ii = np.meshgrid(np.arange(128), np.arange(128), indexing="ij")
    cuml = (jj <= ii).astype(np.float32)
    mask = np.where(ii >= jj, 0.0, -1e9).astype(np.float32)
    mask4 = np.tile(mask, (1, 4))
    idf = np.eye(128, dtype=np.float32)
    idbm = idf.astype(bfdt)
    P = np.zeros((128, 128), np.float32)
    for dd in range(64):
        P[dd, dd + 64] = -1.0
    for dd in range(64, 128):
        P[dd, dd - 64] = 1.0
    ptb = np.ascontiguousarray(P.T).astype(bfdt)
    onesb = np.ones((1, 128), np.float32).astype(bfdt)
    onesf = np.ones((1, 128), np.float32)
    crw = np.zeros((128, 8), np.float32)
    crw[:, 0] = 1e-5
    crw[:, 1] = 1.0
    crw[:, 2] = np.log(2.0)
    u = (np.arange(8) < core).astype(np.float32)
    urow = np.repeat(u, 32)[None, :].astype(np.float32)
    unot = 1.0 - urow
    mrow = urow.copy()
    return dict(cosfm=cosfm, sinfm=sinfm, cumlt=cuml, mask4=mask4, idf=idf,
                idb=idbm, ptb=ptb, onesb=onesb, onesf=onesf, crow=crw,
                urow=np.ascontiguousarray(urow), unot=np.ascontiguousarray(unot),
                mrow=np.ascontiguousarray(mrow))


_PROG_CACHE = {}


def _get_prog(taps_key=()):
    key = tuple(taps_key)
    if key not in _PROG_CACHE:
        _PROG_CACHE[key] = build_program(key)
    return _PROG_CACHE[key]


def kernel(**inputs):
    f32 = np.float32
    x = np.ascontiguousarray(np.asarray(inputs["L0_attn_hidden_states"], f32)
                             .reshape(NCORES, NCH, 128, DM))
    Wq = np.asarray(inputs["L0_attn_Wq"], f32)
    Wk = np.asarray(inputs["L0_attn_Wk"], f32)
    Wv = np.asarray(inputs["L0_attn_Wv"], f32)
    Wo = np.asarray(inputs["L0_attn_Wo"], f32)
    Wg = np.asarray(inputs["L0_attn_Wg"], f32)
    Win = np.asarray(inputs["L0_attn_Win"], f32)
    dt_b = np.asarray(inputs["L0_attn_dt_bias"], f32)
    A_log = np.asarray(inputs["L0_attn_A_log_bias"], f32)
    gnorm = np.asarray(inputs["L0_attn_gnorm_w"], f32)
    W_base = np.asarray(inputs["L0_ffn_W_base"], f32)
    b_base = np.asarray(inputs["L0_ffn_b_base"], f32)
    W_gate = np.asarray(inputs["L0_ffn_W_gate"], f32)
    W_thr = np.asarray(inputs["L0_ffn_W_thr"], f32)
    b_thr = np.asarray(inputs["L0_ffn_b_thr"], f32)
    A_w = np.asarray(inputs["L0_ffn_A_w"], f32)
    B_w = np.asarray(inputs["L0_ffn_B_w"], f32)

    wq_t = _tile_oi(Wq, H, 128)
    wg_t = _tile_oi(Wg, NG, 512)
    wk_t = _tile_oi(Wk, KVH, 128)
    wv_t = _tile_oi(Wv, KVH, 128)
    wo_t = _tile_oi(Wo, NHF, 256)
    wb_t = _tile_oi(W_base, NHF, 256)
    a_hat = np.ascontiguousarray(A_w.transpose(2, 0, 1).reshape(DM, 128))
    smallw = np.concatenate([a_hat, W_gate.T, W_thr[:, None], Win.T], axis=1)
    smallw_t = np.ascontiguousarray(smallw.reshape(KC, 128, 169).transpose(1, 0, 2))
    bhat = np.ascontiguousarray(B_w.transpose(0, 2, 1).reshape(128, DM))
    bbase = np.ascontiguousarray(b_base[None, :])
    prow = np.concatenate([gnorm, A_log, dt_b, b_thr])[None, :].astype(f32)

    shared = dict(wq=wq_t, wg=wg_t, wk=wk_t, wv=wv_t, wo=wo_t, wb=wb_t,
                  smallw=smallw_t, bhat=bhat, bbase=bbase,
                  prow=np.ascontiguousarray(prow))

    taps_key = tuple(_DEBUG_TAPS)
    nc, taps = _get_prog(taps_key)
    in_maps = []
    for c in range(NCORES):
        m = dict(shared)
        m["x"] = np.ascontiguousarray(x[c])
        m.update(_consts(c))
        in_maps.append(m)
    res = run_bass_kernel_spmd(nc, in_maps, core_ids=list(range(NCORES)))
    out = np.concatenate([np.asarray(r["o_out"], f32).reshape(T, DM)
                          for r in res.results], axis=0)
    if taps_key:
        kernel._last_results = res.results
    return out.reshape(1, NCORES * T, DM)


if __name__ == "__main__":
    nc, _ = build_program()
    print("BUILD OK")


# version 03
# speedup vs baseline: 1.0388x; 1.0388x over previous
import numpy as np
import ml_dtypes
import concourse.bass as bass
import concourse.mybir as mybir
from concourse.bass_utils import run_bass_kernel_spmd
from contextlib import ExitStack

F32 = mybir.dt.float32
BF16 = mybir.dt.bfloat16
AF = mybir.ActivationFunctionType
OP = mybir.AluOpType
AX = mybir.AxisListType
bfdt = ml_dtypes.bfloat16

NCORES = 8
T = 1024
NCH = 8
CH = 128
H = 32
KVH = 8
D = 128
DM = 4096
KC = 32
NG = 8
NHF = 16

ENGS = ("pe", "dve", "act", "pool", "sp")

_DEBUG_TAPS = []


class Sched:
    def __init__(self):
        self.q = {e: [] for e in ENGS}
        self.cnt = {}

    def emit(self, eng, fn, waits=(), inc=None, inc_by=1):
        waits = tuple(w for w in waits if w is not None)
        self.q[eng].append((fn, waits, inc, inc_by))
        if inc is not None:
            self.cnt[inc] = self.cnt.get(inc, 0) + inc_by
            return (inc, self.cnt[inc])
        return None

    def tok(self, sem):
        return (sem, self.cnt.get(sem, 0))

    def check(self):
        ptr = {e: 0 for e in ENGS}
        val = {}
        progress = True
        while progress:
            progress = False
            for e in ENGS:
                while ptr[e] < len(self.q[e]):
                    fn, waits, inc, inc_by = self.q[e][ptr[e]]
                    if all(val.get(s, 0) >= v for s, v in waits):
                        if inc is not None:
                            val[inc] = val.get(inc, 0) + inc_by
                        ptr[e] += 1
                        progress = True
                    else:
                        break
        stuck = {e: ptr[e] for e in ENGS if ptr[e] < len(self.q[e])}
        if stuck:
            det = {}
            for e, p in stuck.items():
                fn, waits, inc, _ = self.q[e][p]
                det[e] = (p, len(self.q[e]),
                          [(s, v, val.get(s, 0)) for s, v in waits if val.get(s, 0) < v])
            raise RuntimeError(f"schedule deadlock: {det}")

    def materialize(self, nc, sems):
        with nc.Block() as block:
            decos = {"pe": block.tensor, "dve": block.vector, "act": block.scalar,
                     "pool": block.gpsimd, "sp": block.sync}
            for eng in ENGS:
                items = self.q[eng]

                def body_fn(e, items=items, eng=eng):
                    last = {}
                    for fn, waits, inc, inc_by in items:
                        for s, v in waits:
                            if last.get(s, 0) < v:
                                e.wait_ge(sems[s], v)
                                last[s] = v
                        inst = fn(e)
                        if inc is not None:
                            assert inst is not None, f"inc op returned None on {eng}"
                            inst.then_inc(sems[inc], inc_by)
                decos[eng](body_fn)


class Bank:
    def __init__(self, tile):
        self.t = tile
        self.wr = None
        self.readers = []

    def pre_write(self):
        w = list(self.readers)
        self.readers = []
        if self.wr is not None:
            w.append(self.wr)
        return w

    def pre_read(self):
        return [self.wr] if self.wr is not None else []


def build_program(debug_taps=()):
    nc = bass.Bass("TRN2", target_bir_lowering=False)
    tap_phases = set(debug_taps)

    def din(name, shape, dt=F32):
        return nc.dram_tensor(name, list(shape), dt, kind="ExternalInput").ap()

    x_in = din("x", [NCH, 128, DM])
    wq_in = din("wq", [H, 128, KC, 128])
    wg_in = din("wg", [H, 128, KC, 128])
    wk_in = din("wk", [KVH, 128, KC, 128])
    wv_in = din("wv", [KVH, 128, KC, 128])
    wo_in = din("wo", [KC, 128, KC, 128])
    wb_in = din("wb", [KC, 128, KC, 128])
    xfm_in = din("x_fm", [KC, 128, T])
    smallw_in = din("smallw", [128, KC, 169])
    bhat_in = din("bhat", [128, DM])
    bbase_in = din("bbase", [1, DM])
    prow_in = din("prow", [1, 193])
    cosfm_in = din("cosfm", [128, T])
    sinfm_in = din("sinfm", [128, T])
    cum_in = din("cumlt", [128, 128])
    mask4_in = din("mask4", [128, 512])
    idb_in = din("idb", [128, 128], BF16)
    idf_in = din("idf", [128, 128])
    ptb_in = din("ptb", [128, 128], BF16)
    onesb_in = din("onesb", [1, 512], BF16)
    onesf_in = din("onesf", [1, 128])
    crow_in = din("crow", [128, 8])
    urow_in = din("urow", [1, 256])
    unot_in = din("unot", [1, 256])
    mrow_in = din("mrow", [1, 256])

    o_out = nc.dram_tensor("o_out", [KC, 128, T], F32, kind="ExternalOutput").ap()

    cc_in = nc.dram_tensor("cc_in", [128, 4128], F32, kind="Internal")
    cc_out = nc.dram_tensor("cc_out", [NCORES * 128, 4128], F32, kind="Internal")
    yscr = nc.dram_tensor("yscr", [NG, 128, NCH, 512], BF16, kind="Internal")
    bscr = nc.dram_tensor("bscr", [KC, 128, T], F32, kind="Internal")

    taps = {}

    def tapdecl(name, shape, dt=F32):
        taps[name] = nc.dram_tensor("tap_" + name, list(shape), dt,
                                    kind="ExternalOutput").ap()
        return taps[name]

    st = ExitStack()

    def sbuf(name, shape, dt):
        return st.enter_context(nc.sbuf_tensor(name, list(shape), dt))

    cosfm = sbuf("sb_cosfm", [128, T], F32)
    sinfm = sbuf("sb_sinfm", [128, T], F32)
    cum = sbuf("sb_cum", [128, 128], F32)
    mask4 = sbuf("sb_mask4", [128, 512], F32)
    idb = sbuf("sb_idb", [128, 128], BF16)
    idf = sbuf("sb_idf", [128, 128], F32)
    ptb = sbuf("sb_ptb", [128, 128], BF16)
    onesb = sbuf("sb_onesb", [1, 512], BF16)
    onesf = sbuf("sb_onesf", [1, 128], F32)
    crow = sbuf("sb_crow", [128, 8], F32)
    urow = sbuf("sb_urow", [1, 256], F32)
    unot = sbuf("sb_unot", [1, 256], F32)
    mrow = sbuf("sb_mrow", [1, 256], F32)
    prow = sbuf("sb_prow", [1, 193], F32)
    preps = sbuf("sb_preps", [128, 193], F32)
    arep = sbuf("sb_arep", [128, 32], F32)
    stats = sbuf("sb_stats", [128, 16], F32)
    tlcur = sbuf("sb_tlcur", [2, 512], F32)
    rhcur = sbuf("sb_rhcur", [2, 512], F32)
    grow_t = sbuf("sb_grow_t", [1, 32], F32)
    dallrow = sbuf("sb_dallrow", [1, 256], F32)
    dprow = sbuf("sb_dprow", [1, 256], F32)
    sfx = sbuf("sb_sfx", [1, 256], F32)
    prow2 = sbuf("sb_prow2", [1, 256], F32)
    prep_sc = sbuf("sb_prep_sc", [128, 256], F32)

    u1 = sbuf("sb_u1", [128, 32768], BF16)
    ovb = sbuf("sb_ovb", [128, 32768], BF16)
    wslot_t = sbuf("sb_wslot", [128, 8192], BF16)
    arena = sbuf("sb_arena", [128, 18432], BF16)
    arena_f = arena.bitcast(F32)
    ovb_f = ovb.bitcast(F32)

    k_tm = ovb[:, 0:8192]
    v_tm = ovb[:, 8192:16384]
    state = ovb_f[:, 8192:12288]
    dt_all = ovb_f[:, 12288:12544]
    g_all = ovb_f[:, 12544:12800]
    expg_all = ovb_f[:, 12800:13056]
    kdsc_all = ovb_f[:, 13056:13312]
    drep_all = ovb_f[:, 13312:13568]
    hw_fm = ovb[:, 27136:28160]
    w_all = ovb_f[:, 14080:14144]
    glsum = ovb_f[:, 14144:14176]
    dall = ovb_f[:, 14176:14208]
    gT = ovb_f[0:32, 14208:15232]
    ngT = ovb_f[0:32, 15232:16256]
    gatesil = ovb[:, 0:32768]

    gs3 = gatesil.rearrange("p (hc t) -> p hc t", t=1024)
    k_tm3 = k_tm.rearrange("p (n f) -> p n f", f=1024)
    v_tm3 = v_tm.rearrange("p (n f) -> p n f", f=1024)
    xn_fm = u1[:, 0:32768].rearrange("p (k t) -> p k t", t=1024)
    yg_fm = u1[:, 0:32768].rearrange("p (ec r) -> p ec r", r=1024)
    wslot = [wslot_t[:, 0:4096].rearrange("p (k d) -> p k d", d=128),
             wslot_t[:, 4096:8192].rearrange("p (k d) -> p k d", d=128)]

    pbk = [st.enter_context(nc.psum_tensor(f"pb{i}", [128, 512], F32)) for i in range(6)]
    tbk = [st.enter_context(nc.psum_tensor(f"tb{i}", [128, 1024], BF16)) for i in range(2)]
    B = [Bank(t) for t in pbk]
    TB = [Bank(t) for t in tbk]

    RQ = 10
    sem_list = ["pe", "dve", "act", "pool", "sp", "cc"]
    sem_list += [f"dsp{i}" for i in range(RQ)] + [f"dpool{i}" for i in range(RQ)]
    sems = {n: st.enter_context(nc.semaphore(n)) for n in sem_list}
    qrot = {"sp": 0, "pool": 0}

    S = Sched()

    def dma(queue, out_ap, in_ap, waits=()):
        base = {"sp": "dsp", "pool": "dpool"}[queue]
        semn = f"{base}{qrot[queue] % RQ}"
        qrot[queue] += 1
        return S.emit(queue, lambda e, o=out_ap, i=in_ap: e.dma_start(out=o, in_=i),
                      waits=waits, inc=semn, inc_by=16)

    def fence(queue, via="sp"):
        base = {"sp": "dsp", "pool": "dpool"}[queue]
        waits = [(f"{base}{i}", S.cnt[f"{base}{i}"]) for i in range(RQ)
                 if S.cnt.get(f"{base}{i}", 0) > 0]
        return S.emit(via, lambda e: e.nop(), waits=waits, inc=via)

    ctoks = []
    for dst, src in [(cosfm, cosfm_in), (sinfm, sinfm_in), (cum, cum_in),
                     (mask4, mask4_in), (idb, idb_in), (idf, idf_in), (ptb, ptb_in),
                     (onesb, onesb_in), (onesf, onesf_in), (crow, crow_in),
                     (urow, urow_in), (unot, unot_in), (mrow, mrow_in),
                     (prow, prow_in)]:
        ctoks.append(dma("sp", dst[:, :], src))
    tok_consts = fence("sp")

    S.emit("dve", lambda e: e.memset(tlcur[0:2, :], 1.0), inc="dve")
    S.emit("dve", lambda e: e.memset(rhcur[0:2, :], 1.0), inc="dve")
    tok_gl = S.emit("dve", lambda e: e.memset(glsum[:, :], 0.0), inc="dve")
    tok_st0 = S.emit("dve", lambda e: e.memset(state[:, :], 0.0), inc="dve")

    wts = B[2].pre_write() + [tok_consts]
    tok = S.emit("pe", lambda e: e.matmul(pbk[2][:, 0:193], onesf[0:1, :],
                                          prow[0:1, :], start=True, stop=True),
                 waits=wts, inc="pe")
    B[2].wr = tok
    tok_preps = S.emit("dve", lambda e: e.tensor_copy(preps[:, :], pbk[2][:, 0:193]),
                       waits=B[2].pre_read(), inc="dve")
    B[2].readers.append(tok_preps)
    tk = S.emit("act", lambda e: e.activation(arep[:, :], preps[:, 128:160], AF.Exp),
                waits=[tok_preps], inc="act")
    tok_arep = S.emit("dve", lambda e: e.tensor_scalar_mul(arep[:, :], arep[:, :], -1.0),
                      waits=[tk], inc="dve")
    tok_nbthr = S.emit("dve", lambda e: e.tensor_scalar_mul(stats[:, 15:16],
                                                            preps[:, 192:193], -1.0),
                       waits=[tok_preps], inc="dve")

    x_st = arena_f[:, 0:4096]
    xn_tm = arena[:, 8192:12288]
    prev_x_readers = []
    prev_xn_tok = None
    for n in range(NCH):
        tok_x = dma("sp", x_st[:, :], x_in[n], waits=prev_x_readers)
        tok_x = fence("sp")
        prev_x_readers = []
        t_sum = S.emit("dve", lambda e: e.reduce_sum(stats[:, 0:1], x_st[:, :], axis=AX.X),
                       waits=[tok_x], inc="dve")
        t_ssq = S.emit("act", lambda e: e.activation(xn_tm[:, :], x_st[:, :], AF.Square,
                                                     accum_out=stats[:, 1:2]),
                       waits=[tok_x, prev_xn_tok], inc="act")
        t_m = S.emit("dve", lambda e: e.tensor_scalar_mul(stats[:, 2:3], stats[:, 0:1],
                                                          1.0 / DM),
                     waits=[t_sum], inc="dve")
        t_msq = S.emit("dve", lambda e: e.tensor_scalar(stats[:, 3:4], stats[:, 2:3],
                                                        stats[:, 2:3], None, OP.mult),
                       waits=[t_m], inc="dve")
        t_va = S.emit("dve", lambda e: e.tensor_scalar(stats[:, 4:5], stats[:, 1:2],
                                                       1.0 / DM, stats[:, 3:4],
                                                       OP.mult, OP.subtract),
                      waits=[t_ssq, t_msq], inc="dve")
        t_ln = S.emit("act", lambda e: e.activation(stats[:, 5:6], stats[:, 4:5], AF.Ln,
                                                    bias=crow[:, 0:1]),
                      waits=[t_va, tok_consts], inc="act")
        t_rstd = S.emit("act", lambda e: e.activation(stats[:, 6:7], stats[:, 5:6], AF.Exp,
                                                      scale=-0.5),
                        waits=[t_ln], inc="act")
        t_xn = S.emit("dve", lambda e: e.tensor_scalar(xn_tm[:, :], x_st[:, :],
                                                       stats[:, 2:3], stats[:, 6:7],
                                                       OP.subtract, OP.mult),
                      waits=[t_rstd, prev_xn_tok], inc="dve")
        prev_x_readers = [t_xn]
        ev_tok = None
        for grp in range(4):
            bank = TB[grp % 2]
            wts = bank.pre_write() + [t_xn, tok_consts]
            last = None
            for j8 in range(8):
                j = grp * 8 + j8

                def mk(j=j, j8=j8, bank=bank):
                    def f(e):
                        return e.transpose(bank.t[:, j8 * 128:(j8 + 1) * 128],
                                           xn_tm[:, j * 128:(j + 1) * 128], idb[:, :])
                    return f
                last = S.emit("pe", mk(), waits=(wts if j8 == 0 else []),
                              inc="pe" if j8 == 7 else None)
            bank.wr = last

            def mkev(grp=grp, n=n, bank=bank):
                def f(e):
                    dst = xn_fm[:, grp * 8:(grp + 1) * 8, n * 128:(n + 1) * 128]
                    src = bank.t[:, 0:1024].rearrange("p (j t) -> p j t", t=128)
                    return e.tensor_copy(dst, src)
                return f
            ev_tok = S.emit("dve", mkev(), waits=bank.pre_read(), inc="dve")
            bank.readers.append(ev_tok)
        prev_xn_tok = ev_tok
    tok_xn_done = S.tok("dve")

    if "P0" in tap_phases:
        tp = tapdecl("xn_fm", [128, 32768], BF16)
        dma("sp", tp, u1[:, 0:32768], waits=[S.tok("dve")])

    smallw = arena[:, 0:5408].rearrange("p (k c) -> p k c", c=169)
    zt = arena_f[:, 2752:2784]
    zt2 = arena_f[:, 2784:2816]
    zt3 = arena_f[:, 2816:2848]
    hw_tmp = arena[:, 5696:5824]
    gex = arena_f[:, 2912:2920]
    kbfA = arena[:, 6144:6656]
    t1f = arena_f[:, 3328:3840]
    t2f = arena_f[:, 3840:4352]
    kfm_t = arena[:, 8704:9216]
    xc_t = arena[:, 9216:9728]
    kd_t = arena[:, 9728:10240]

    tok_smallw = dma("pool", smallw, smallw_in, waits=[tok_xn_done])
    tok_smallw = fence("pool", via="pool")

    prev_t_wi = None
    prev_t_tr = None
    prev_t_glr = None
    for n in range(NCH):
        bank = B[n % 2]
        wts = bank.pre_write() + [tok_smallw, tok_xn_done]
        last = None
        for k in range(KC):
            def mk(k=k, n=n, bank=bank):
                def f(e):
                    return e.matmul(bank.t[:, 0:169],
                                    xn_fm[:, k, n * 128:(n + 1) * 128],
                                    smallw[:, k, :],
                                    start=(k == 0), stop=(k == KC - 1))
                return f
            last = S.emit("pe", mk(), waits=(wts if k == 0 else []),
                          inc="pe" if k == KC - 1 else None)
        bank.wr = last
        rd = bank.pre_read()
        t_z = S.emit("dve", lambda e, b=bank: e.tensor_add(zt[:, :], b.t[:, 137:169],
                                                           preps[:, 160:192]),
                     waits=rd + [tok_preps, prev_t_wi], inc="dve")
        bank.readers.append(t_z)
        t_n = S.emit("dve", lambda e: e.tensor_scalar_mul(zt2[:, :], zt[:, :], -1.0),
                     waits=[t_z], inc="dve")
        t_a = S.emit("dve", lambda e: e.tensor_max(zt2[:, :], zt2[:, :], zt[:, :]),
                     waits=[t_n], inc="dve")
        t_e = S.emit("act", lambda e: e.activation(zt2[:, :], zt2[:, :], AF.Exp, scale=-1.0),
                     waits=[t_a], inc="act")
        t_l = S.emit("act", lambda e: e.activation(zt2[:, :], zt2[:, :], AF.Ln,
                                                   bias=crow[:, 1:2]),
                     waits=[t_e], inc="act")
        t_r = S.emit("dve", lambda e: e.tensor_scalar_max(zt3[:, :], zt[:, :], 0.0),
                     waits=[t_l], inc="dve")
        t_dt = S.emit("dve", lambda e, n=n: e.tensor_add(dt_all[:, n * 32:(n + 1) * 32],
                                                         zt3[:, :], zt2[:, :]),
                      waits=[t_r], inc="dve")
        t_g = S.emit("dve", lambda e, n=n: e.tensor_mul(g_all[:, n * 32:(n + 1) * 32],
                                                        dt_all[:, n * 32:(n + 1) * 32],
                                                        arep[:, :]),
                     waits=[t_dt, tok_arep], inc="dve")
        wts2 = B[2].pre_write() + [t_g, tok_consts]
        t_cum = S.emit("pe", lambda e, n=n: e.matmul(pbk[2][:, 0:32], cum[:, :],
                                                     g_all[:, n * 32:(n + 1) * 32],
                                                     start=True, stop=True),
                       waits=wts2, inc="pe")
        B[2].wr = t_cum
        t_G = S.emit("dve", lambda e, n=n: e.tensor_copy(g_all[:, n * 32:(n + 1) * 32],
                                                         pbk[2][:, 0:32]),
                     waits=B[2].pre_read(), inc="dve")
        B[2].readers.append(t_G)
        t_eg = S.emit("act", lambda e, n=n: e.activation(expg_all[:, n * 32:(n + 1) * 32],
                                                         g_all[:, n * 32:(n + 1) * 32],
                                                         AF.Exp),
                      waits=[t_G], inc="act")
        t_gr = dma("pool", grow_t[0:1, :], g_all[127:128, n * 32:(n + 1) * 32],
                   waits=[t_G, prev_t_glr])
        t_gr = fence("pool", via="pool")
        wts3 = B[2].pre_write() + [t_gr]
        t_glr = S.emit("pe", lambda e: e.matmul(pbk[2][:, 32:64], onesf[0:1, :],
                                                grow_t[0:1, :], start=True, stop=True),
                       waits=wts3, inc="pe")
        B[2].wr = t_glr
        prev_t_glr = t_glr
        rd2 = B[2].pre_read()
        t_sub = S.emit("dve", lambda e, n=n: e.tensor_sub(zt[:, :], pbk[2][:, 32:64],
                                                          g_all[:, n * 32:(n + 1) * 32]),
                       waits=rd2 + [t_eg], inc="dve")
        B[2].readers.append(t_sub)
        S.emit("act", lambda e, n=n: e.activation(kdsc_all[:, n * 32:(n + 1) * 32],
                                                  zt[:, :], AF.Exp),
               waits=[t_sub], inc="act")
        t_dr = S.emit("act", lambda e, n=n: e.activation(drep_all[:, n * 32:(n + 1) * 32],
                                                         pbk[2][:, 32:64], AF.Exp),
                      waits=rd2, inc="act")
        B[2].readers.append(t_dr)
        t_gl = S.emit("dve", lambda e: e.tensor_add(glsum[:, :], glsum[:, :],
                                                    pbk[2][:, 32:64]),
                      waits=rd2 + [tok_gl], inc="dve")
        B[2].readers.append(t_gl)
        tok_gl = t_gl
        wts4 = B[2].pre_write() + [t_G, tok_consts]
        t_gt = S.emit("pe", lambda e, n=n: e.transpose(pbk[2][0:32, 0:128],
                                                       g_all[:, n * 32:(n + 1) * 32],
                                                       idf[:, :]),
                      waits=wts4, inc="pe")
        B[2].wr = t_gt
        rd3 = B[2].pre_read()
        t_gtc = S.emit("dve", lambda e, n=n: e.tensor_copy(gT[:, n * 128:(n + 1) * 128],
                                                           pbk[2][0:32, 0:128]),
                       waits=rd3, inc="dve")
        B[2].readers.append(t_gtc)
        S.emit("dve", lambda e, n=n: e.tensor_scalar_mul(ngT[:, n * 128:(n + 1) * 128],
                                                         gT[:, n * 128:(n + 1) * 128],
                                                         -1.0),
               waits=[t_gtc], inc="dve")
        t_nm = S.emit("dve", lambda e, b=bank: e.reduce_max(stats[:, 8:9],
                                                            b.t[:, 128:136],
                                                            axis=AX.X, negate=True),
                      waits=rd + [prev_t_wi], inc="dve")
        bank.readers.append(t_nm)
        t_ge = S.emit("act", lambda e, b=bank: e.activation(gex[:, :], b.t[:, 128:136],
                                                            AF.Exp, bias=stats[:, 8:9],
                                                            accum_out=stats[:, 9:10]),
                      waits=[t_nm], inc="act")
        bank.readers.append(t_ge)
        t_s1 = S.emit("act", lambda e, b=bank: e.activation(stats[:, 10:11],
                                                            b.t[:, 136:137], AF.Exp,
                                                            scale=-1.0,
                                                            bias=stats[:, 15:16]),
                      waits=rd + [tok_nbthr], inc="act")
        bank.readers.append(t_s1)
        t_s2 = S.emit("act", lambda e: e.activation(stats[:, 10:11], stats[:, 10:11],
                                                    AF.Ln, bias=crow[:, 1:2]),
                      waits=[t_s1], inc="act")
        t_s3 = S.emit("act", lambda e: e.activation(stats[:, 10:11], stats[:, 10:11],
                                                    AF.Exp, scale=-1.0),
                      waits=[t_s2], inc="act")
        t_nt = S.emit("dve", lambda e: e.tensor_scalar_mul(stats[:, 11:12],
                                                           stats[:, 10:11], -0.125),
                      waits=[t_s3], inc="dve")
        t_gi = S.emit("act", lambda e: e.activation(stats[:, 12:13], stats[:, 9:10], AF.Ln),
                      waits=[t_ge], inc="act")
        t_gi = S.emit("act", lambda e: e.activation(stats[:, 12:13], stats[:, 12:13],
                                                    AF.Exp, scale=-1.0),
                      waits=[t_gi], inc="act")
        t_ad = S.emit("dve", lambda e: e.tensor_scalar(gex[:, :], gex[:, :],
                                                       stats[:, 12:13], stats[:, 11:12],
                                                       OP.mult, OP.add),
                      waits=[t_gi, t_nt], inc="dve")
        t_wr = S.emit("dve", lambda e: e.tensor_scalar_max(gex[:, :], gex[:, :], 0.0),
                      waits=[t_ad], inc="dve")
        t_ws = S.emit("dve", lambda e: e.reduce_sum(stats[:, 13:14], gex[:, :], axis=AX.X),
                      waits=[t_wr], inc="dve")
        t_iz = S.emit("dve", lambda e: e.tensor_scalar(stats[:, 14:15], stats[:, 13:14],
                                                       0.0, None, OP.is_equal),
                      waits=[t_ws], inc="dve")
        t_w2 = S.emit("dve", lambda e: e.tensor_add(stats[:, 13:14], stats[:, 13:14],
                                                    stats[:, 14:15]),
                      waits=[t_iz], inc="dve")
        t_wl = S.emit("act", lambda e: e.activation(stats[:, 13:14], stats[:, 13:14], AF.Ln),
                      waits=[t_w2], inc="act")
        t_wi = S.emit("act", lambda e: e.activation(stats[:, 13:14], stats[:, 13:14],
                                                    AF.Exp, scale=-1.0, bias=crow[:, 2:3]),
                      waits=[t_wl], inc="act")
        prev_t_wi = t_wi
        t_wf = S.emit("dve", lambda e, n=n: e.tensor_scalar(w_all[:, n * 8:(n + 1) * 8],
                                                            gex[:, :], stats[:, 13:14],
                                                            None, OP.mult),
                      waits=[t_wi], inc="dve")
        last_hw = None
        for ei in range(8):
            def mkhw(ei=ei, n=n, bank=bank):
                def f(e):
                    return e.tensor_scalar(hw_tmp[:, ei * 16:(ei + 1) * 16],
                                           bank.t[:, ei * 16:(ei + 1) * 16],
                                           w_all[:, n * 8 + ei:n * 8 + ei + 1],
                                           None, OP.mult)
                return f
            last_hw = S.emit("dve", mkhw(), waits=([t_wf, prev_t_tr] if ei == 0 else []),
                             inc="dve" if ei == 7 else None)
        bank.readers.append(last_hw)
        tbank = TB[n % 2]
        wts5 = tbank.pre_write() + [last_hw, tok_consts]
        t_tr = S.emit("pe", lambda e, tb=tbank: e.transpose(tb.t[:, 0:128], hw_tmp[:, :],
                                                            idb[:, :]),
                      waits=wts5, inc="pe")
        tbank.wr = t_tr
        prev_t_tr = t_tr
        t_he = S.emit("dve", lambda e, n=n, tb=tbank: e.tensor_copy(
            hw_fm[:, n * 128:(n + 1) * 128], tb.t[:, 0:128]),
            waits=tbank.pre_read(), inc="dve")
        tbank.readers.append(t_he)
    tok_small_done = S.tok("dve")

    slot_readers = [[], []]
    sl_i = 0
    pj_i = 0
    for which, w_in_, dst3 in [("k", wk_in, k_tm3), ("v", wv_in, v_tm3)]:
        for kv in range(KVH):
            sl = sl_i % 2
            sl_i += 1
            tok_w = dma("pool", wslot[sl], w_in_[kv], waits=slot_readers[sl])
            tok_w = fence("pool", via="pool")
            slot_readers[sl] = []
            for half in range(2):
                bank = B[pj_i % 2]
                pj_i += 1
                wts = bank.pre_write() + [tok_w, tok_xn_done]
                last = None
                for k in range(KC):
                    def mk(k=k, half=half, sl=sl, bank=bank):
                        def f(e):
                            return e.matmul(bank.t[:, 0:512], wslot[sl][:, k, :],
                                            xn_fm[:, k, half * 512:(half + 1) * 512],
                                            start=(k == 0), stop=(k == KC - 1))
                        return f
                    last = S.emit("pe", mk(), waits=(wts if k == 0 else []),
                                  inc="pe" if k == KC - 1 else None)
                bank.wr = last
                slot_readers[sl].append(last)
                rd = bank.pre_read()
                if which == "k":
                    t_c = S.emit("dve", lambda e, b=bank: e.tensor_copy(kbfA[:, :],
                                                                        b.t[:, 0:512]),
                                 waits=rd, inc="dve")
                    bank.readers.append(t_c)
                    wts2 = B[2].pre_write() + [t_c]
                    t_rot = S.emit("pe", lambda e: e.matmul(pbk[2][:, 0:512], ptb[:, :],
                                                            kbfA[:, :], start=True,
                                                            stop=True),
                                   waits=wts2, inc="pe")
                    B[2].wr = t_rot
                    t_t1 = S.emit("dve", lambda e, b=bank, half=half: e.tensor_mul(
                        t1f[:, :], b.t[:, 0:512],
                        cosfm[:, half * 512:(half + 1) * 512]),
                        waits=rd, inc="dve")
                    bank.readers.append(t_t1)
                    rd2 = B[2].pre_read()
                    t_t2 = S.emit("dve", lambda e, half=half: e.tensor_mul(
                        t2f[:, :], pbk[2][:, 0:512],
                        sinfm[:, half * 512:(half + 1) * 512]),
                        waits=rd2, inc="dve")
                    B[2].readers.append(t_t2)
                    t_src = S.emit("dve", lambda e: e.tensor_add(kfm_t[:, :], t1f[:, :],
                                                                 t2f[:, :]),
                                   waits=[t_t1, t_t2], inc="dve")
                else:
                    t_src = S.emit("dve", lambda e, b=bank: e.tensor_copy(kfm_t[:, :],
                                                                          b.t[:, 0:512]),
                                   waits=rd, inc="dve")
                    bank.readers.append(t_src)
                tb2 = TB[pj_i % 2]
                wts3 = tb2.pre_write() + [t_src, tok_consts]
                last_t = None
                for j in range(4):
                    def mktr(j=j, tb2=tb2):
                        def f(e):
                            return e.transpose(tb2.t[:, j * 128:(j + 1) * 128],
                                               kfm_t[:, j * 128:(j + 1) * 128],
                                               idb[:, :])
                        return f
                    last_t = S.emit("pe", mktr(), waits=(wts3 if j == 0 else []),
                                    inc="pe" if j == 3 else None)
                tb2.wr = last_t

                def mkev(half=half, kv=kv, dst3=dst3, tb2=tb2):
                    def f(e):
                        dst = dst3[:, 4 * half:4 * half + 4, kv * 128:(kv + 1) * 128]
                        src = tb2.t[:, 0:512].rearrange("p (j t) -> p j t", t=128)
                        return e.tensor_copy(dst, src)
                    return f
                t_ev = S.emit("dve", mkev(), waits=tb2.pre_read(), inc="dve")
                tb2.readers.append(t_ev)
    tok_kv_done = S.tok("dve")

    prev_cs = None
    for n in range(NCH):
        for g in range(NG):
            last_xc = None
            for hp in range(4):
                h = 4 * g + hp

                def mkxc(hp=hp, h=h, n=n, g=g):
                    def f(e):
                        return e.tensor_scalar(xc_t[:, hp * 128:(hp + 1) * 128],
                                               v_tm3[:, n, g * 128:(g + 1) * 128],
                                               dt_all[:, n * 32 + h:n * 32 + h + 1],
                                               None, OP.mult)
                    return f
                last_xc = S.emit("dve", mkxc(),
                                 waits=([tok_kv_done, prev_cs] if hp == 0 else []),
                                 inc="dve" if hp == 3 else None)
            last_kd = None
            for hp in range(4):
                h = 4 * g + hp

                def mkkd(hp=hp, h=h, n=n, g=g):
                    def f(e):
                        return e.tensor_scalar(kd_t[:, hp * 128:(hp + 1) * 128],
                                               k_tm3[:, n, g * 128:(g + 1) * 128],
                                               kdsc_all[:, n * 32 + h:n * 32 + h + 1],
                                               None, OP.mult)
                    return f
                last_kd = S.emit("dve", mkkd(), inc="dve" if hp == 3 else None)
            bank = B[5]
            wts = bank.pre_write() + [last_xc, last_kd]
            last_cs = None
            for hp in range(4):
                def mkcs(hp=hp, bank=bank):
                    def f(e):
                        return e.matmul(bank.t[:, hp * 128:(hp + 1) * 128],
                                        kd_t[:, hp * 128:(hp + 1) * 128],
                                        xc_t[:, hp * 128:(hp + 1) * 128],
                                        start=True, stop=True)
                    return f
                last_cs = S.emit("pe", mkcs(), waits=(wts if hp == 0 else []),
                                 inc="pe" if hp == 3 else None)
            bank.wr = last_cs
            prev_cs = last_cs
            rd = bank.pre_read()
            last_up = None
            for hp in range(4):
                h = 4 * g + hp

                def mkup(hp=hp, h=h, n=n, bank=bank):
                    def f(e):
                        blk = state[:, h * 128:(h + 1) * 128]
                        return e.scalar_tensor_tensor(
                            blk, blk, drep_all[:, n * 32 + h:n * 32 + h + 1],
                            bank.t[:, hp * 128:(hp + 1) * 128], OP.mult, OP.add)
                    return f
                last_up = S.emit("dve", mkup(),
                                 waits=(rd + [tok_st0] if hp == 0 else []),
                                 inc="dve" if hp == 3 else None)
            bank.readers.append(last_up)
    tok_state_done = S.tok("dve")
    tok_dall = S.emit("act", lambda e: e.activation(dall[:, :], glsum[:, :], AF.Exp),
                      waits=[tok_gl], inc="act")

    if "P1" in tap_phases:
        for nm, buf, wd, dt_ in [("k_tm", k_tm, 8192, BF16), ("v_tm", v_tm, 8192, BF16),
                                 ("dt_all", dt_all, 256, F32), ("g_all", g_all, 256, F32),
                                 ("kdsc_all", kdsc_all, 256, F32),
                                 ("drep_all", drep_all, 256, F32),
                                 ("s_c", state, 4096, F32), ("dall", dall, 32, F32),
                                 ("hw_fm", hw_fm, 1024, BF16),
                                 ("w_all", w_all, 64, F32),
                                 ("expg_all", expg_all, 256, F32)]:
            tp = tapdecl(nm, [128, wd], dt_)
            dma("sp", tp, buf[:, 0:wd],
                waits=[tok_state_done, tok_dall, S.tok("act"), S.tok("dve")])

    t_cc1 = dma("pool", cc_in.ap()[:, 0:4096], state[:, :], waits=[tok_state_done])
    t_cc2 = dma("pool", cc_in.ap()[:, 4096:4128], dall[:, :], waits=[tok_dall])
    tok_ccin = fence("pool", via="pool")
    tok_cc = S.emit("pool", lambda e: e.collective_compute(
        "AllGather", OP.bypass, replica_groups=[list(range(NCORES))],
        ins=[cc_in.ap().opt()], outs=[cc_out.ap().opt()]),
        waits=[tok_ccin], inc="cc")

    bhat_sb = arena[:, 0:4096]
    bhat3 = bhat_sb.rearrange("p (fc c) -> p fc c", c=128)
    bbase_sb = arena[0:1, 4096:8192]
    xfm_st = arena_f[:, 4096:6144]
    boutw = arena_f[:, 6144:7168]
    tok_bh = dma("pool", bhat_sb[:, :], bhat_in,
                 waits=slot_readers[0] + slot_readers[1] + [S.tok("pe"), S.tok("dve")])
    dma("pool", bbase_sb[0:1, :], bbase_in)
    tok_bh = fence("pool", via="pool")
    wsl_i = 0
    xf_readers = [[S.tok("dve"), S.tok("pe")], [S.tok("dve"), S.tok("pe")]]
    bout_tok = [None, None]
    for fc in range(KC):
        sl = wsl_i % 2
        wsl_i += 1
        tok_w = dma("pool", wslot[sl], wb_in[fc], waits=slot_readers[sl])
        tok_w = fence("pool", via="pool")
        slot_readers[sl] = []
        xsl = fc % 2
        tok_x2 = dma("sp", xfm_st[:, xsl * 1024:(xsl + 1) * 1024], xfm_in[fc],
                     waits=xf_readers[xsl])
        tok_x2 = fence("sp")
        bout_tok[0] = bout_tok[1] = tok_x2
        xf_readers[xsl] = []
        pp = fc % 2
        bkA, bkB = B[2 * pp], B[2 * pp + 1]
        wtsA = bkA.pre_write() + [tok_w, tok_bh, tok_small_done]
        wtsB = bkB.pre_write()
        for k in range(KC):
            def mkA(k=k, sl=sl, bkA=bkA):
                def f(e):
                    return e.matmul(bkA.t[:, 0:512], wslot[sl][:, k, :],
                                    xn_fm[:, k, 0:512], start=(k == 0), stop=False)
                return f
            def mkB(k=k, sl=sl, bkB=bkB):
                def f(e):
                    return e.matmul(bkB.t[:, 0:512], wslot[sl][:, k, :],
                                    xn_fm[:, k, 512:1024], start=(k == 0), stop=False)
                return f
            S.emit("pe", mkA(), waits=(wtsA if k == 0 else []))
            S.emit("pe", mkB(), waits=(wtsB if k == 0 else []))
        for bk, tsl in ((bkA, slice(0, 512)), (bkB, slice(512, 1024))):
            def mkmoe(fc=fc, bk=bk, tsl=tsl):
                def f(e):
                    return e.matmul(bk.t[:, 0:512], bhat3[:, fc, :], hw_fm[:, tsl],
                                    start=False, stop=False)
                return f
            S.emit("pe", mkmoe())
        lasts = []
        for bk in (bkA, bkB):
            def mkbias(fc=fc, bk=bk):
                def f(e):
                    return e.matmul(bk.t[:, 0:512],
                                    bbase_sb[0:1, fc * 128:(fc + 1) * 128],
                                    onesb[0:1, 0:512], start=False, stop=True)
                return f
            tokl = S.emit("pe", mkbias(), inc="pe")
            bk.wr = tokl
            lasts.append(tokl)
        slot_readers[sl] += lasts
        for half, bk in ((0, bkA), (1, bkB)):
            bsl = half
            def mkev2(half=half, bk=bk, xsl=xsl, bsl=bsl):
                def f(e):
                    return e.tensor_add(boutw[:, bsl * 512:(bsl + 1) * 512],
                                        bk.t[:, 0:512],
                                        xfm_st[:, xsl * 1024 + half * 512:
                                               xsl * 1024 + (half + 1) * 512])
                return f
            t_ev = S.emit("dve", mkev2(), waits=bk.pre_read() + [tok_x2, bout_tok[bsl]],
                          inc="dve")
            bk.readers.append(t_ev)
            xf_readers[xsl].append(t_ev)
            def mkst(fc=fc, half=half, bsl=bsl):
                def f(e):
                    return e.dma_start(out=bscr.ap()[fc][:, half * 512:(half + 1) * 512],
                                       in_=boutw[:, bsl * 512:(bsl + 1) * 512])
                return f
            dma("sp", bscr.ap()[fc][:, half * 512:(half + 1) * 512],
                boutw[:, bsl * 512:(bsl + 1) * 512], waits=[t_ev])
    if "B1" in tap_phases:
        tp = tapdecl("bscr", [KC, 128, T], F32)
        dma("sp", tp, bscr.ap(), waits=[fence("sp")])

    sslot = arena_f[:, 7168:9216]
    tok_dr = dma("pool", dallrow[0:1, :].rearrange("o (r c) -> o r c", c=32),
                 cc_out.ap().rearrange("(r p) c -> r p c", p=128)[:, 0:1, 4096:4128],
                 waits=[tok_cc])
    tok_dr = fence("pool", via="pool")
    t_d1 = S.emit("dve", lambda e: e.tensor_mul(dprow[0:1, :], dallrow[0:1, :],
                                                urow[0:1, :]),
                  waits=[tok_dr, tok_consts], inc="dve")
    t_d2 = S.emit("dve", lambda e: e.tensor_add(dprow[0:1, :], dprow[0:1, :],
                                                unot[0:1, :]),
                  waits=[t_d1], inc="dve")
    t_sf = S.emit("dve", lambda e: e.memset(sfx[0:1, 224:256], 1.0), waits=[], inc="dve")
    for j in range(6, -1, -1):
        def mksf(j=j):
            def f(e):
                return e.tensor_mul(sfx[0:1, j * 32:(j + 1) * 32],
                                    sfx[0:1, (j + 1) * 32:(j + 2) * 32],
                                    dprow[0:1, (j + 1) * 32:(j + 2) * 32])
            return f
        t_sf = S.emit("dve", mksf(), waits=[t_d2, t_sf], inc="dve")
    t_pr = S.emit("dve", lambda e: e.tensor_mul(prow2[0:1, :], sfx[0:1, :], mrow[0:1, :]),
                  waits=[t_sf], inc="dve")
    wts = B[2].pre_write() + [t_pr]
    t_mm = S.emit("pe", lambda e: e.matmul(pbk[2][:, 0:256], onesf[0:1, :],
                                           prow2[0:1, :], start=True, stop=True),
                  waits=wts, inc="pe")
    B[2].wr = t_mm
    t_prep = S.emit("dve", lambda e: e.tensor_copy(prep_sc[:, :], pbk[2][:, 0:256]),
                    waits=B[2].pre_read(), inc="dve")
    B[2].readers.append(t_prep)
    ss_readers = []
    first = True
    for j in range(NCORES):
        for hh in range(2):
            tok_s = dma("pool", sslot[:, :],
                        cc_out.ap()[j * 128:(j + 1) * 128, hh * 2048:(hh + 1) * 2048],
                        waits=ss_readers + [tok_cc, (tok_ccin if first else None)])
            tok_s = fence("pool", via="pool")
            ss_readers = []
            last_in = None
            for hq in range(16):
                h = hh * 16 + hq

                def mkin(j=j, h=h, hq=hq, first=first):
                    def f(e):
                        blk = state[:, h * 128:(h + 1) * 128]
                        src = sslot[:, hq * 128:(hq + 1) * 128]
                        sc = prep_sc[:, j * 32 + h:j * 32 + h + 1]
                        if j == 0:
                            return e.tensor_scalar(blk, src, sc, None, OP.mult)
                        return e.scalar_tensor_tensor(blk, src, sc, blk, OP.mult, OP.add)
                    return f
                last_in = S.emit("dve", mkin(),
                                 waits=([tok_s, t_prep] if hq == 0 else []),
                                 inc="dve" if hq == 15 else None)
            ss_readers.append(last_in)
        first = False
    tok_incoming = S.tok("dve")

    if "CC" in tap_phases:
        tp = tapdecl("incoming", [128, 4096], F32)
        dma("sp", tp, state[:, :], waits=[tok_incoming])

    qt_bf = arena[:, 0:512]
    t1q = arena_f[:, 256:768]
    t2q = arena_f[:, 768:1280]
    kfm_a = arena[:, 2560:2688]
    xc_a = arena[:, 2688:3200]
    kd_a = arena[:, 3200:3712]
    sbf_a = arena[:, 3712:4224]
    dsb_a = arena_f[:, 2112:2624]
    tsb_a = arena_f[:, 2624:3136]
    ysb_a = arena_f[:, 3136:3648]
    yint_a = arena_f[:, 3648:4160]
    sq_a = arena_f[:, 4160:4672]
    stbf_a = arena[:, 9344:9856]
    qfm_g = arena[:, 9856:13952]
    qfm_g3 = qfm_g.rearrange("p (n c) -> p n c", c=512)
    ynst = arena[:, 13952:18048]
    ynst_tok = [None, None]

    tok_spq = fence("sp")
    a2_start = [tok_incoming, tok_spq, S.tok("act"), S.tok("pe")]
    prev_t_dx = None
    prev_t_y = None
    prev_yb = None
    prev_cs_a2 = None
    prev_t_ln2 = None
    for g in range(NG):
        qfm_toks = []
        first_qw = True
        for hp in range(4):
            h = 4 * g + hp
            sl = sl_i % 2
            sl_i += 1
            tok_w = dma("pool", wslot[sl], wq_in[h],
                        waits=slot_readers[sl] + ([S.tok("pe")] if g == 0 else []))
            tok_w = fence("pool", via="pool")
            slot_readers[sl] = []
            bkA, bkB = B[0], B[1]
            wtsA = bkA.pre_write() + [tok_w] + (a2_start if g == 0 else [])
            wtsB = bkB.pre_write()
            lastA = lastB = None
            for k in range(KC):
                def mkqA(k=k, sl=sl, bkA=bkA):
                    def f(e):
                        return e.matmul(bkA.t[:, 0:512], wslot[sl][:, k, :],
                                        xn_fm[:, k, 0:512], start=(k == 0),
                                        stop=(k == KC - 1))
                    return f
                def mkqB(k=k, sl=sl, bkB=bkB):
                    def f(e):
                        return e.matmul(bkB.t[:, 0:512], wslot[sl][:, k, :],
                                        xn_fm[:, k, 512:1024], start=(k == 0),
                                        stop=(k == KC - 1))
                    return f
                lastA = S.emit("pe", mkqA(), waits=(wtsA if k == 0 else []),
                               inc="pe" if k == KC - 1 else None)
                lastB = S.emit("pe", mkqB(), waits=(wtsB if k == 0 else []),
                               inc="pe" if k == KC - 1 else None)
            bkA.wr = lastA
            bkB.wr = lastB
            slot_readers[sl] += [lastA, lastB]
            for half, bank in ((0, bkA), (1, bkB)):
                rd = bank.pre_read()
                t_c = S.emit("dve", lambda e, b=bank: e.tensor_copy(qt_bf[:, :],
                                                                    b.t[:, 0:512]),
                             waits=rd, inc="dve")
                bank.readers.append(t_c)
                wts2 = B[2].pre_write() + [t_c]
                t_rot = S.emit("pe", lambda e: e.matmul(pbk[2][:, 0:512], ptb[:, :],
                                                        qt_bf[:, :], start=True,
                                                        stop=True),
                               waits=wts2, inc="pe")
                B[2].wr = t_rot
                t_t1 = S.emit("dve", lambda e, b=bank, half=half: e.tensor_mul(
                    t1q[:, :], b.t[:, 0:512],
                    cosfm[:, half * 512:(half + 1) * 512]),
                    waits=rd, inc="dve")
                bank.readers.append(t_t1)
                rd2 = B[2].pre_read()
                t_t2 = S.emit("dve", lambda e, half=half: e.tensor_mul(
                    t2q[:, :], pbk[2][:, 0:512],
                    sinfm[:, half * 512:(half + 1) * 512]),
                    waits=rd2, inc="dve")
                B[2].readers.append(t_t2)

                def mkqw(half=half, hp=hp):
                    def f(e):
                        dst = qfm_g.rearrange("p (n c t) -> p n c t", c=4, t=128)[
                            :, 4 * half:4 * half + 4, hp, :]
                        return e.tensor_add(dst, t1q[:, :].rearrange(
                            "p (n t) -> p n t", t=128), t2q[:, :].rearrange(
                            "p (n t) -> p n t", t=128))
                    return f
                t_qw = S.emit("dve", mkqw(),
                              waits=[t_t1, t_t2] + ([prev_yb] if first_qw else []),
                              inc="dve")
                first_qw = False
                qfm_toks.append(t_qw)
        if "A2DBG" in tap_phases and g == 0:
            tp = tapdecl("qfm_g0", [128, 4096], BF16)
            dma("sp", tp, qfm_g[:, :], waits=qfm_toks)
        for n in range(NCH):
            tb2 = TB[n % 2]
            wts = tb2.pre_write() + [tok_kv_done]
            t_tr = S.emit("pe", lambda e, n=n, g=g, tb2=tb2: e.transpose(
                tb2.t[:, 0:128], k_tm3[:, n, g * 128:(g + 1) * 128], idb[:, :]),
                waits=wts, inc="pe")
            tb2.wr = t_tr
            t_kf = S.emit("dve", lambda e, tb2=tb2: e.tensor_copy(kfm_a[:, :],
                                                                  tb2.t[:, 0:128]),
                          waits=tb2.pre_read(), inc="dve")
            tb2.readers.append(t_kf)
            bank3 = B[3]
            wts = bank3.pre_write() + [t_kf] + qfm_toks
            t_sc = S.emit("pe", lambda e, n=n: e.matmul(pbk[3][:, 0:512], kfm_a[:, :],
                                                        qfm_g3[:, n, :], start=True,
                                                        stop=True),
                          waits=wts, inc="pe")
            bank3.wr = t_sc
            t_tl = dma("pool", tlcur[0:1, :].rearrange("o (c t) -> o c t", t=128),
                       ngT[4 * g:4 * g + 4, n * 128:(n + 1) * 128], waits=[S.tok("pe")])
            t_rh = dma("pool", rhcur[1:2, :].rearrange("o (c t) -> o c t", t=128),
                       gT[4 * g:4 * g + 4, n * 128:(n + 1) * 128], waits=[])
            t_rh = fence("pool", via="pool")
            bank2 = B[2]
            wts = bank2.pre_write() + [t_rh]
            last_T = None
            for hp in range(4):
                def mkT(hp=hp):
                    def f(e):
                        return e.matmul(pbk[2][:, hp * 128:(hp + 1) * 128],
                                        tlcur[0:2, hp * 128:(hp + 1) * 128],
                                        rhcur[0:2, hp * 128:(hp + 1) * 128],
                                        start=True, stop=True)
                    return f
                last_T = S.emit("pe", mkT(), waits=(wts if hp == 0 else []),
                                inc="pe" if hp == 3 else None)
            bank2.wr = last_T
            rd2 = bank2.pre_read()
            t_tm = S.emit("dve", lambda e: e.tensor_add(tsb_a[:, :], pbk[2][:, 0:512],
                                                        mask4[:, :]),
                          waits=rd2 + [prev_t_dx], inc="dve")
            bank2.readers.append(t_tm)
            t_dx = S.emit("act", lambda e: e.activation(dsb_a[:, :], tsb_a[:, :], AF.Exp),
                          waits=[t_tm], inc="act")
            prev_t_dx = t_dx
            rd3 = bank3.pre_read()
            t_sb = S.emit("dve", lambda e: e.tensor_mul(sbf_a[:, :], pbk[3][:, 0:512],
                                                        dsb_a[:, :]),
                          waits=rd3 + [t_dx], inc="dve")
            bank3.readers.append(t_sb)
            last_xc = None
            for hp in range(4):
                h = 4 * g + hp

                def mkxc(hp=hp, h=h, n=n, g=g):
                    def f(e):
                        return e.tensor_scalar(xc_a[:, hp * 128:(hp + 1) * 128],
                                               v_tm3[:, n, g * 128:(g + 1) * 128],
                                               dt_all[:, n * 32 + h:n * 32 + h + 1],
                                               None, OP.mult)
                    return f
                last_xc = S.emit("dve", mkxc(),
                                 waits=([prev_cs_a2] if hp == 0 else []),
                                 inc="dve" if hp == 3 else None)
            last_kd = None
            for hp in range(4):
                h = 4 * g + hp

                def mkkd(hp=hp, h=h, n=n, g=g):
                    def f(e):
                        return e.tensor_scalar(kd_a[:, hp * 128:(hp + 1) * 128],
                                               k_tm3[:, n, g * 128:(g + 1) * 128],
                                               kdsc_all[:, n * 32 + h:n * 32 + h + 1],
                                               None, OP.mult)
                    return f
                last_kd = S.emit("dve", mkkd(), inc="dve" if hp == 3 else None)
            t_st = S.emit("dve", lambda e, g=g: e.tensor_copy(
                stbf_a[:, :], state[:, g * 512:(g + 1) * 512]),
                waits=[tok_incoming, prev_yb], inc="dve")
            bank4 = B[4]
            wts4 = bank4.pre_write() + [t_sb, last_xc]
            last_ya = None
            for hp in range(4):
                def mkya(hp=hp):
                    def f(e):
                        return e.matmul(pbk[4][:, hp * 128:(hp + 1) * 128],
                                        sbf_a[:, hp * 128:(hp + 1) * 128],
                                        xc_a[:, hp * 128:(hp + 1) * 128],
                                        start=True, stop=True)
                    return f
                last_ya = S.emit("pe", mkya(), waits=(wts4 if hp == 0 else []),
                                 inc="pe" if hp == 3 else None)
            bank4.wr = last_ya
            bank5 = B[5]
            wts5 = bank5.pre_write() + [t_st] + qfm_toks
            last_yb = None
            for hp in range(4):
                def mkyb(hp=hp, n=n):
                    def f(e):
                        return e.matmul(pbk[5][:, hp * 128:(hp + 1) * 128],
                                        qfm_g3[:, n, hp * 128:(hp + 1) * 128],
                                        stbf_a[:, hp * 128:(hp + 1) * 128],
                                        start=True, stop=True)
                    return f
                last_yb = S.emit("pe", mkyb(), waits=(wts5 if hp == 0 else []),
                                 inc="pe" if hp == 3 else None)
            bank5.wr = last_yb
            prev_yb = last_yb
            rd5 = bank5.pre_read()
            last_yi = None
            for hp in range(4):
                h = 4 * g + hp

                def mkyi(hp=hp, h=h, n=n):
                    def f(e):
                        return e.activation(yint_a[:, hp * 128:(hp + 1) * 128],
                                            pbk[5][:, hp * 128:(hp + 1) * 128],
                                            AF.Identity,
                                            scale=expg_all[:, n * 32 + h:n * 32 + h + 1])
                    return f
                last_yi = S.emit("act", mkyi(),
                                 waits=(rd5 + [prev_t_y] if hp == 0 else []),
                                 inc="act" if hp == 3 else None)
            bank5.readers.append(last_yi)
            rd4 = bank4.pre_read()
            t_y = S.emit("dve", lambda e: e.tensor_add(ysb_a[:, :], pbk[4][:, 0:512],
                                                       yint_a[:, :]),
                         waits=rd4 + [last_yi], inc="dve")
            bank4.readers.append(t_y)
            prev_t_y = t_y
            wts5b = bank5.pre_write() + [last_kd, last_xc]
            last_cs = None
            for hp in range(4):
                def mkcs(hp=hp):
                    def f(e):
                        return e.matmul(pbk[5][:, hp * 128:(hp + 1) * 128],
                                        kd_a[:, hp * 128:(hp + 1) * 128],
                                        xc_a[:, hp * 128:(hp + 1) * 128],
                                        start=True, stop=True)
                    return f
                last_cs = S.emit("pe", mkcs(), waits=(wts5b if hp == 0 else []),
                                 inc="pe" if hp == 3 else None)
            bank5.wr = last_cs
            prev_cs_a2 = last_cs
            rd5b = bank5.pre_read()
            last_up = None
            for hp in range(4):
                h = 4 * g + hp

                def mkup(hp=hp, h=h, n=n):
                    def f(e):
                        blk = state[:, h * 128:(h + 1) * 128]
                        return e.scalar_tensor_tensor(
                            blk, blk, drep_all[:, n * 32 + h:n * 32 + h + 1],
                            pbk[5][:, hp * 128:(hp + 1) * 128], OP.mult, OP.add)
                    return f
                last_up = S.emit("dve", mkup(), waits=(rd5b + [t_st] if hp == 0 else []),
                                 inc="dve" if hp == 3 else None)
            bank5.readers.append(last_up)
            t_sq2 = S.emit("dve", lambda e: e.tensor_mul(sq_a[:, :], ysb_a[:, :],
                                                         ysb_a[:, :]),
                           waits=[t_y], inc="dve")
            t_ssq2 = S.emit("dve", lambda e: e.tensor_reduce(
                stats[:, 0:4], sq_a[:, :].rearrange("p (c t) -> p c t", t=128),
                AX.X, OP.add),
                waits=[t_sq2, prev_t_ln2], inc="dve")
            t_ln2 = S.emit("act", lambda e: e.activation(stats[:, 4:8], stats[:, 0:4],
                                                         AF.Ln, scale=1.0 / 128,
                                                         bias=crow[:, 0:1]),
                           waits=[t_ssq2], inc="act")
            prev_t_ln2 = t_ln2
            t_rs2 = S.emit("act", lambda e: e.activation(stats[:, 4:8], stats[:, 4:8],
                                                         AF.Exp, scale=-0.5),
                           waits=[t_ln2], inc="act")
            if "A2DBG" in tap_phases and g == 0 and n == 0:
                for nm_, buf_, wd_, dt2 in [("ysb00", ysb_a, 512, F32),
                                            ("yint00", yint_a, 512, F32),
                                            ("dsb00", dsb_a, 512, F32),
                                            ("tsb00", tsb_a, 512, F32),
                                            ("sbf00", sbf_a, 512, BF16),
                                            ("stbf00", stbf_a, 512, BF16),
                                            ("xc00", xc_a, 512, BF16),
                                            ("kd00", kd_a, 512, BF16),
                                            ("kfm00", kfm_a, 128, BF16)]:
                    tp_ = tapdecl(nm_, [128, wd_], dt2)
                    dma("sp", tp_, buf_[:, 0:wd_],
                        waits=[t_y, last_yi, S.tok("dve"), S.tok("act")])
            sl = (n // 4) % 2
            last_yn = None
            for hp in range(4):
                h = 4 * g + hp

                def mkyn(hp=hp, h=h, n=n, sl=sl):
                    def f(e):
                        dst = ynst[:, sl * 2048 + (n % 4) * 512 + hp * 128:
                                   sl * 2048 + (n % 4) * 512 + (hp + 1) * 128]
                        return e.scalar_tensor_tensor(dst,
                                                      ysb_a[:, hp * 128:(hp + 1) * 128],
                                                      stats[:, 4 + hp:5 + hp],
                                                      preps[:, 0:128],
                                                      OP.mult, OP.mult)
                    return f
                last_yn = S.emit("dve", mkyn(),
                                 waits=([t_rs2, ynst_tok[sl]] if hp == 0 else []),
                                 inc="dve" if hp == 3 else None)
            if n % 4 == 3:
                half = n // 4
                dma("sp", yscr.ap()[g, :, half * 4:half * 4 + 4, :],
                    ynst[:, sl * 2048:(sl + 1) * 2048].rearrange(
                        "p (c t) -> p c t", t=512),
                    waits=[last_yn])
                ynst_tok[sl] = fence("sp")
    tok_a2_done = [S.tok("dve"), S.tok("act"), S.tok("pe"), fence("sp")]

    if "A2" in tap_phases:
        tp = tapdecl("yscr", [NG, 128, NCH, 512], BF16)
        dma("sp", tp, yscr.ap(), waits=tok_a2_done)
        tp2 = tapdecl("state_a2", [128, 4096], F32)
        dma("sp", tp2, state[:, :], waits=tok_a2_done)

    tok_spq2 = fence("sp")
    wg_first = [S.tok("dve"), S.tok("pe"), S.tok("act"), tok_spq2]
    for hc in range(H):
        sl = wsl_i % 2
        wsl_i += 1
        tok_w = dma("pool", wslot[sl], wg_in[hc],
                    waits=slot_readers[sl] + (wg_first if hc < 2 else []))
        tok_w = fence("pool", via="pool")
        slot_readers[sl] = []
        pp = hc % 2
        bkA, bkB = B[2 * pp], B[2 * pp + 1]
        wtsA = bkA.pre_write() + [tok_w] + (wg_first if hc == 0 else [])
        wtsB = bkB.pre_write()
        lastA = lastB = None
        for k in range(KC):
            def mkA(k=k, sl=sl, bkA=bkA):
                def f(e):
                    return e.matmul(bkA.t[:, 0:512], wslot[sl][:, k, :],
                                    xn_fm[:, k, 0:512], start=(k == 0),
                                    stop=(k == KC - 1))
                return f
            def mkB(k=k, sl=sl, bkB=bkB):
                def f(e):
                    return e.matmul(bkB.t[:, 0:512], wslot[sl][:, k, :],
                                    xn_fm[:, k, 512:1024], start=(k == 0),
                                    stop=(k == KC - 1))
                return f
            lastA = S.emit("pe", mkA(), waits=(wtsA if k == 0 else []),
                           inc="pe" if k == KC - 1 else None)
            lastB = S.emit("pe", mkB(), waits=(wtsB if k == 0 else []),
                           inc="pe" if k == KC - 1 else None)
        bkA.wr = lastA
        bkB.wr = lastB
        slot_readers[sl] += [lastA, lastB]
        for half, bk in ((0, bkA), (1, bkB)):
            def mksil(hc=hc, half=half, bk=bk):
                def f(e):
                    return e.activation(gs3[:, hc, half * 512:(half + 1) * 512],
                                        bk.t[:, 0:512], AF.Silu)
                return f
            t_sil = S.emit("act", mksil(), waits=bk.pre_read(), inc="act")
            bk.readers.append(t_sil)
    tok_b2a = [S.tok("act"), S.tok("pe")]

    ynst2 = arena[:, 0:4096]
    yn2_tok = [None, None]
    yg_first = list(tok_b2a) + [S.tok("dve"), fence("sp")]
    first_b2b = True
    for g in range(NG):
        for half in range(2):
            sl = (g * 2 + half) % 2
            tok_yn = dma("sp", ynst2[:, sl * 2048:(sl + 1) * 2048].rearrange(
                "p (c t) -> p c t", t=512),
                yscr.ap()[g, :, half * 4:half * 4 + 4, :],
                waits=[yn2_tok[sl]] + (yg_first if first_b2b else []))
            tok_yn = fence("sp")
            first_b2b = False
            for nn in range(4):
                n = half * 4 + nn
                tb2 = TB[n % 2]
                wts = tb2.pre_write() + [tok_yn]
                last_t = None
                for hp in range(4):
                    def mktr(hp=hp, tb2=tb2, sl=sl, nn=nn):
                        def f(e):
                            return e.transpose(tb2.t[:, hp * 128:(hp + 1) * 128],
                                               ynst2[:, sl * 2048 + nn * 512 + hp * 128:
                                                     sl * 2048 + nn * 512 + (hp + 1) * 128],
                                               idb[:, :])
                        return f
                    last_t = S.emit("pe", mktr(), waits=(wts if hp == 0 else []),
                                    inc="pe" if hp == 3 else None)
                tb2.wr = last_t

                def mkyg(g=g, n=n, tb2=tb2):
                    def f(e):
                        dst = yg_fm[:, 4 * g:4 * g + 4, n * 128:(n + 1) * 128]
                        src_ = tb2.t[:, 0:512].rearrange("p (c t) -> p c t", t=128)
                        gsl = gs3[:, 4 * g:4 * g + 4, n * 128:(n + 1) * 128]
                        return e.tensor_mul(dst, gsl, src_)
                    return f
                t_yg = S.emit("dve", mkyg(), waits=tb2.pre_read(), inc="dve")
                tb2.readers.append(t_yg)
                yn2_tok[sl] = t_yg
    tok_b2b = [S.tok("dve"), S.tok("pe")]

    if "B2" in tap_phases:
        tp = tapdecl("yg_fm", [128, 32768], BF16)
        dma("sp", tp, u1[:, 0:32768], waits=tok_b2b)
        tp2 = tapdecl("gatesil", [128, 32768], BF16)
        dma("sp", tp2, gatesil[:, :], waits=tok_b2b)

    bpst = arena_f[:, 2048:4096]
    outst = arena_f[:, 4096:5120]
    bp_tok = [None, None]
    bp_readers = [[], []]
    out_tok = [None, None]
    c_first = list(tok_b2b) + [fence("sp")]
    for fc in range(KC):
        sl = wsl_i % 2
        wsl_i += 1
        tok_w = dma("pool", wslot[sl], wo_in[fc],
                    waits=slot_readers[sl] + (c_first if fc < 2 else []))
        tok_w = fence("pool", via="pool")
        slot_readers[sl] = []
        bsl = fc % 2
        tok_bp = dma("sp", bpst[:, bsl * 1024:(bsl + 1) * 1024], bscr.ap()[fc],
                     waits=bp_readers[bsl] + (c_first if fc == 0 else []))
        tok_bp = fence("sp")
        out_tok[0] = out_tok[1] = tok_bp
        bp_readers[bsl] = []
        pp = fc % 2
        bkA, bkB = B[2 * pp], B[2 * pp + 1]
        wtsA = bkA.pre_write() + [tok_w] + (c_first if fc == 0 else [])
        wtsB = bkB.pre_write()
        lastA = lastB = None
        for ec in range(KC):
            def mkA(ec=ec, sl=sl, bkA=bkA):
                def f(e):
                    return e.matmul(bkA.t[:, 0:512], wslot[sl][:, ec, :],
                                    yg_fm[:, ec, 0:512], start=(ec == 0),
                                    stop=(ec == KC - 1))
                return f
            def mkB(ec=ec, sl=sl, bkB=bkB):
                def f(e):
                    return e.matmul(bkB.t[:, 0:512], wslot[sl][:, ec, :],
                                    yg_fm[:, ec, 512:1024], start=(ec == 0),
                                    stop=(ec == KC - 1))
                return f
            lastA = S.emit("pe", mkA(), waits=(wtsA if ec == 0 else []),
                           inc="pe" if ec == KC - 1 else None)
            lastB = S.emit("pe", mkB(), waits=(wtsB if ec == 0 else []),
                           inc="pe" if ec == KC - 1 else None)
        bkA.wr = lastA
        bkB.wr = lastB
        slot_readers[sl] += [lastA, lastB]
        for half, bk in ((0, bkA), (1, bkB)):
            osl = half

            def mkev(half=half, bk=bk, bsl=bsl, osl=osl):
                def f(e):
                    return e.tensor_add(outst[:, osl * 512:(osl + 1) * 512],
                                        bk.t[:, 0:512],
                                        bpst[:, bsl * 1024 + half * 512:
                                             bsl * 1024 + (half + 1) * 512])
                return f
            t_ev = S.emit("dve", mkev(), waits=bk.pre_read() + [tok_bp, out_tok[osl]],
                          inc="dve")
            bk.readers.append(t_ev)
            bp_readers[bsl].append(t_ev)
            dma("sp", o_out[fc][:, half * 512:(half + 1) * 512],
                outst[:, osl * 512:(osl + 1) * 512], waits=[t_ev])

    final_waits = [(f"dsp{i}", S.cnt[f"dsp{i}"]) for i in range(RQ)
                   if S.cnt.get(f"dsp{i}", 0) > 0]
    final_waits += [(f"dpool{i}", S.cnt[f"dpool{i}"]) for i in range(RQ)
                    if S.cnt.get(f"dpool{i}", 0) > 0]
    S.emit("sp", lambda e: e.nop(), waits=final_waits)

    S.check()
    S.materialize(nc, sems)
    st.close()
    return nc, taps



def _tile_oi(w, nblk, bs):
    O, I = w.shape
    assert O == nblk * bs and I == KC * 128
    return np.ascontiguousarray(
        np.asarray(w).reshape(nblk, bs, KC, 128).transpose(0, 3, 2, 1))


def _consts(core):
    c0 = core * T
    inv = 1.0 / (1e4 ** (np.arange(0, D, 2, dtype=np.float64) / D))
    pos = np.arange(c0, c0 + T, dtype=np.float64)
    ang = pos[None, :] * np.concatenate([inv, inv])[:, None]
    cosfm = np.cos(ang).astype(np.float32)
    sinfm = np.sin(ang).astype(np.float32)
    jj, ii = np.meshgrid(np.arange(128), np.arange(128), indexing="ij")
    cuml = (jj <= ii).astype(np.float32)
    mask = np.where(ii >= jj, 0.0, -1e9).astype(np.float32)
    mask4 = np.tile(mask, (1, 4))
    idf = np.eye(128, dtype=np.float32)
    idbm = idf.astype(bfdt)
    P = np.zeros((128, 128), np.float32)
    for dd in range(64):
        P[dd, dd + 64] = -1.0
    for dd in range(64, 128):
        P[dd, dd - 64] = 1.0
    ptb = np.ascontiguousarray(P.T).astype(bfdt)
    onesb = np.ones((1, 512), np.float32).astype(bfdt)
    onesf = np.ones((1, 128), np.float32)
    crw = np.zeros((128, 8), np.float32)
    crw[:, 0] = 1e-5
    crw[:, 1] = 1.0
    crw[:, 2] = np.log(2.0)
    u = (np.arange(8) < core).astype(np.float32)
    urow = np.repeat(u, 32)[None, :].astype(np.float32)
    unot = 1.0 - urow
    mrow = urow.copy()
    return dict(cosfm=cosfm, sinfm=sinfm, cumlt=cuml, mask4=mask4, idf=idf,
                idb=idbm, ptb=ptb, onesb=onesb, onesf=onesf, crow=crw,
                urow=np.ascontiguousarray(urow), unot=np.ascontiguousarray(unot),
                mrow=np.ascontiguousarray(mrow))


_PROG_CACHE = {}


def _get_prog(taps_key=()):
    key = tuple(taps_key)
    if key not in _PROG_CACHE:
        _PROG_CACHE[key] = build_program(key)
    return _PROG_CACHE[key]


def kernel(**inputs):
    f32 = np.float32
    x = np.ascontiguousarray(np.asarray(inputs["L0_attn_hidden_states"], f32)
                             .reshape(NCORES, NCH, 128, DM))
    Wq = np.asarray(inputs["L0_attn_Wq"], f32)
    Wk = np.asarray(inputs["L0_attn_Wk"], f32)
    Wv = np.asarray(inputs["L0_attn_Wv"], f32)
    Wo = np.asarray(inputs["L0_attn_Wo"], f32)
    Wg = np.asarray(inputs["L0_attn_Wg"], f32)
    Win = np.asarray(inputs["L0_attn_Win"], f32)
    dt_b = np.asarray(inputs["L0_attn_dt_bias"], f32)
    A_log = np.asarray(inputs["L0_attn_A_log_bias"], f32)
    gnorm = np.asarray(inputs["L0_attn_gnorm_w"], f32)
    W_base = np.asarray(inputs["L0_ffn_W_base"], f32)
    b_base = np.asarray(inputs["L0_ffn_b_base"], f32)
    W_gate = np.asarray(inputs["L0_ffn_W_gate"], f32)
    W_thr = np.asarray(inputs["L0_ffn_W_thr"], f32)
    b_thr = np.asarray(inputs["L0_ffn_b_thr"], f32)
    A_w = np.asarray(inputs["L0_ffn_A_w"], f32)
    B_w = np.asarray(inputs["L0_ffn_B_w"], f32)

    wq_t = _tile_oi(Wq, H, 128)
    wg_t = _tile_oi(Wg, H, 128)
    wk_t = _tile_oi(Wk, KVH, 128)
    wv_t = _tile_oi(Wv, KVH, 128)
    wo_t = _tile_oi(Wo, KC, 128)
    wb_t = _tile_oi(W_base, KC, 128)
    a_hat = np.ascontiguousarray(A_w.transpose(2, 0, 1).reshape(DM, 128))
    smallw = np.concatenate([a_hat, W_gate.T, W_thr[:, None], Win.T], axis=1)
    smallw_t = np.ascontiguousarray(smallw.reshape(KC, 128, 169).transpose(1, 0, 2))
    bhat = np.ascontiguousarray(B_w.transpose(0, 2, 1).reshape(128, DM))
    bbase = np.ascontiguousarray(b_base[None, :])
    prow = np.concatenate([gnorm, A_log, dt_b, b_thr])[None, :].astype(f32)

    shared = dict(wq=wq_t, wg=wg_t, wk=wk_t, wv=wv_t, wo=wo_t, wb=wb_t,
                  smallw=smallw_t, bhat=bhat, bbase=bbase,
                  prow=np.ascontiguousarray(prow))

    taps_key = tuple(_DEBUG_TAPS)
    nc, taps = _get_prog(taps_key)
    in_maps = []
    for c in range(NCORES):
        m = dict(shared)
        m["x"] = np.ascontiguousarray(x[c])
        m["x_fm"] = np.ascontiguousarray(
            x[c].reshape(T, DM).T.reshape(KC, 128, T))
        m.update(_consts(c))
        in_maps.append(m)
    res = run_bass_kernel_spmd(nc, in_maps, core_ids=list(range(NCORES)))
    out = np.concatenate([np.ascontiguousarray(
        np.asarray(r["o_out"], f32).reshape(DM, T).T)
        for r in res.results], axis=0)
    if taps_key:
        kernel._last_results = res.results
    return out.reshape(1, NCORES * T, DM)


if __name__ == "__main__":
    nc, _ = build_program()
    print("BUILD OK")


# version 04 (final)
# speedup vs baseline: 1.1512x; 1.1082x over previous
import numpy as np
import ml_dtypes
import concourse.bass as bass
import concourse.mybir as mybir
from concourse.bass_utils import run_bass_kernel_spmd
from contextlib import ExitStack

F32 = mybir.dt.float32
BF16 = mybir.dt.bfloat16
AF = mybir.ActivationFunctionType
OP = mybir.AluOpType
AX = mybir.AxisListType
bfdt = ml_dtypes.bfloat16

NCORES = 8
T = 1024
NCH = 8
CH = 128
H = 32
KVH = 8
D = 128
DM = 4096
KC = 32
NG = 8
NHF = 16

ENGS = ("pe", "dve", "act", "pool", "sp")

_DEBUG_TAPS = []


class Sched:
    def __init__(self):
        self.q = {e: [] for e in ENGS}
        self.cnt = {}

    def emit(self, eng, fn, waits=(), inc=None, inc_by=1):
        waits = tuple(w for w in waits if w is not None)
        self.q[eng].append((fn, waits, inc, inc_by))
        if inc is not None:
            self.cnt[inc] = self.cnt.get(inc, 0) + inc_by
            return (inc, self.cnt[inc])
        return None

    def tok(self, sem):
        return (sem, self.cnt.get(sem, 0))

    def check(self):
        ptr = {e: 0 for e in ENGS}
        val = {}
        progress = True
        while progress:
            progress = False
            for e in ENGS:
                while ptr[e] < len(self.q[e]):
                    fn, waits, inc, inc_by = self.q[e][ptr[e]]
                    if all(val.get(s, 0) >= v for s, v in waits):
                        if inc is not None:
                            val[inc] = val.get(inc, 0) + inc_by
                        ptr[e] += 1
                        progress = True
                    else:
                        break
        stuck = {e: ptr[e] for e in ENGS if ptr[e] < len(self.q[e])}
        if stuck:
            det = {}
            for e, p in stuck.items():
                fn, waits, inc, _ = self.q[e][p]
                det[e] = (p, len(self.q[e]),
                          [(s, v, val.get(s, 0)) for s, v in waits if val.get(s, 0) < v])
            raise RuntimeError(f"schedule deadlock: {det}")

    def materialize(self, nc, sems):
        with nc.Block() as block:
            decos = {"pe": block.tensor, "dve": block.vector, "act": block.scalar,
                     "pool": block.gpsimd, "sp": block.sync}
            for eng in ENGS:
                items = self.q[eng]

                def body_fn(e, items=items, eng=eng):
                    last = {}
                    for fn, waits, inc, inc_by in items:
                        for s, v in waits:
                            if last.get(s, 0) < v:
                                e.wait_ge(sems[s], v)
                                last[s] = v
                        inst = fn(e)
                        if inc is not None:
                            assert inst is not None, f"inc op returned None on {eng}"
                            inst.then_inc(sems[inc], inc_by)
                decos[eng](body_fn)


class Bank:
    def __init__(self, tile):
        self.t = tile
        self.wr = None
        self.readers = []

    def pre_write(self):
        w = list(self.readers)
        self.readers = []
        if self.wr is not None:
            w.append(self.wr)
        return w

    def pre_read(self):
        return [self.wr] if self.wr is not None else []


def build_program(debug_taps=()):
    nc = bass.Bass("TRN2", target_bir_lowering=False)
    tap_phases = set(debug_taps)

    def din(name, shape, dt=F32):
        return nc.dram_tensor(name, list(shape), dt, kind="ExternalInput").ap()

    x_in = din("x", [NCH, 128, DM])
    wq_in = din("wq", [H, 128, KC, 128])
    wg_in = din("wg", [H, 128, KC, 128])
    wk_in = din("wk", [KVH, 128, KC, 128])
    wv_in = din("wv", [KVH, 128, KC, 128])
    wo_in = din("wo", [KC, 128, KC, 128])
    wb_in = din("wb", [KC, 128, KC, 128])
    xfm_in = din("x_fm", [KC, 128, T])
    smallw_in = din("smallw", [128, KC, 169])
    bhat_in = din("bhat", [128, DM])
    bbase_in = din("bbase", [1, DM])
    prow_in = din("prow", [1, 193])
    cosfm_in = din("cosfm", [128, T])
    sinfm_in = din("sinfm", [128, T])
    cum_in = din("cumlt", [128, 128])
    mask4_in = din("mask4", [128, 512])
    idb_in = din("idb", [128, 128], BF16)
    idf_in = din("idf", [128, 128])
    ptb_in = din("ptb", [128, 128], BF16)
    onesb_in = din("onesb", [1, 512], BF16)
    onesf_in = din("onesf", [1, 128])
    crow_in = din("crow", [128, 8])
    urow_in = din("urow", [1, 256])
    unot_in = din("unot", [1, 256])
    mrow_in = din("mrow", [1, 256])

    o_out = nc.dram_tensor("o_out", [KC, 128, T], F32, kind="ExternalOutput").ap()

    cc_in = nc.dram_tensor("cc_in", [128, 4128], F32, kind="Internal")
    cc_out = nc.dram_tensor("cc_out", [NCORES * 128, 4128], F32, kind="Internal")
    yscr = nc.dram_tensor("yscr", [NG, 128, NCH, 512], BF16, kind="Internal")
    bscr = nc.dram_tensor("bscr", [KC, 128, T], F32, kind="Internal")

    taps = {}

    def tapdecl(name, shape, dt=F32):
        taps[name] = nc.dram_tensor("tap_" + name, list(shape), dt,
                                    kind="ExternalOutput").ap()
        return taps[name]

    st = ExitStack()

    def sbuf(name, shape, dt):
        return st.enter_context(nc.sbuf_tensor(name, list(shape), dt))

    cosfm = sbuf("sb_cosfm", [128, T], F32)
    sinfm = sbuf("sb_sinfm", [128, T], F32)
    cum = sbuf("sb_cum", [128, 128], F32)
    mask4 = sbuf("sb_mask4", [128, 512], F32)
    idb = sbuf("sb_idb", [128, 128], BF16)
    idf = sbuf("sb_idf", [128, 128], F32)
    ptb = sbuf("sb_ptb", [128, 128], BF16)
    onesb = sbuf("sb_onesb", [1, 512], BF16)
    onesf = sbuf("sb_onesf", [1, 128], F32)
    crow = sbuf("sb_crow", [128, 8], F32)
    urow = sbuf("sb_urow", [1, 256], F32)
    unot = sbuf("sb_unot", [1, 256], F32)
    mrow = sbuf("sb_mrow", [1, 256], F32)
    prow = sbuf("sb_prow", [1, 193], F32)
    preps = sbuf("sb_preps", [128, 193], F32)
    arep = sbuf("sb_arep", [128, 32], F32)
    stats = sbuf("sb_stats", [128, 16], F32)
    tlcur = sbuf("sb_tlcur", [2, 512], F32)
    rhcur = sbuf("sb_rhcur", [2, 512], F32)
    grow_t = sbuf("sb_grow_t", [1, 32], F32)
    dallrow = sbuf("sb_dallrow", [1, 256], F32)
    dprow = sbuf("sb_dprow", [1, 256], F32)
    sfx = sbuf("sb_sfx", [1, 256], F32)
    prow2 = sbuf("sb_prow2", [1, 256], F32)
    fjunk = sbuf("sb_fjunk", [1, 32], F32)
    prep_sc = sbuf("sb_prep_sc", [128, 256], F32)

    u1 = sbuf("sb_u1", [128, 32768], BF16)
    ovb = sbuf("sb_ovb", [128, 32768], BF16)
    wslot_t = sbuf("sb_wslot", [128, 8192], BF16)
    arena = sbuf("sb_arena", [128, 18432], BF16)
    arena_f = arena.bitcast(F32)
    ovb_f = ovb.bitcast(F32)

    k_tm = ovb[:, 0:8192]
    v_tm = ovb[:, 8192:16384]
    state = ovb_f[:, 8192:12288]
    dt_all = ovb_f[:, 12288:12544]
    g_all = ovb_f[:, 12544:12800]
    expg_all = ovb_f[:, 12800:13056]
    kdsc_all = ovb_f[:, 13056:13312]
    drep_all = ovb_f[:, 13312:13568]
    hw_fm = ovb[:, 27136:28160]
    w_all = ovb_f[:, 14080:14144]
    glsum = ovb_f[:, 14144:14176]
    dall = ovb_f[:, 14176:14208]
    gT = ovb_f[0:32, 14208:15232]
    ngT = ovb_f[0:32, 15232:16256]
    gatesil = ovb[:, 0:32768]

    gs3 = gatesil.rearrange("p (hc t) -> p hc t", t=1024)
    k_tm3 = k_tm.rearrange("p (n f) -> p n f", f=1024)
    v_tm3 = v_tm.rearrange("p (n f) -> p n f", f=1024)
    xn_fm = u1[:, 0:32768].rearrange("p (k t) -> p k t", t=1024)
    yg_fm = u1[:, 0:32768].rearrange("p (ec r) -> p ec r", r=1024)
    wslot = [wslot_t[:, 0:4096].rearrange("p (k d) -> p k d", d=128),
             wslot_t[:, 4096:8192].rearrange("p (k d) -> p k d", d=128)]

    pbk = [st.enter_context(nc.psum_tensor(f"pb{i}", [128, 512], F32)) for i in range(6)]
    tbk = [st.enter_context(nc.psum_tensor(f"tb{i}", [128, 1024], BF16)) for i in range(2)]
    B = [Bank(t) for t in pbk]
    TB = [Bank(t) for t in tbk]

    RQ = 10
    sem_list = ["pe", "dve", "act", "pool", "sp", "cc", "fsp", "fact", "fpool"]
    sem_list += [f"dsp{i}" for i in range(RQ)] + [f"dpool{i}" for i in range(RQ)]
    sems = {n: st.enter_context(nc.semaphore(n)) for n in sem_list}
    qrot = {"sp": 0, "pool": 0}

    S = Sched()

    def dma(queue, out_ap, in_ap, waits=()):
        base = {"sp": "dsp", "pool": "dpool"}[queue]
        semn = f"{base}{qrot[queue] % RQ}"
        qrot[queue] += 1
        return S.emit(queue, lambda e, o=out_ap, i=in_ap: e.dma_start(out=o, in_=i),
                      waits=waits, inc=semn, inc_by=16)

    FEN = {"sp": "fsp", "act": "fact", "pool": "fpool"}
    FJ = {"sp": 0, "act": 4, "pool": 8}

    def fence(queue, via="sp"):
        base = {"sp": "dsp", "pool": "dpool"}[queue]
        waits = [(f"{base}{i}", S.cnt[f"{base}{i}"]) for i in range(RQ)
                 if S.cnt.get(f"{base}{i}", 0) > 0]
        off = FJ[via]
        if via == "sp":
            return S.emit(via, lambda e: e.nop(), waits=waits, inc=FEN[via])
        if via == "act":
            return S.emit(via, lambda e: e.memzero(fjunk[0:1, off:off + 1]),
                          waits=waits, inc=FEN[via])
        return S.emit(via, lambda e: e.memset(fjunk[0:1, off:off + 1], 0.0),
                      waits=waits, inc=FEN[via])

    ctoks = []
    for dst, src in [(cosfm, cosfm_in), (sinfm, sinfm_in), (cum, cum_in),
                     (mask4, mask4_in), (idb, idb_in), (idf, idf_in), (ptb, ptb_in),
                     (onesb, onesb_in), (onesf, onesf_in), (crow, crow_in),
                     (urow, urow_in), (unot, unot_in), (mrow, mrow_in),
                     (prow, prow_in)]:
        ctoks.append(dma("sp", dst[:, :], src))
    tok_consts = fence("sp")

    S.emit("dve", lambda e: e.memset(tlcur[0:2, :], 1.0), inc="dve")
    S.emit("dve", lambda e: e.memset(rhcur[0:2, :], 1.0), inc="dve")
    tok_gl = S.emit("dve", lambda e: e.memset(glsum[:, :], 0.0), inc="dve")
    tok_st0 = S.emit("dve", lambda e: e.memset(state[:, :], 0.0), inc="dve")

    wts = B[2].pre_write() + [tok_consts]
    tok = S.emit("pe", lambda e: e.matmul(pbk[2][:, 0:193], onesf[0:1, :],
                                          prow[0:1, :], start=True, stop=True),
                 waits=wts, inc="pe")
    B[2].wr = tok
    tok_preps = S.emit("dve", lambda e: e.tensor_copy(preps[:, :], pbk[2][:, 0:193]),
                       waits=B[2].pre_read(), inc="dve")
    B[2].readers.append(tok_preps)
    tk = S.emit("act", lambda e: e.activation(arep[:, :], preps[:, 128:160], AF.Exp),
                waits=[tok_preps], inc="act")
    tok_arep = S.emit("dve", lambda e: e.tensor_scalar_mul(arep[:, :], arep[:, :], -1.0),
                      waits=[tk], inc="dve")
    tok_nbthr = S.emit("dve", lambda e: e.tensor_scalar_mul(stats[:, 15:16],
                                                            preps[:, 192:193], -1.0),
                       waits=[tok_preps], inc="dve")

    x_st = arena_f[:, 0:4096]
    xn_tm = arena[:, 8192:12288]
    prev_x_readers = []
    prev_xn_tok = None
    for n in range(NCH):
        tok_x = dma("sp", x_st[:, :], x_in[n], waits=prev_x_readers)
        tok_x = fence("sp")
        prev_x_readers = []
        t_sum = S.emit("dve", lambda e: e.reduce_sum(stats[:, 0:1], x_st[:, :], axis=AX.X),
                       waits=[tok_x], inc="dve")
        t_ssq = S.emit("act", lambda e: e.activation(xn_tm[:, :], x_st[:, :], AF.Square,
                                                     accum_out=stats[:, 1:2]),
                       waits=[tok_x, prev_xn_tok], inc="act")
        t_m = S.emit("dve", lambda e: e.tensor_scalar_mul(stats[:, 2:3], stats[:, 0:1],
                                                          1.0 / DM),
                     waits=[t_sum], inc="dve")
        t_msq = S.emit("dve", lambda e: e.tensor_scalar(stats[:, 3:4], stats[:, 2:3],
                                                        stats[:, 2:3], None, OP.mult),
                       waits=[t_m], inc="dve")
        t_va = S.emit("dve", lambda e: e.tensor_scalar(stats[:, 4:5], stats[:, 1:2],
                                                       1.0 / DM, stats[:, 3:4],
                                                       OP.mult, OP.subtract),
                      waits=[t_ssq, t_msq], inc="dve")
        t_ln = S.emit("act", lambda e: e.activation(stats[:, 5:6], stats[:, 4:5], AF.Ln,
                                                    bias=crow[:, 0:1]),
                      waits=[t_va, tok_consts], inc="act")
        t_rstd = S.emit("act", lambda e: e.activation(stats[:, 6:7], stats[:, 5:6], AF.Exp,
                                                      scale=-0.5),
                        waits=[t_ln], inc="act")
        t_xn = S.emit("dve", lambda e: e.tensor_scalar(xn_tm[:, :], x_st[:, :],
                                                       stats[:, 2:3], stats[:, 6:7],
                                                       OP.subtract, OP.mult),
                      waits=[t_rstd, prev_xn_tok], inc="dve")
        prev_x_readers = [t_xn]
        ev_tok = None
        for grp in range(4):
            bank = TB[grp % 2]
            wts = bank.pre_write() + [t_xn, tok_consts]
            last = None
            for j8 in range(8):
                j = grp * 8 + j8

                def mk(j=j, j8=j8, bank=bank):
                    def f(e):
                        return e.transpose(bank.t[:, j8 * 128:(j8 + 1) * 128],
                                           xn_tm[:, j * 128:(j + 1) * 128], idb[:, :])
                    return f
                last = S.emit("pe", mk(), waits=(wts if j8 == 0 else []),
                              inc="pe" if j8 == 7 else None)
            bank.wr = last

            def mkev(grp=grp, n=n, bank=bank):
                def f(e):
                    dst = xn_fm[:, grp * 8:(grp + 1) * 8, n * 128:(n + 1) * 128]
                    src = bank.t[:, 0:1024].rearrange("p (j t) -> p j t", t=128)
                    return e.tensor_copy(dst, src)
                return f
            ev_tok = S.emit("dve", mkev(), waits=bank.pre_read(), inc="dve")
            bank.readers.append(ev_tok)
        prev_xn_tok = ev_tok
    tok_xn_done = S.tok("dve")

    if "P0" in tap_phases:
        tp = tapdecl("xn_fm", [128, 32768], BF16)
        dma("sp", tp, u1[:, 0:32768], waits=[S.tok("dve")])

    smallw = arena[:, 0:5408].rearrange("p (k c) -> p k c", c=169)
    zt = arena_f[:, 2752:2784]
    zt2 = arena_f[:, 2784:2816]
    zt3 = arena_f[:, 2816:2848]
    hw_tmp = arena[:, 5696:5824]
    gex = arena_f[:, 2912:2920]
    kbfA = arena[:, 6144:6656]
    t1f = arena_f[:, 3328:3840]
    t2f = arena_f[:, 3840:4352]
    kfm_t = arena[:, 8704:9216]
    xc_t = arena[:, 9216:9728]
    kd_t = arena[:, 9728:10240]

    tok_smallw = dma("pool", smallw, smallw_in, waits=[tok_xn_done])
    tok_smallw = fence("pool", via="act")

    prev_t_wi = None
    prev_t_tr = None
    prev_t_glr = None
    for n in range(NCH):
        bank = B[n % 2]
        wts = bank.pre_write() + [tok_smallw, tok_xn_done]
        last = None
        for k in range(KC):
            def mk(k=k, n=n, bank=bank):
                def f(e):
                    return e.matmul(bank.t[:, 0:169],
                                    xn_fm[:, k, n * 128:(n + 1) * 128],
                                    smallw[:, k, :],
                                    start=(k == 0), stop=(k == KC - 1))
                return f
            last = S.emit("pe", mk(), waits=(wts if k == 0 else []),
                          inc="pe" if k == KC - 1 else None)
        bank.wr = last
        rd = bank.pre_read()
        t_z = S.emit("dve", lambda e, b=bank: e.tensor_add(zt[:, :], b.t[:, 137:169],
                                                           preps[:, 160:192]),
                     waits=rd + [tok_preps, prev_t_wi], inc="dve")
        bank.readers.append(t_z)
        t_n = S.emit("dve", lambda e: e.tensor_scalar_mul(zt2[:, :], zt[:, :], -1.0),
                     waits=[t_z], inc="dve")
        t_a = S.emit("dve", lambda e: e.tensor_max(zt2[:, :], zt2[:, :], zt[:, :]),
                     waits=[t_n], inc="dve")
        t_e = S.emit("act", lambda e: e.activation(zt2[:, :], zt2[:, :], AF.Exp, scale=-1.0),
                     waits=[t_a], inc="act")
        t_l = S.emit("act", lambda e: e.activation(zt2[:, :], zt2[:, :], AF.Ln,
                                                   bias=crow[:, 1:2]),
                     waits=[t_e], inc="act")
        t_r = S.emit("dve", lambda e: e.tensor_scalar_max(zt3[:, :], zt[:, :], 0.0),
                     waits=[t_l], inc="dve")
        t_dt = S.emit("dve", lambda e, n=n: e.tensor_add(dt_all[:, n * 32:(n + 1) * 32],
                                                         zt3[:, :], zt2[:, :]),
                      waits=[t_r], inc="dve")
        t_g = S.emit("dve", lambda e, n=n: e.tensor_mul(g_all[:, n * 32:(n + 1) * 32],
                                                        dt_all[:, n * 32:(n + 1) * 32],
                                                        arep[:, :]),
                     waits=[t_dt, tok_arep], inc="dve")
        wts2 = B[2].pre_write() + [t_g, tok_consts]
        t_cum = S.emit("pe", lambda e, n=n: e.matmul(pbk[2][:, 0:32], cum[:, :],
                                                     g_all[:, n * 32:(n + 1) * 32],
                                                     start=True, stop=True),
                       waits=wts2, inc="pe")
        B[2].wr = t_cum
        t_G = S.emit("dve", lambda e, n=n: e.tensor_copy(g_all[:, n * 32:(n + 1) * 32],
                                                         pbk[2][:, 0:32]),
                     waits=B[2].pre_read(), inc="dve")
        B[2].readers.append(t_G)
        t_eg = S.emit("act", lambda e, n=n: e.activation(expg_all[:, n * 32:(n + 1) * 32],
                                                         g_all[:, n * 32:(n + 1) * 32],
                                                         AF.Exp),
                      waits=[t_G], inc="act")
        t_gr = dma("pool", grow_t[0:1, :], g_all[127:128, n * 32:(n + 1) * 32],
                   waits=[t_G, prev_t_glr])
        t_gr = fence("pool", via="act")
        wts3 = B[2].pre_write() + [t_gr]
        t_glr = S.emit("pe", lambda e: e.matmul(pbk[2][:, 32:64], onesf[0:1, :],
                                                grow_t[0:1, :], start=True, stop=True),
                       waits=wts3, inc="pe")
        B[2].wr = t_glr
        prev_t_glr = t_glr
        rd2 = B[2].pre_read()
        t_sub = S.emit("dve", lambda e, n=n: e.tensor_sub(zt[:, :], pbk[2][:, 32:64],
                                                          g_all[:, n * 32:(n + 1) * 32]),
                       waits=rd2 + [t_eg], inc="dve")
        B[2].readers.append(t_sub)
        S.emit("act", lambda e, n=n: e.activation(kdsc_all[:, n * 32:(n + 1) * 32],
                                                  zt[:, :], AF.Exp),
               waits=[t_sub], inc="act")
        t_dr = S.emit("act", lambda e, n=n: e.activation(drep_all[:, n * 32:(n + 1) * 32],
                                                         pbk[2][:, 32:64], AF.Exp),
                      waits=rd2, inc="act")
        B[2].readers.append(t_dr)
        t_gl = S.emit("dve", lambda e: e.tensor_add(glsum[:, :], glsum[:, :],
                                                    pbk[2][:, 32:64]),
                      waits=rd2 + [tok_gl], inc="dve")
        B[2].readers.append(t_gl)
        tok_gl = t_gl
        wts4 = B[2].pre_write() + [t_G, tok_consts]
        t_gt = S.emit("pe", lambda e, n=n: e.transpose(pbk[2][0:32, 0:128],
                                                       g_all[:, n * 32:(n + 1) * 32],
                                                       idf[:, :]),
                      waits=wts4, inc="pe")
        B[2].wr = t_gt
        rd3 = B[2].pre_read()
        t_gtc = S.emit("dve", lambda e, n=n: e.tensor_copy(gT[:, n * 128:(n + 1) * 128],
                                                           pbk[2][0:32, 0:128]),
                       waits=rd3, inc="dve")
        B[2].readers.append(t_gtc)
        S.emit("dve", lambda e, n=n: e.tensor_scalar_mul(ngT[:, n * 128:(n + 1) * 128],
                                                         gT[:, n * 128:(n + 1) * 128],
                                                         -1.0),
               waits=[t_gtc], inc="dve")
        t_nm = S.emit("dve", lambda e, b=bank: e.reduce_max(stats[:, 8:9],
                                                            b.t[:, 128:136],
                                                            axis=AX.X, negate=True),
                      waits=rd + [prev_t_wi], inc="dve")
        bank.readers.append(t_nm)
        t_ge = S.emit("act", lambda e, b=bank: e.activation(gex[:, :], b.t[:, 128:136],
                                                            AF.Exp, bias=stats[:, 8:9],
                                                            accum_out=stats[:, 9:10]),
                      waits=[t_nm], inc="act")
        bank.readers.append(t_ge)
        t_s1 = S.emit("act", lambda e, b=bank: e.activation(stats[:, 10:11],
                                                            b.t[:, 136:137], AF.Exp,
                                                            scale=-1.0,
                                                            bias=stats[:, 15:16]),
                      waits=rd + [tok_nbthr], inc="act")
        bank.readers.append(t_s1)
        t_s2 = S.emit("act", lambda e: e.activation(stats[:, 10:11], stats[:, 10:11],
                                                    AF.Ln, bias=crow[:, 1:2]),
                      waits=[t_s1], inc="act")
        t_s3 = S.emit("act", lambda e: e.activation(stats[:, 10:11], stats[:, 10:11],
                                                    AF.Exp, scale=-1.0),
                      waits=[t_s2], inc="act")
        t_nt = S.emit("dve", lambda e: e.tensor_scalar_mul(stats[:, 11:12],
                                                           stats[:, 10:11], -0.125),
                      waits=[t_s3], inc="dve")
        t_gi = S.emit("act", lambda e: e.activation(stats[:, 12:13], stats[:, 9:10], AF.Ln),
                      waits=[t_ge], inc="act")
        t_gi = S.emit("act", lambda e: e.activation(stats[:, 12:13], stats[:, 12:13],
                                                    AF.Exp, scale=-1.0),
                      waits=[t_gi], inc="act")
        t_ad = S.emit("dve", lambda e: e.tensor_scalar(gex[:, :], gex[:, :],
                                                       stats[:, 12:13], stats[:, 11:12],
                                                       OP.mult, OP.add),
                      waits=[t_gi, t_nt], inc="dve")
        t_wr = S.emit("dve", lambda e: e.tensor_scalar_max(gex[:, :], gex[:, :], 0.0),
                      waits=[t_ad], inc="dve")
        t_ws = S.emit("dve", lambda e: e.reduce_sum(stats[:, 13:14], gex[:, :], axis=AX.X),
                      waits=[t_wr], inc="dve")
        t_iz = S.emit("dve", lambda e: e.tensor_scalar(stats[:, 14:15], stats[:, 13:14],
                                                       0.0, None, OP.is_equal),
                      waits=[t_ws], inc="dve")
        t_w2 = S.emit("dve", lambda e: e.tensor_add(stats[:, 13:14], stats[:, 13:14],
                                                    stats[:, 14:15]),
                      waits=[t_iz], inc="dve")
        t_wl = S.emit("act", lambda e: e.activation(stats[:, 13:14], stats[:, 13:14], AF.Ln),
                      waits=[t_w2], inc="act")
        t_wi = S.emit("act", lambda e: e.activation(stats[:, 13:14], stats[:, 13:14],
                                                    AF.Exp, scale=-1.0, bias=crow[:, 2:3]),
                      waits=[t_wl], inc="act")
        prev_t_wi = t_wi
        t_wf = S.emit("dve", lambda e, n=n: e.tensor_scalar(w_all[:, n * 8:(n + 1) * 8],
                                                            gex[:, :], stats[:, 13:14],
                                                            None, OP.mult),
                      waits=[t_wi], inc="dve")
        last_hw = None
        for ei in range(8):
            def mkhw(ei=ei, n=n, bank=bank):
                def f(e):
                    return e.tensor_scalar(hw_tmp[:, ei * 16:(ei + 1) * 16],
                                           bank.t[:, ei * 16:(ei + 1) * 16],
                                           w_all[:, n * 8 + ei:n * 8 + ei + 1],
                                           None, OP.mult)
                return f
            last_hw = S.emit("dve", mkhw(), waits=([t_wf, prev_t_tr] if ei == 0 else []),
                             inc="dve" if ei == 7 else None)
        bank.readers.append(last_hw)
        tbank = TB[n % 2]
        wts5 = tbank.pre_write() + [last_hw, tok_consts]
        t_tr = S.emit("pe", lambda e, tb=tbank: e.transpose(tb.t[:, 0:128], hw_tmp[:, :],
                                                            idb[:, :]),
                      waits=wts5, inc="pe")
        tbank.wr = t_tr
        prev_t_tr = t_tr
        t_he = S.emit("dve", lambda e, n=n, tb=tbank: e.tensor_copy(
            hw_fm[:, n * 128:(n + 1) * 128], tb.t[:, 0:128]),
            waits=tbank.pre_read(), inc="dve")
        tbank.readers.append(t_he)
    tok_small_done = S.tok("dve")

    slot_readers = [[], []]
    sl_i = 0
    pj_i = 0
    for which, w_in_, dst3 in [("k", wk_in, k_tm3), ("v", wv_in, v_tm3)]:
        for kv in range(KVH):
            sl = sl_i % 2
            sl_i += 1
            tok_w = dma("pool", wslot[sl], w_in_[kv], waits=slot_readers[sl])
            tok_w = fence("pool", via="act")
            slot_readers[sl] = []
            for half in range(2):
                bank = B[pj_i % 2]
                pj_i += 1
                wts = bank.pre_write() + [tok_w, tok_xn_done]
                last = None
                for k in range(KC):
                    def mk(k=k, half=half, sl=sl, bank=bank):
                        def f(e):
                            return e.matmul(bank.t[:, 0:512], wslot[sl][:, k, :],
                                            xn_fm[:, k, half * 512:(half + 1) * 512],
                                            start=(k == 0), stop=(k == KC - 1))
                        return f
                    last = S.emit("pe", mk(), waits=(wts if k == 0 else []),
                                  inc="pe" if k == KC - 1 else None)
                bank.wr = last
                slot_readers[sl].append(last)
                rd = bank.pre_read()
                if which == "k":
                    t_c = S.emit("dve", lambda e, b=bank: e.tensor_copy(kbfA[:, :],
                                                                        b.t[:, 0:512]),
                                 waits=rd, inc="dve")
                    bank.readers.append(t_c)
                    wts2 = B[2].pre_write() + [t_c]
                    t_rot = S.emit("pe", lambda e: e.matmul(pbk[2][:, 0:512], ptb[:, :],
                                                            kbfA[:, :], start=True,
                                                            stop=True),
                                   waits=wts2, inc="pe")
                    B[2].wr = t_rot
                    t_t1 = S.emit("dve", lambda e, b=bank, half=half: e.tensor_mul(
                        t1f[:, :], b.t[:, 0:512],
                        cosfm[:, half * 512:(half + 1) * 512]),
                        waits=rd, inc="dve")
                    bank.readers.append(t_t1)
                    rd2 = B[2].pre_read()
                    t_t2 = S.emit("dve", lambda e, half=half: e.tensor_mul(
                        t2f[:, :], pbk[2][:, 0:512],
                        sinfm[:, half * 512:(half + 1) * 512]),
                        waits=rd2, inc="dve")
                    B[2].readers.append(t_t2)
                    t_src = S.emit("dve", lambda e: e.tensor_add(kfm_t[:, :], t1f[:, :],
                                                                 t2f[:, :]),
                                   waits=[t_t1, t_t2], inc="dve")
                else:
                    t_src = S.emit("dve", lambda e, b=bank: e.tensor_copy(kfm_t[:, :],
                                                                          b.t[:, 0:512]),
                                   waits=rd, inc="dve")
                    bank.readers.append(t_src)
                tb2 = TB[pj_i % 2]
                wts3 = tb2.pre_write() + [t_src, tok_consts]
                last_t = None
                for j in range(4):
                    def mktr(j=j, tb2=tb2):
                        def f(e):
                            return e.transpose(tb2.t[:, j * 128:(j + 1) * 128],
                                               kfm_t[:, j * 128:(j + 1) * 128],
                                               idb[:, :])
                        return f
                    last_t = S.emit("pe", mktr(), waits=(wts3 if j == 0 else []),
                                    inc="pe" if j == 3 else None)
                tb2.wr = last_t

                def mkev(half=half, kv=kv, dst3=dst3, tb2=tb2):
                    def f(e):
                        dst = dst3[:, 4 * half:4 * half + 4, kv * 128:(kv + 1) * 128]
                        src = tb2.t[:, 0:512].rearrange("p (j t) -> p j t", t=128)
                        return e.tensor_copy(dst, src)
                    return f
                t_ev = S.emit("dve", mkev(), waits=tb2.pre_read(), inc="dve")
                tb2.readers.append(t_ev)
    tok_kv_done = S.tok("dve")

    prev_cs = None
    for n in range(NCH):
        for g in range(NG):
            last_xc = None
            for hp in range(4):
                h = 4 * g + hp

                def mkxc(hp=hp, h=h, n=n, g=g):
                    def f(e):
                        return e.tensor_scalar(xc_t[:, hp * 128:(hp + 1) * 128],
                                               v_tm3[:, n, g * 128:(g + 1) * 128],
                                               dt_all[:, n * 32 + h:n * 32 + h + 1],
                                               None, OP.mult)
                    return f
                last_xc = S.emit("dve", mkxc(),
                                 waits=([tok_kv_done, prev_cs] if hp == 0 else []),
                                 inc="dve" if hp == 3 else None)
            last_kd = None
            for hp in range(4):
                h = 4 * g + hp

                def mkkd(hp=hp, h=h, n=n, g=g):
                    def f(e):
                        return e.tensor_scalar(kd_t[:, hp * 128:(hp + 1) * 128],
                                               k_tm3[:, n, g * 128:(g + 1) * 128],
                                               kdsc_all[:, n * 32 + h:n * 32 + h + 1],
                                               None, OP.mult)
                    return f
                last_kd = S.emit("dve", mkkd(), inc="dve" if hp == 3 else None)
            bank = B[5]
            wts = bank.pre_write() + [last_xc, last_kd]
            last_cs = None
            for hp in range(4):
                def mkcs(hp=hp, bank=bank):
                    def f(e):
                        return e.matmul(bank.t[:, hp * 128:(hp + 1) * 128],
                                        kd_t[:, hp * 128:(hp + 1) * 128],
                                        xc_t[:, hp * 128:(hp + 1) * 128],
                                        start=True, stop=True)
                    return f
                last_cs = S.emit("pe", mkcs(), waits=(wts if hp == 0 else []),
                                 inc="pe" if hp == 3 else None)
            bank.wr = last_cs
            prev_cs = last_cs
            rd = bank.pre_read()
            last_up = None
            for hp in range(4):
                h = 4 * g + hp

                def mkup(hp=hp, h=h, n=n, bank=bank):
                    def f(e):
                        blk = state[:, h * 128:(h + 1) * 128]
                        return e.scalar_tensor_tensor(
                            blk, blk, drep_all[:, n * 32 + h:n * 32 + h + 1],
                            bank.t[:, hp * 128:(hp + 1) * 128], OP.mult, OP.add)
                    return f
                last_up = S.emit("dve", mkup(),
                                 waits=(rd + [tok_st0] if hp == 0 else []),
                                 inc="dve" if hp == 3 else None)
            bank.readers.append(last_up)
    tok_state_done = S.tok("dve")
    tok_dall = S.emit("act", lambda e: e.activation(dall[:, :], glsum[:, :], AF.Exp),
                      waits=[tok_gl], inc="act")

    if "P1" in tap_phases:
        for nm, buf, wd, dt_ in [("k_tm", k_tm, 8192, BF16), ("v_tm", v_tm, 8192, BF16),
                                 ("dt_all", dt_all, 256, F32), ("g_all", g_all, 256, F32),
                                 ("kdsc_all", kdsc_all, 256, F32),
                                 ("drep_all", drep_all, 256, F32),
                                 ("s_c", state, 4096, F32), ("dall", dall, 32, F32),
                                 ("hw_fm", hw_fm, 1024, BF16),
                                 ("w_all", w_all, 64, F32),
                                 ("expg_all", expg_all, 256, F32)]:
            tp = tapdecl(nm, [128, wd], dt_)
            dma("sp", tp, buf[:, 0:wd],
                waits=[tok_state_done, tok_dall, S.tok("act"), S.tok("dve")])

    t_cc1 = dma("pool", cc_in.ap()[:, 0:4096], state[:, :], waits=[tok_state_done])
    t_cc2 = dma("pool", cc_in.ap()[:, 4096:4128], dall[:, :], waits=[tok_dall])
    tok_ccin = fence("pool", via="pool")
    tok_cc = S.emit("pool", lambda e: e.collective_compute(
        "AllGather", OP.bypass, replica_groups=[list(range(NCORES))],
        ins=[cc_in.ap().opt()], outs=[cc_out.ap().opt()]),
        waits=[tok_ccin], inc="cc")

    bhat_sb = arena[:, 0:4096]
    bhat3 = bhat_sb.rearrange("p (fc c) -> p fc c", c=128)
    bbase_sb = arena[0:1, 4096:8192]
    xfm_st = arena_f[:, 4096:6144]
    boutw = arena_f[:, 6144:7168]
    tok_bh = dma("pool", bhat_sb[:, :], bhat_in,
                 waits=slot_readers[0] + slot_readers[1] + [S.tok("pe"), S.tok("dve")])
    dma("pool", bbase_sb[0:1, :], bbase_in)
    tok_bh = fence("pool", via="act")
    wsl_i = 0
    xf_readers = [[S.tok("dve"), S.tok("pe")], [S.tok("dve"), S.tok("pe")]]
    bout_tok = [None, None]
    for fc in range(KC):
        sl = wsl_i % 2
        wsl_i += 1
        tok_w = dma("pool", wslot[sl], wb_in[fc], waits=slot_readers[sl])
        tok_w = fence("pool", via="act")
        slot_readers[sl] = []
        xsl = fc % 2
        tok_x2 = dma("sp", xfm_st[:, xsl * 1024:(xsl + 1) * 1024], xfm_in[fc],
                     waits=xf_readers[xsl])
        tok_x2 = fence("sp")
        bout_tok[0] = bout_tok[1] = tok_x2
        xf_readers[xsl] = []
        pp = fc % 2
        bkA, bkB = B[2 * pp], B[2 * pp + 1]
        wtsA = bkA.pre_write() + [tok_w, tok_bh, tok_small_done]
        wtsB = bkB.pre_write()
        for k in range(KC):
            def mkA(k=k, sl=sl, bkA=bkA):
                def f(e):
                    return e.matmul(bkA.t[:, 0:512], wslot[sl][:, k, :],
                                    xn_fm[:, k, 0:512], start=(k == 0), stop=False)
                return f
            def mkB(k=k, sl=sl, bkB=bkB):
                def f(e):
                    return e.matmul(bkB.t[:, 0:512], wslot[sl][:, k, :],
                                    xn_fm[:, k, 512:1024], start=(k == 0), stop=False)
                return f
            S.emit("pe", mkA(), waits=(wtsA if k == 0 else []))
            S.emit("pe", mkB(), waits=(wtsB if k == 0 else []))
        for bk, tsl in ((bkA, slice(0, 512)), (bkB, slice(512, 1024))):
            def mkmoe(fc=fc, bk=bk, tsl=tsl):
                def f(e):
                    return e.matmul(bk.t[:, 0:512], bhat3[:, fc, :], hw_fm[:, tsl],
                                    start=False, stop=False)
                return f
            S.emit("pe", mkmoe())
        lasts = []
        for bk in (bkA, bkB):
            def mkbias(fc=fc, bk=bk):
                def f(e):
                    return e.matmul(bk.t[:, 0:512],
                                    bbase_sb[0:1, fc * 128:(fc + 1) * 128],
                                    onesb[0:1, 0:512], start=False, stop=True)
                return f
            tokl = S.emit("pe", mkbias(), inc="pe")
            bk.wr = tokl
            lasts.append(tokl)
        slot_readers[sl] += lasts
        for half, bk in ((0, bkA), (1, bkB)):
            bsl = half
            def mkev2(half=half, bk=bk, xsl=xsl, bsl=bsl):
                def f(e):
                    return e.tensor_add(boutw[:, bsl * 512:(bsl + 1) * 512],
                                        bk.t[:, 0:512],
                                        xfm_st[:, xsl * 1024 + half * 512:
                                               xsl * 1024 + (half + 1) * 512])
                return f
            t_ev = S.emit("dve", mkev2(), waits=bk.pre_read() + [tok_x2, bout_tok[bsl]],
                          inc="dve")
            bk.readers.append(t_ev)
            xf_readers[xsl].append(t_ev)
            def mkst(fc=fc, half=half, bsl=bsl):
                def f(e):
                    return e.dma_start(out=bscr.ap()[fc][:, half * 512:(half + 1) * 512],
                                       in_=boutw[:, bsl * 512:(bsl + 1) * 512])
                return f
            dma("sp", bscr.ap()[fc][:, half * 512:(half + 1) * 512],
                boutw[:, bsl * 512:(bsl + 1) * 512], waits=[t_ev])
    if "B1" in tap_phases:
        tp = tapdecl("bscr", [KC, 128, T], F32)
        dma("sp", tp, bscr.ap(), waits=[fence("sp")])

    sslot = arena_f[:, 7168:9216]
    tok_dr = dma("pool", dallrow[0:1, :].rearrange("o (r c) -> o r c", c=32),
                 cc_out.ap().rearrange("(r p) c -> r p c", p=128)[:, 0:1, 4096:4128],
                 waits=[tok_cc])
    tok_dr = fence("pool", via="act")
    t_d1 = S.emit("dve", lambda e: e.tensor_mul(dprow[0:1, :], dallrow[0:1, :],
                                                urow[0:1, :]),
                  waits=[tok_dr, tok_consts], inc="dve")
    t_d2 = S.emit("dve", lambda e: e.tensor_add(dprow[0:1, :], dprow[0:1, :],
                                                unot[0:1, :]),
                  waits=[t_d1], inc="dve")
    t_sf = S.emit("dve", lambda e: e.memset(sfx[0:1, 224:256], 1.0), waits=[], inc="dve")
    for j in range(6, -1, -1):
        def mksf(j=j):
            def f(e):
                return e.tensor_mul(sfx[0:1, j * 32:(j + 1) * 32],
                                    sfx[0:1, (j + 1) * 32:(j + 2) * 32],
                                    dprow[0:1, (j + 1) * 32:(j + 2) * 32])
            return f
        t_sf = S.emit("dve", mksf(), waits=[t_d2, t_sf], inc="dve")
    t_pr = S.emit("dve", lambda e: e.tensor_mul(prow2[0:1, :], sfx[0:1, :], mrow[0:1, :]),
                  waits=[t_sf], inc="dve")
    wts = B[2].pre_write() + [t_pr]
    t_mm = S.emit("pe", lambda e: e.matmul(pbk[2][:, 0:256], onesf[0:1, :],
                                           prow2[0:1, :], start=True, stop=True),
                  waits=wts, inc="pe")
    B[2].wr = t_mm
    t_prep = S.emit("dve", lambda e: e.tensor_copy(prep_sc[:, :], pbk[2][:, 0:256]),
                    waits=B[2].pre_read(), inc="dve")
    B[2].readers.append(t_prep)
    ss_readers = []
    first = True
    for j in range(NCORES):
        for hh in range(2):
            tok_s = dma("pool", sslot[:, :],
                        cc_out.ap()[j * 128:(j + 1) * 128, hh * 2048:(hh + 1) * 2048],
                        waits=ss_readers + [tok_cc, (tok_ccin if first else None)])
            tok_s = fence("pool", via="act")
            ss_readers = []
            last_in = None
            for hq in range(16):
                h = hh * 16 + hq

                def mkin(j=j, h=h, hq=hq, first=first):
                    def f(e):
                        blk = state[:, h * 128:(h + 1) * 128]
                        src = sslot[:, hq * 128:(hq + 1) * 128]
                        sc = prep_sc[:, j * 32 + h:j * 32 + h + 1]
                        if j == 0:
                            return e.tensor_scalar(blk, src, sc, None, OP.mult)
                        return e.scalar_tensor_tensor(blk, src, sc, blk, OP.mult, OP.add)
                    return f
                last_in = S.emit("dve", mkin(),
                                 waits=([tok_s, t_prep] if hq == 0 else []),
                                 inc="dve" if hq == 15 else None)
            ss_readers.append(last_in)
        first = False
    tok_incoming = S.tok("dve")

    if "CC" in tap_phases:
        tp = tapdecl("incoming", [128, 4096], F32)
        dma("sp", tp, state[:, :], waits=[tok_incoming])

    qt_bf = arena[:, 0:512]
    t1q = arena_f[:, 256:768]
    t2q = arena_f[:, 768:1280]
    kfm_a = [arena[:, 2560:2688], arena[:, 2688:2816]]
    xc_a = [arena[:, 2816:3328], arena[:, 3328:3840]]
    kd_a = [arena[:, 3840:4352], arena[:, 4352:4864]]
    sbf_a = [arena[:, 4864:5376], arena[:, 5376:5888]]
    stbf_a = arena[:, 5888:6400]
    tsb_a = arena_f[:, 3200:3712]
    dsb_a = arena_f[:, 3712:4224]
    ysb_a = arena_f[:, 4224:4736]
    sq_a = tsb_a
    yint_a = dsb_a
    qfm_b = [arena[:, 9472:13568], arena[:, 13568:17664]]
    ynst = [arena[:, 17664:18176], prep_sc.bitcast(BF16)[:, 0:512]]
    ynst_tok = [None, None]
    qfm_readers = [[], []]

    tok_spq = fence("sp")
    a2_start = [tok_incoming, tok_spq, S.tok("act"), S.tok("pe"), S.tok("pool")]
    prev_t_dx = None
    prev_t_y = None
    prev_yb = None
    prev_cs_a2 = None
    prev_t_ln2 = None
    prev_t_tm = None
    prev_qw = None
    prev_up = None
    for g in range(NG):
        gpar = g % 2
        qfm_g3 = qfm_b[gpar].rearrange("p (n c) -> p n c", c=512)
        qfm_toks = []
        first_qw = True
        for hp in range(4):
            h = 4 * g + hp
            sl = sl_i % 2
            sl_i += 1
            tok_w = dma("pool", wslot[sl], wq_in[h],
                        waits=slot_readers[sl] + ([S.tok("pe")] if g == 0 else []))
            tok_w = fence("pool", via="act")
            slot_readers[sl] = []
            for half in range(2):
                bank = B[0]
                wts = bank.pre_write() + [tok_w] + (a2_start if g == 0 else [])
                last = None
                for k in range(KC):
                    def mk(k=k, half=half, sl=sl, bank=bank):
                        def f(e):
                            return e.matmul(bank.t[:, 0:512], wslot[sl][:, k, :],
                                            xn_fm[:, k, half * 512:(half + 1) * 512],
                                            start=(k == 0), stop=(k == KC - 1))
                        return f
                    last = S.emit("pe", mk(), waits=(wts if k == 0 else []),
                                  inc="pe" if k == KC - 1 else None)
                bank.wr = last
                slot_readers[sl].append(last)
                rd = bank.pre_read()
                t_c = S.emit("dve", lambda e, b=bank: e.tensor_copy(qt_bf[:, :],
                                                                     b.t[:, 0:512]),
                             waits=rd, inc="dve")
                bank.readers.append(t_c)
                wts2 = B[1].pre_write() + [t_c]
                t_rot = S.emit("pe", lambda e: e.matmul(pbk[1][:, 0:512], ptb[:, :],
                                                        qt_bf[:, :], start=True,
                                                        stop=True),
                               waits=wts2, inc="pe")
                B[1].wr = t_rot
                t_t1 = S.emit("dve", lambda e, b=bank, half=half: e.tensor_mul(
                    t1q[:, :], b.t[:, 0:512],
                    cosfm[:, half * 512:(half + 1) * 512]),
                    waits=rd + [prev_qw], inc="dve")
                bank.readers.append(t_t1)
                rd2 = B[1].pre_read()
                t_t2 = S.emit("dve", lambda e, half=half: e.tensor_mul(
                    t2q[:, :], pbk[1][:, 0:512],
                    sinfm[:, half * 512:(half + 1) * 512]),
                    waits=rd2, inc="dve")
                B[1].readers.append(t_t2)

                def mkqw(half=half, hp=hp, gpar=gpar):
                    def f(e):
                        dst = qfm_b[gpar].rearrange("p (n c t) -> p n c t", c=4, t=128)[
                            :, 4 * half:4 * half + 4, hp, :]
                        return e.tensor_add(dst, t1q[:, :].rearrange(
                            "p (n t) -> p n t", t=128), t2q[:, :].rearrange(
                            "p (n t) -> p n t", t=128))
                    return f
                t_qw = S.emit("pool", mkqw(),
                              waits=[t_t1, t_t2] +
                              (qfm_readers[gpar] if first_qw else []),
                              inc="pool")
                if first_qw:
                    qfm_readers[gpar] = []
                first_qw = False
                prev_qw = t_qw
                qfm_toks.append(t_qw)
        for n in range(NCH):
            npar = n % 2
            bkX = B[2 + npar]
            bkY = B[4 + npar]
            tb2 = TB[npar]
            wts = tb2.pre_write() + ([tok_kv_done] if g == 0 and n == 0 else [])
            t_tr = S.emit("pe", lambda e, n=n, g=g, tb2=tb2: e.transpose(
                tb2.t[:, 0:128], k_tm3[:, n, g * 128:(g + 1) * 128], idb[:, :]),
                waits=wts, inc="pe")
            tb2.wr = t_tr
            t_kf = S.emit("dve", lambda e, tb2=tb2, npar=npar: e.tensor_copy(
                kfm_a[npar][:, :], tb2.t[:, 0:128]),
                waits=tb2.pre_read(), inc="dve")
            tb2.readers.append(t_kf)
            t_tl = dma("pool", tlcur[0:1, :].rearrange("o (c t) -> o c t", t=128),
                       ngT[4 * g:4 * g + 4, n * 128:(n + 1) * 128],
                       waits=[S.tok("pe")])
            t_rh = dma("pool", rhcur[1:2, :].rearrange("o (c t) -> o c t", t=128),
                       gT[4 * g:4 * g + 4, n * 128:(n + 1) * 128], waits=[])
            t_rh = fence("pool", via="act")
            bank1 = B[1]
            wtsT = bank1.pre_write() + [t_rh]
            last_T = None
            for hp in range(4):
                def mkT(hp=hp):
                    def f(e):
                        return e.matmul(pbk[1][:, hp * 128:(hp + 1) * 128],
                                        tlcur[0:2, hp * 128:(hp + 1) * 128],
                                        rhcur[0:2, hp * 128:(hp + 1) * 128],
                                        start=True, stop=True)
                    return f
                last_T = S.emit("pe", mkT(), waits=(wtsT if hp == 0 else []),
                                inc="pe" if hp == 3 else None)
            bank1.wr = last_T
            rdT = bank1.pre_read()
            t_tm = S.emit("dve", lambda e: e.tensor_add(tsb_a[:, :], pbk[1][:, 0:512],
                                                        mask4[:, :]),
                          waits=rdT + [prev_t_dx, prev_t_tm], inc="dve")
            bank1.readers.append(t_tm)
            prev_t_tm = t_tm
            t_dx = S.emit("act", lambda e: e.activation(dsb_a[:, :], tsb_a[:, :],
                                                        AF.Exp),
                          waits=[t_tm], inc="act")
            prev_t_dx = t_dx
            wtsS = bkX.pre_write() + [t_kf] + qfm_toks
            t_sc = S.emit("pe", lambda e, n=n, npar=npar, bkX=bkX, qfm_g3=qfm_g3:
                          e.matmul(bkX.t[:, 0:512], kfm_a[npar][:, :],
                                   qfm_g3[:, n, :], start=True, stop=True),
                          waits=wtsS, inc="pe")
            bkX.wr = t_sc
            qfm_readers[gpar].append(t_sc)
            rdS = bkX.pre_read()
            t_sb = S.emit("dve", lambda e, npar=npar, bkX=bkX: e.tensor_mul(
                sbf_a[npar][:, :], bkX.t[:, 0:512], dsb_a[:, :]),
                waits=rdS + [t_dx], inc="dve")
            bkX.readers.append(t_sb)
            last_xc = None
            for hp in range(4):
                h = 4 * g + hp

                def mkxc(hp=hp, h=h, n=n, g=g, npar=npar):
                    def f(e):
                        return e.tensor_scalar(xc_a[npar][:, hp * 128:(hp + 1) * 128],
                                               v_tm3[:, n, g * 128:(g + 1) * 128],
                                               dt_all[:, n * 32 + h:n * 32 + h + 1],
                                               None, OP.mult)
                    return f
                last_xc = S.emit("dve", mkxc(),
                                 waits=([prev_cs_a2] if hp == 0 else []),
                                 inc="dve" if hp == 3 else None)
            last_kd = None
            for hp in range(4):
                h = 4 * g + hp

                def mkkd(hp=hp, h=h, n=n, g=g, npar=npar):
                    def f(e):
                        return e.tensor_scalar(kd_a[npar][:, hp * 128:(hp + 1) * 128],
                                               k_tm3[:, n, g * 128:(g + 1) * 128],
                                               kdsc_all[:, n * 32 + h:n * 32 + h + 1],
                                               None, OP.mult)
                    return f
                last_kd = S.emit("dve", mkkd(), inc="dve" if hp == 3 else None)
            t_st = S.emit("pool", lambda e, g=g: e.tensor_copy(
                stbf_a[:, :], state[:, g * 512:(g + 1) * 512]),
                waits=[tok_incoming, prev_yb, prev_up], inc="pool")
            wtsA = bkY.pre_write() + [t_sb, last_xc]
            last_ya = None
            for hp in range(4):
                def mkya(hp=hp, npar=npar, bkY=bkY):
                    def f(e):
                        return e.matmul(bkY.t[:, hp * 128:(hp + 1) * 128],
                                        sbf_a[npar][:, hp * 128:(hp + 1) * 128],
                                        xc_a[npar][:, hp * 128:(hp + 1) * 128],
                                        start=True, stop=True)
                    return f
                last_ya = S.emit("pe", mkya(), waits=(wtsA if hp == 0 else []),
                                 inc="pe" if hp == 3 else None)
            bkY.wr = last_ya
            wtsB2 = bkX.pre_write() + [t_st] + qfm_toks
            last_yb = None
            for hp in range(4):
                def mkyb(hp=hp, n=n, bkX=bkX, qfm_g3=qfm_g3):
                    def f(e):
                        return e.matmul(bkX.t[:, hp * 128:(hp + 1) * 128],
                                        qfm_g3[:, n, hp * 128:(hp + 1) * 128],
                                        stbf_a[:, hp * 128:(hp + 1) * 128],
                                        start=True, stop=True)
                    return f
                last_yb = S.emit("pe", mkyb(), waits=(wtsB2 if hp == 0 else []),
                                 inc="pe" if hp == 3 else None)
            bkX.wr = last_yb
            prev_yb = last_yb
            qfm_readers[gpar].append(last_yb)
            rdB = bkX.pre_read()
            last_yi = None
            for hp in range(4):
                h = 4 * g + hp

                def mkyi(hp=hp, h=h, n=n, bkX=bkX):
                    def f(e):
                        return e.activation(yint_a[:, hp * 128:(hp + 1) * 128],
                                            bkX.t[:, hp * 128:(hp + 1) * 128],
                                            AF.Identity,
                                            scale=expg_all[:, n * 32 + h:n * 32 + h + 1])
                    return f
                last_yi = S.emit("act", mkyi(),
                                 waits=(rdB + [prev_t_y] if hp == 0 else []),
                                 inc="act" if hp == 3 else None)
            bkX.readers.append(last_yi)
            rdA = bkY.pre_read()
            t_y = S.emit("dve", lambda e, bkY=bkY: e.tensor_add(ysb_a[:, :],
                                                                bkY.t[:, 0:512],
                                                                yint_a[:, :]),
                         waits=rdA + [last_yi], inc="dve")
            bkY.readers.append(t_y)
            prev_t_y = t_y
            wtsC = bkX.pre_write() + [last_kd, last_xc]
            last_cs = None
            for hp in range(4):
                def mkcs(hp=hp, npar=npar, bkX=bkX):
                    def f(e):
                        return e.matmul(bkX.t[:, hp * 128:(hp + 1) * 128],
                                        kd_a[npar][:, hp * 128:(hp + 1) * 128],
                                        xc_a[npar][:, hp * 128:(hp + 1) * 128],
                                        start=True, stop=True)
                    return f
                last_cs = S.emit("pe", mkcs(), waits=(wtsC if hp == 0 else []),
                                 inc="pe" if hp == 3 else None)
            bkX.wr = last_cs
            prev_cs_a2 = last_cs
            rdC = bkX.pre_read()
            last_up = None
            for hp in range(4):
                h = 4 * g + hp

                def mkup(hp=hp, h=h, n=n, bkX=bkX):
                    def f(e):
                        blk = state[:, h * 128:(h + 1) * 128]
                        return e.scalar_tensor_tensor(
                            blk, blk, drep_all[:, n * 32 + h:n * 32 + h + 1],
                            bkX.t[:, hp * 128:(hp + 1) * 128], OP.mult, OP.add)
                    return f
                last_up = S.emit("dve", mkup(), waits=(rdC + [t_st] if hp == 0 else []),
                                 inc="dve" if hp == 3 else None)
            bkX.readers.append(last_up)
            prev_up = last_up
            t_sq2 = S.emit("dve", lambda e: e.tensor_mul(sq_a[:, :], ysb_a[:, :],
                                                         ysb_a[:, :]),
                           waits=[t_y, prev_t_dx], inc="dve")
            t_ssq2 = S.emit("dve", lambda e: e.tensor_reduce(
                stats[:, 0:4], sq_a[:, :].rearrange("p (c t) -> p c t", t=128),
                AX.X, OP.add),
                waits=[t_sq2, prev_t_ln2], inc="dve")
            t_ln2 = S.emit("act", lambda e: e.activation(stats[:, 4:8], stats[:, 0:4],
                                                         AF.Ln, scale=1.0 / 128,
                                                         bias=crow[:, 0:1]),
                           waits=[t_ssq2], inc="act")
            t_rs2 = S.emit("act", lambda e: e.activation(stats[:, 4:8], stats[:, 4:8],
                                                         AF.Exp, scale=-0.5),
                           waits=[t_ln2], inc="act")
            prev_t_ln2 = t_rs2
            last_yn = None
            for hp in range(4):
                h = 4 * g + hp

                def mkyn(hp=hp, h=h, n=n, npar=npar):
                    def f(e):
                        dst = ynst[npar][:, hp * 128:(hp + 1) * 128]
                        return e.scalar_tensor_tensor(dst,
                                                      ysb_a[:, hp * 128:(hp + 1) * 128],
                                                      stats[:, 4 + hp:5 + hp],
                                                      preps[:, 0:128],
                                                      OP.mult, OP.mult)
                    return f
                last_yn = S.emit("dve", mkyn(),
                                 waits=([t_rs2, ynst_tok[npar]] if hp == 0 else []),
                                 inc="dve" if hp == 3 else None)
            dma("sp", yscr.ap()[g, :, n, :], ynst[npar][:, :], waits=[last_yn])
            ynst_tok[npar] = fence("sp")
    tok_a2_done = [S.tok("dve"), S.tok("act"), S.tok("pe"), S.tok("pool"),
                   fence("sp")]

    if "A2" in tap_phases:
        tp = tapdecl("yscr", [NG, 128, NCH, 512], BF16)
        dma("sp", tp, yscr.ap(), waits=tok_a2_done)
        tp2 = tapdecl("state_a2", [128, 4096], F32)
        dma("sp", tp2, state[:, :], waits=tok_a2_done)

    tok_spq2 = fence("sp")
    wg_first = [S.tok("dve"), S.tok("pe"), S.tok("act"), S.tok("pool"), tok_spq2]
    for hc in range(H):
        sl = wsl_i % 2
        wsl_i += 1
        tok_w = dma("pool", wslot[sl], wg_in[hc],
                    waits=slot_readers[sl] + (wg_first if hc < 2 else []))
        tok_w = fence("pool", via="act")
        slot_readers[sl] = []
        pp = hc % 2
        bkA, bkB = B[2 * pp], B[2 * pp + 1]
        wtsA = bkA.pre_write() + [tok_w] + (wg_first if hc == 0 else [])
        wtsB = bkB.pre_write()
        lastA = lastB = None
        for k in range(KC):
            def mkA(k=k, sl=sl, bkA=bkA):
                def f(e):
                    return e.matmul(bkA.t[:, 0:512], wslot[sl][:, k, :],
                                    xn_fm[:, k, 0:512], start=(k == 0),
                                    stop=(k == KC - 1))
                return f
            def mkB(k=k, sl=sl, bkB=bkB):
                def f(e):
                    return e.matmul(bkB.t[:, 0:512], wslot[sl][:, k, :],
                                    xn_fm[:, k, 512:1024], start=(k == 0),
                                    stop=(k == KC - 1))
                return f
            lastA = S.emit("pe", mkA(), waits=(wtsA if k == 0 else []),
                           inc="pe" if k == KC - 1 else None)
            lastB = S.emit("pe", mkB(), waits=(wtsB if k == 0 else []),
                           inc="pe" if k == KC - 1 else None)
        bkA.wr = lastA
        bkB.wr = lastB
        slot_readers[sl] += [lastA, lastB]
        for half, bk in ((0, bkA), (1, bkB)):
            def mksil(hc=hc, half=half, bk=bk):
                def f(e):
                    return e.activation(gs3[:, hc, half * 512:(half + 1) * 512],
                                        bk.t[:, 0:512], AF.Silu)
                return f
            t_sil = S.emit("act", mksil(), waits=bk.pre_read(), inc="act")
            bk.readers.append(t_sil)
    tok_b2a = [S.tok("act"), S.tok("pe")]

    ynst2 = arena[:, 0:4096]
    yn2_tok = [None, None]
    yg_first = list(tok_b2a) + [S.tok("dve"), S.tok("pool"), fence("sp")]
    first_b2b = True
    for g in range(NG):
        for half in range(2):
            sl = (g * 2 + half) % 2
            tok_yn = dma("sp", ynst2[:, sl * 2048:(sl + 1) * 2048].rearrange(
                "p (c t) -> p c t", t=512),
                yscr.ap()[g, :, half * 4:half * 4 + 4, :],
                waits=[yn2_tok[sl]] + (yg_first if first_b2b else []))
            tok_yn = fence("sp")
            first_b2b = False
            for nn in range(4):
                n = half * 4 + nn
                tb2 = TB[n % 2]
                wts = tb2.pre_write() + [tok_yn]
                last_t = None
                for hp in range(4):
                    def mktr(hp=hp, tb2=tb2, sl=sl, nn=nn):
                        def f(e):
                            return e.transpose(tb2.t[:, hp * 128:(hp + 1) * 128],
                                               ynst2[:, sl * 2048 + nn * 512 + hp * 128:
                                                     sl * 2048 + nn * 512 + (hp + 1) * 128],
                                               idb[:, :])
                        return f
                    last_t = S.emit("pe", mktr(), waits=(wts if hp == 0 else []),
                                    inc="pe" if hp == 3 else None)
                tb2.wr = last_t

                def mkyg(g=g, n=n, tb2=tb2):
                    def f(e):
                        dst = yg_fm[:, 4 * g:4 * g + 4, n * 128:(n + 1) * 128]
                        src_ = tb2.t[:, 0:512].rearrange("p (c t) -> p c t", t=128)
                        gsl = gs3[:, 4 * g:4 * g + 4, n * 128:(n + 1) * 128]
                        return e.tensor_mul(dst, gsl, src_)
                    return f
                t_yg = S.emit("dve", mkyg(), waits=tb2.pre_read(), inc="dve")
                tb2.readers.append(t_yg)
                yn2_tok[sl] = t_yg
    tok_b2b = [S.tok("dve"), S.tok("pe")]

    if "B2" in tap_phases:
        tp = tapdecl("yg_fm", [128, 32768], BF16)
        dma("sp", tp, u1[:, 0:32768], waits=tok_b2b)
        tp2 = tapdecl("gatesil", [128, 32768], BF16)
        dma("sp", tp2, gatesil[:, :], waits=tok_b2b)

    bpst = arena_f[:, 2048:4096]
    outst = arena_f[:, 4096:5120]
    bp_tok = [None, None]
    bp_readers = [[], []]
    out_tok = [None, None]
    c_first = list(tok_b2b) + [fence("sp")]
    for fc in range(KC):
        sl = wsl_i % 2
        wsl_i += 1
        tok_w = dma("pool", wslot[sl], wo_in[fc],
                    waits=slot_readers[sl] + (c_first if fc < 2 else []))
        tok_w = fence("pool", via="act")
        slot_readers[sl] = []
        bsl = fc % 2
        tok_bp = dma("sp", bpst[:, bsl * 1024:(bsl + 1) * 1024], bscr.ap()[fc],
                     waits=bp_readers[bsl] + (c_first if fc == 0 else []))
        tok_bp = fence("sp")
        out_tok[0] = out_tok[1] = tok_bp
        bp_readers[bsl] = []
        pp = fc % 2
        bkA, bkB = B[2 * pp], B[2 * pp + 1]
        wtsA = bkA.pre_write() + [tok_w] + (c_first if fc == 0 else [])
        wtsB = bkB.pre_write()
        lastA = lastB = None
        for ec in range(KC):
            def mkA(ec=ec, sl=sl, bkA=bkA):
                def f(e):
                    return e.matmul(bkA.t[:, 0:512], wslot[sl][:, ec, :],
                                    yg_fm[:, ec, 0:512], start=(ec == 0),
                                    stop=(ec == KC - 1))
                return f
            def mkB(ec=ec, sl=sl, bkB=bkB):
                def f(e):
                    return e.matmul(bkB.t[:, 0:512], wslot[sl][:, ec, :],
                                    yg_fm[:, ec, 512:1024], start=(ec == 0),
                                    stop=(ec == KC - 1))
                return f
            lastA = S.emit("pe", mkA(), waits=(wtsA if ec == 0 else []),
                           inc="pe" if ec == KC - 1 else None)
            lastB = S.emit("pe", mkB(), waits=(wtsB if ec == 0 else []),
                           inc="pe" if ec == KC - 1 else None)
        bkA.wr = lastA
        bkB.wr = lastB
        slot_readers[sl] += [lastA, lastB]
        for half, bk in ((0, bkA), (1, bkB)):
            osl = half

            def mkev(half=half, bk=bk, bsl=bsl, osl=osl):
                def f(e):
                    return e.tensor_add(outst[:, osl * 512:(osl + 1) * 512],
                                        bk.t[:, 0:512],
                                        bpst[:, bsl * 1024 + half * 512:
                                             bsl * 1024 + (half + 1) * 512])
                return f
            t_ev = S.emit("dve", mkev(), waits=bk.pre_read() + [tok_bp, out_tok[osl]],
                          inc="dve")
            bk.readers.append(t_ev)
            bp_readers[bsl].append(t_ev)
            dma("sp", o_out[fc][:, half * 512:(half + 1) * 512],
                outst[:, osl * 512:(osl + 1) * 512], waits=[t_ev])

    final_waits = [(f"dsp{i}", S.cnt[f"dsp{i}"]) for i in range(RQ)
                   if S.cnt.get(f"dsp{i}", 0) > 0]
    final_waits += [(f"dpool{i}", S.cnt[f"dpool{i}"]) for i in range(RQ)
                    if S.cnt.get(f"dpool{i}", 0) > 0]
    S.emit("sp", lambda e: e.nop(), waits=final_waits)

    S.check()
    S.materialize(nc, sems)
    st.close()
    return nc, taps



def _tile_oi(w, nblk, bs):
    O, I = w.shape
    assert O == nblk * bs and I == KC * 128
    return np.ascontiguousarray(
        np.asarray(w).reshape(nblk, bs, KC, 128).transpose(0, 3, 2, 1))


def _consts(core):
    c0 = core * T
    inv = 1.0 / (1e4 ** (np.arange(0, D, 2, dtype=np.float64) / D))
    pos = np.arange(c0, c0 + T, dtype=np.float64)
    ang = pos[None, :] * np.concatenate([inv, inv])[:, None]
    cosfm = np.cos(ang).astype(np.float32)
    sinfm = np.sin(ang).astype(np.float32)
    jj, ii = np.meshgrid(np.arange(128), np.arange(128), indexing="ij")
    cuml = (jj <= ii).astype(np.float32)
    mask = np.where(ii >= jj, 0.0, -1e9).astype(np.float32)
    mask4 = np.tile(mask, (1, 4))
    idf = np.eye(128, dtype=np.float32)
    idbm = idf.astype(bfdt)
    P = np.zeros((128, 128), np.float32)
    for dd in range(64):
        P[dd, dd + 64] = -1.0
    for dd in range(64, 128):
        P[dd, dd - 64] = 1.0
    ptb = np.ascontiguousarray(P.T).astype(bfdt)
    onesb = np.ones((1, 512), np.float32).astype(bfdt)
    onesf = np.ones((1, 128), np.float32)
    crw = np.zeros((128, 8), np.float32)
    crw[:, 0] = 1e-5
    crw[:, 1] = 1.0
    crw[:, 2] = np.log(2.0)
    u = (np.arange(8) < core).astype(np.float32)
    urow = np.repeat(u, 32)[None, :].astype(np.float32)
    unot = 1.0 - urow
    mrow = urow.copy()
    return dict(cosfm=cosfm, sinfm=sinfm, cumlt=cuml, mask4=mask4, idf=idf,
                idb=idbm, ptb=ptb, onesb=onesb, onesf=onesf, crow=crw,
                urow=np.ascontiguousarray(urow), unot=np.ascontiguousarray(unot),
                mrow=np.ascontiguousarray(mrow))


_PROG_CACHE = {}


def _get_prog(taps_key=()):
    key = tuple(taps_key)
    if key not in _PROG_CACHE:
        _PROG_CACHE[key] = build_program(key)
    return _PROG_CACHE[key]


def kernel(**inputs):
    f32 = np.float32
    x = np.ascontiguousarray(np.asarray(inputs["L0_attn_hidden_states"], f32)
                             .reshape(NCORES, NCH, 128, DM))
    Wq = np.asarray(inputs["L0_attn_Wq"], f32)
    Wk = np.asarray(inputs["L0_attn_Wk"], f32)
    Wv = np.asarray(inputs["L0_attn_Wv"], f32)
    Wo = np.asarray(inputs["L0_attn_Wo"], f32)
    Wg = np.asarray(inputs["L0_attn_Wg"], f32)
    Win = np.asarray(inputs["L0_attn_Win"], f32)
    dt_b = np.asarray(inputs["L0_attn_dt_bias"], f32)
    A_log = np.asarray(inputs["L0_attn_A_log_bias"], f32)
    gnorm = np.asarray(inputs["L0_attn_gnorm_w"], f32)
    W_base = np.asarray(inputs["L0_ffn_W_base"], f32)
    b_base = np.asarray(inputs["L0_ffn_b_base"], f32)
    W_gate = np.asarray(inputs["L0_ffn_W_gate"], f32)
    W_thr = np.asarray(inputs["L0_ffn_W_thr"], f32)
    b_thr = np.asarray(inputs["L0_ffn_b_thr"], f32)
    A_w = np.asarray(inputs["L0_ffn_A_w"], f32)
    B_w = np.asarray(inputs["L0_ffn_B_w"], f32)

    wq_t = _tile_oi(Wq, H, 128)
    wg_t = _tile_oi(Wg, H, 128)
    wk_t = _tile_oi(Wk, KVH, 128)
    wv_t = _tile_oi(Wv, KVH, 128)
    wo_t = _tile_oi(Wo, KC, 128)
    wb_t = _tile_oi(W_base, KC, 128)
    a_hat = np.ascontiguousarray(A_w.transpose(2, 0, 1).reshape(DM, 128))
    smallw = np.concatenate([a_hat, W_gate.T, W_thr[:, None], Win.T], axis=1)
    smallw_t = np.ascontiguousarray(smallw.reshape(KC, 128, 169).transpose(1, 0, 2))
    bhat = np.ascontiguousarray(B_w.transpose(0, 2, 1).reshape(128, DM))
    bbase = np.ascontiguousarray(b_base[None, :])
    prow = np.concatenate([gnorm, A_log, dt_b, b_thr])[None, :].astype(f32)

    shared = dict(wq=wq_t, wg=wg_t, wk=wk_t, wv=wv_t, wo=wo_t, wb=wb_t,
                  smallw=smallw_t, bhat=bhat, bbase=bbase,
                  prow=np.ascontiguousarray(prow))

    taps_key = tuple(_DEBUG_TAPS)
    nc, taps = _get_prog(taps_key)
    in_maps = []
    for c in range(NCORES):
        m = dict(shared)
        m["x"] = np.ascontiguousarray(x[c])
        m["x_fm"] = np.ascontiguousarray(
            x[c].reshape(T, DM).T.reshape(KC, 128, T))
        m.update(_consts(c))
        in_maps.append(m)
    res = run_bass_kernel_spmd(nc, in_maps, core_ids=list(range(NCORES)))
    out = np.concatenate([np.ascontiguousarray(
        np.asarray(r["o_out"], f32).reshape(DM, T).T)
        for r in res.results], axis=0)
    if taps_key:
        kernel._last_results = res.results
    return out.reshape(1, NCORES * T, DM)


if __name__ == "__main__":
    nc, _ = build_program()
    print("BUILD OK")
